# version 19
# baseline (speedup 1.0000x reference)
import contextlib
import numpy as np
import concourse.bass as bass
import concourse.mybir as mybir
from concourse.bass_utils import run_bass_kernel_spmd

F32 = mybir.dt.float32
BF16 = mybir.dt.bfloat16
AF = mybir.ActivationFunctionType
OP = mybir.AluOpType
AX = mybir.AxisListType

NCORES = 8
D = 2048
SEQ = 8192
TOWN = 1024
NCH = 16
DT = 16
HFFN = 8192
MAGIC = 12582912.0
QK_SCALE = 0.08838834764831845
LN_EPS = 1e-5
RMS_EPS = float(np.finfo(np.float32).eps)
NEG_BIG = -3.0e38
RG = [list(range(NCORES))]


class Sem:
    def __init__(self, h, name):
        self.h = h
        self.name = name
        self.n = 0

    def inc(self, instr, k=1):
        instr.then_inc(self.h, k)
        self.n += k
        return self.n


class PStream:

    def __init__(self, nc, scope, sems, name):
        self.sems = []
        for i in range(2):
            nm = f"{name}{i}"
            s = Sem(scope.enter_context(nc.semaphore(nm)), nm)
            sems[nm] = s
            self.sems.append(s)

    def issue(self, q, out, in_, grp):
        s = self.sems[grp % 2]
        s.inc(q.dma_start(out=out, in_=in_), 16)
        return (grp % 2, s.n)

    def tok(self, grp, per_grp=1):
        return (grp % 2, (grp // 2 + 1) * 16 * per_grp)

    def wait(self, eng, tok):
        eng.wait_ge(self.sems[tok[0]].h, tok[1])


def build_nc(phases=99, dbg=(), p4_nch=NCH, p4_tail=True):
    nc = bass.Bass("TRN2", target_bir_lowering=False, debug=False)

    x_own = nc.dram_tensor("x_own", [TOWN, D], F32, kind="ExternalInput").ap()
    wqkv = nc.dram_tensor("wqkv", [D, 768], F32, kind="ExternalInput").ap()
    bqkv = nc.dram_tensor("bqkv", [1, 768], F32, kind="ExternalInput").ap()
    wout = nc.dram_tensor("wout", [D, D], F32, kind="ExternalInput").ap()
    bout = nc.dram_tensor("bout", [1, D], F32, kind="ExternalInput").ap()
    wgu = nc.dram_tensor("wgu", [2048, 2048], F32, kind="ExternalInput").ap()
    wp = nc.dram_tensor("wp", [2048, 1024], F32, kind="ExternalInput").ap()
    out = nc.dram_tensor("out", [TOWN, D], F32, kind="ExternalOutput").ap()

    ggu_in = nc.dram_tensor("ggu_in", [2048, 2048], BF16).ap()
    ggu_g = nc.dram_tensor("ggu_g", [16384, 2048], BF16, addr_space="Shared").ap()
    gp_in = nc.dram_tensor("gp_in", [1024, 2048], BF16).ap()
    gp_g = nc.dram_tensor("gp_g", [8192, 2048], BF16, addr_space="Shared").ap()
    xnt_in = nc.dram_tensor("xnt_in", [2048, 1024], BF16).ap()
    xnt_g = nc.dram_tensor("xnt_g", [16384, 1024], BF16, addr_space="Shared").ap()
    h_dram = nc.dram_tensor("h_dram", [HFFN, TOWN], BF16).ap()
    hn_dram = nc.dram_tensor("hn_dram", [HFFN, TOWN], BF16).ap()
    a2a_in = nc.dram_tensor("a2a_in", [SEQ, 256], BF16).ap()
    ag_o = nc.dram_tensor("ag_o", [NCORES * SEQ, 256], BF16, addr_space="Shared").ap()
    a2a_out = nc.dram_tensor("a2a_out", [SEQ, 256], BF16).ap()
    oofs = nc.dram_tensor("oofs", [1, NCORES], mybir.dt.int32, kind="ExternalInput").ap()

    dbg_specs = {
        "ggu_g": ([16384, 2048], BF16), "gp_g": ([8192, 2048], BF16),
        "xnt_g": ([16384, 1024], BF16),
        "qt_d": ([256, SEQ], BF16), "kt_d": ([256, SEQ], BF16),
        "v_d": ([SEQ, 256], BF16),
        "h_d": ([HFFN, TOWN], BF16), "rstd_d": ([128, TOWN], F32),
        "a2a_d": ([SEQ, 256], BF16), "hn_d": ([HFFN, TOWN], BF16),
    }
    dbg_aps = {name: nc.dram_tensor("d_" + name, *dbg_specs[name],
                                    kind="ExternalOutput").ap()
               for name in dbg}

    es = contextlib.ExitStack()
    with es:
        ident_bf = es.enter_context(nc.sbuf_tensor("ident_bf", [128, 128], BF16))
        ident_f = es.enter_context(nc.sbuf_tensor("ident_f", [128, 128], F32))
        ones_row = es.enter_context(nc.sbuf_tensor("ones_row", [1, 512], BF16))
        ones_col = es.enter_context(nc.sbuf_tensor("ones_col", [128, 1], BF16))
        ones_rowf = es.enter_context(nc.sbuf_tensor("ones_rowf", [1, 128], F32))
        rstd_bc = es.enter_context(nc.sbuf_tensor("rstd_bc", [128, TOWN], F32))
        oofs_sb = es.enter_context(nc.sbuf_tensor("oofs_sb", [1, NCORES], mybir.dt.int32))

        pA = es.enter_context(nc.psum_tensor("pA", [128, 2048], F32))
        pB = es.enter_context(nc.psum_tensor("pB", [128, 2048], F32))

        def bA(i):
            return pA[:, i * 512:(i + 1) * 512]

        def bB(i):
            return pB[:, i * 512:(i + 1) * 512]

        sems = {}
        for nm in ["dL", "dG", "cPE", "cAC", "cVE", "cGP", "cCC"]:
            sems[nm] = Sem(es.enter_context(nc.semaphore(nm)), nm)
        dL, dG = sems["dL"], sems["dG"]
        cPE, cAC, cVE, cGP, cCC = (sems[k] for k in ["cPE", "cAC", "cVE", "cGP", "cCC"])

        def snap():
            return {nm: s.n for nm, s in sems.items()}

        block = es.enter_context(nc.Block())

        def barrier(eng, snp, skip=()):
            for nm, v in snp.items():
                if v > 0 and nm not in skip:
                    eng.wait_ge(sems[nm].h, v)

        @block.gpsimd
        def _(g):
            for t_ in (ident_bf, ident_f):
                cGP.inc(g.memset(t_[:], 0.0))
                g.wait_ge(cGP.h, cGP.n)
                cGP.inc(g.affine_select(out=t_[:], in_=t_[:], compare_op=OP.not_equal,
                                        fill=1.0, base=0, pattern=[[-1, 128]],
                                        channel_multiplier=1))
            cGP.inc(g.memset(ones_row[:], 1.0))
            cGP.inc(g.memset(ones_col[:], 1.0))
            cGP.inc(g.memset(ones_rowf[:], 1.0))

        qh = es.enter_context(nc.sbuf_tensor("qh", [128, 2, 2048], BF16))
        qg = es.enter_context(nc.sbuf_tensor("qg", [128, 2, 32], F32))
        qsb = es.enter_context(nc.sbuf_tensor("qsb", [128, 2, 32], BF16))
        qgr = es.enter_context(nc.sbuf_tensor("qgr", [128, 2, 32], F32))
        qstage = es.enter_context(nc.sbuf_tensor("qstage", [128, 2, 512], BF16))
        qr = es.enter_context(nc.sbuf_tensor("qr", [128, 2048], F32))

        s1 = snap()
        with contextlib.ExitStack() as e1:
            xnt = e1.enter_context(nc.sbuf_tensor("xnt", [128, DT, 1024], BF16))
            xt = e1.enter_context(nc.sbuf_tensor("xt", [128, 2, 2048], F32))
            sqs = e1.enter_context(nc.sbuf_tensor("sqs", [128, 2048], F32))
            xn_tm = e1.enter_context(nc.sbuf_tensor("xn_tm", [128, 2, 2048], BF16))
            stt = e1.enter_context(nc.sbuf_tensor("stt", [128, 2, 8], F32))
            xn_f = e1.enter_context(nc.sbuf_tensor("xn_f", [128, 2048], F32))
            NA = 6
            NV1 = 9
            SX1 = PStream(nc, es, sems, "SX1")

            @block.sync
            def _(sy):
                barrier(sy, s1)
                for i in range(8):
                    if i >= 2:
                        sy.wait_ge(cVE.h, s1["cVE"] + (i - 1) * NV1)
                        sy.wait_ge(cAC.h, s1["cAC"] + (i - 2) * NA + 1)
                    SX1.issue(sy, xt[:, i % 2, :], x_own[i * 128:(i + 1) * 128, :], i)

            @block.scalar
            def _(sc):
                barrier(sc, s1)
                for i in range(8):
                    par = i % 2
                    SX1.wait(sc, SX1.tok(i))
                    cAC.inc(sc.activation(out=sqs[:], in_=xt[:, par, :], func=AF.Square,
                                          accum_out=stt[:, par, 6:7]))
                    sc.wait_ge(cVE.h, s1["cVE"] + i * NV1 + 5)
                    cAC.inc(sc.activation(out=stt[:, par, 2:3], in_=stt[:, par, 1:2],
                                          func=AF.Sqrt))
                    for grp in range(4):
                        sc.wait_ge(cPE.h, s1["cPE"] + i * 4 + grp + 1)
                        cAC.inc(sc.activation(
                            out=xnt[:, grp * 4:(grp + 1) * 4, i * 128:(i + 1) * 128],
                            in_=bB(grp % 2), func=AF.Copy))

            @block.vector
            def _(v):
                barrier(v, s1)
                for i in range(8):
                    par = i % 2
                    SX1.wait(v, SX1.tok(i))
                    cVE.inc(v.reduce_sum(stt[:, par, 0:1], xt[:, par, :], AX.X))
                    v.wait_ge(cAC.h, s1["cAC"] + i * NA + 1)
                    v.wait_ge(cVE.h, cVE.n)
                    cVE.inc(v.tensor_scalar(stt[:, par, 4:5], stt[:, par, 0:1],
                                            -1.0 / D, None, OP.mult))
                    v.wait_ge(cVE.h, cVE.n)
                    cVE.inc(v.tensor_tensor(stt[:, par, 7:8], stt[:, par, 4:5],
                                            stt[:, par, 4:5], OP.mult))
                    v.wait_ge(cVE.h, cVE.n)
                    cVE.inc(v.tensor_scalar(stt[:, par, 1:2], stt[:, par, 6:7],
                                            1.0 / D, LN_EPS, OP.mult, OP.add))
                    v.wait_ge(cVE.h, cVE.n)
                    cVE.inc(v.tensor_tensor(stt[:, par, 1:2], stt[:, par, 1:2],
                                            stt[:, par, 7:8], OP.subtract))
                    v.wait_ge(cAC.h, s1["cAC"] + i * NA + 2)
                    cVE.inc(v.reciprocal(stt[:, par, 3:4], stt[:, par, 2:3]))
                    v.wait_ge(cVE.h, cVE.n)
                    cVE.inc(v.tensor_tensor(stt[:, par, 5:6], stt[:, par, 4:5],
                                            stt[:, par, 3:4], OP.mult))
                    v.wait_ge(cVE.h, cVE.n)
                    cVE.inc(v.tensor_scalar(xn_f[:], xt[:, par, :],
                                            stt[:, par, 3:4], stt[:, par, 5:6],
                                            OP.mult, OP.add))
                    v.wait_ge(cVE.h, cVE.n)
                    if i >= 2:
                        v.wait_ge(cPE.h, s1["cPE"] + (i - 1) * 4)
                    cVE.inc(v.tensor_copy(xn_tm[:, par, :], xn_f[:]))

            @block.tensor
            def _(te):
                barrier(te, s1)
                for i in range(8):
                    te.wait_ge(cVE.h, s1["cVE"] + (i + 1) * NV1)
                    for grp in range(4):
                        gidx = i * 4 + grp
                        if gidx >= 2:
                            pi, pg = (gidx - 2) // 4, (gidx - 2) % 4
                            te.wait_ge(cAC.h, s1["cAC"] + pi * NA + 2 + pg + 1)
                        last = None
                        for j in range(4):
                            dt_i = grp * 4 + j
                            last = te.matmul(bB(grp % 2)[:, j * 128:(j + 1) * 128],
                                             xn_tm[:, i % 2, dt_i * 128:(dt_i + 1) * 128],
                                             ident_bf[:], start=True, stop=True)
                        cPE.inc(last)

            ac_p1 = s1["cAC"] + 8 * NA

            @block.sync
            def _(sy):
                sy.wait_ge(cAC.h, ac_p1)
                dL.inc(sy.dma_start(out=xnt_in.rearrange("(a p) t -> p a t", p=128),
                                    in_=xnt[:]), 16)

            st_p1 = dL.n

            @block.gpsimd
            def _(g):
                g.wait_ge(dL.h, st_p1)
                cCC.inc(g.collective_compute("AllGather", OP.bypass, replica_groups=RG,
                                             ins=[xnt_in.opt()], outs=[xnt_g.opt()]), 1)
        cc_xnt = cCC.n

        exB = es.enter_context(contextlib.ExitStack())
        QT = exB.enter_context(nc.sbuf_tensor("QT", [128, 2, SEQ], BF16))
        KT = exB.enter_context(nc.sbuf_tensor("KT", [128, 2, SEQ], BF16))
        Vsb = exB.enter_context(nc.sbuf_tensor("Vsb", [128, 64, 256], BF16))
        qwf = exB.enter_context(nc.sbuf_tensor("qwf", [128, 2, 2048], F32))
        qw2 = exB.enter_context(nc.sbuf_tensor("qw2", [128, 2, 2048], BF16))
        wqkv_bf = exB.enter_context(nc.sbuf_tensor("wqkv_bf", [128, DT, 768], BF16))
        bqkv_bf = exB.enter_context(nc.sbuf_tensor("bqkv_bf", [1, 768], BF16))

        @block.gpsimd
        def _(g):
            dG.inc(g.dma_start(out=bqkv_bf[:], in_=bqkv[:, :]), 16)
            dG.inc(g.dma_start(out=wqkv_bf[:],
                               in_=wqkv.rearrange("(a p) c -> p a c", p=128)), 16)

        dg_wqkv = dG.n

        s0 = snap()
        if True:

            tiles = [("gu", i) for i in range(16)] + [("wp", i) for i in range(16)]
            NV = 8
            NVQ = 6
            SQW = PStream(nc, es, sems, "SQW")
            SST0 = PStream(nc, es, sems, "SST0")

            def tF(t):
                return 2048 if t[0] == "gu" else 1024

            pe_groups_after = []
            tot = 0
            for t in tiles:
                tot += tF(t) // 512
                pe_groups_after.append(tot)
            NBG = tot
            NT = len(tiles)
            PQ_cast, PQ_clip, PQ_mult = {}, {}, {}
            _pos = s0["cGP"]
            for i in range(NT):
                _pos += 1; PQ_cast[i] = _pos
                if i >= 1:
                    _pos += 1; PQ_clip[i - 1] = _pos
                    _pos += 1; PQ_mult[i - 1] = _pos
            _pos += 1; PQ_clip[NT - 1] = _pos
            _pos += 1; PQ_mult[NT - 1] = _pos

            def pool_quant(g, i):
                t = tiles[i]
                par = i % 2
                F = tF(t)
                G = F // 64
                h3 = qh[:, par, :F].rearrange("p (g k) -> p g k", k=64)
                sb = qsb[:, par, :G].unsqueeze(2).broadcast_to([128, G, 64])
                g.wait_ge(cVE.h, s0["cVE"] + (i + 1) * NVQ)
                cGP.inc(g.tensor_scalar(qr[:, :F], qr[:, :F],
                                        1.0, -1.0, OP.min, OP.max))
                g.wait_ge(cGP.h, cGP.n)
                if i >= 2:
                    g.wait_ge(cPE.h, s0["cPE"] + pe_groups_after[i - 2])
                cGP.inc(g.tensor_tensor(
                    h3, qr[:, :F].rearrange("p (g k) -> p g k", k=64),
                    sb, OP.mult))

            @block.gpsimd
            def _(g):
                for i, t in enumerate(tiles):
                    SQW.wait(g, SQW.tok(i))
                    if i >= 2:
                        g.wait_ge(cVE.h, s0["cVE"] + (i - 1) * NVQ)
                    cGP.inc(g.tensor_copy(qw2[:, i % 2, :tF(t)],
                                          qwf[:, i % 2, :tF(t)]))
                    if i >= 1:
                        pool_quant(g, i - 1)
                pool_quant(g, len(tiles) - 1)

            @block.vector
            def _(v):
                for i, t in enumerate(tiles):
                    par = i % 2
                    F = tF(t)
                    G = F // 64
                    v.wait_ge(cGP.h, PQ_cast[i])
                    if i >= 2:
                        v.wait_ge(cGP.h, PQ_mult[i - 2])
                        v.wait_ge(cPE.h, s0["cPE"] + pe_groups_after[i - 2])
                    w3 = qw2[:, par, :F].rearrange("p (g k) -> p g k", k=64)
                    h3 = qh[:, par, :F].rearrange("p (g k) -> p g k", k=64)
                    cVE.inc(v.tensor_reduce(qg[:, par, :G], w3, AX.X, OP.add,
                                            apply_absolute_value=True))
                    v.wait_ge(cVE.h, cVE.n)
                    cVE.inc(v.tensor_scalar(qg[:, par, :G], qg[:, par, :G],
                                            1.0 / 64.0, 1e-8, OP.mult, OP.max))
                    v.wait_ge(cVE.h, cVE.n)
                    cVE.inc(v.tensor_copy(qsb[:, par, :G], qg[:, par, :G]))
                    v.wait_ge(cVE.h, cVE.n)
                    cVE.inc(v.reciprocal(qgr[:, par, :G], qsb[:, par, :G]))
                    v.wait_ge(cVE.h, cVE.n)
                    rb = qgr[:, par, :G].unsqueeze(2).broadcast_to([128, G, 64])
                    cVE.inc(v.tensor_tensor(h3, w3, rb, OP.mult))
                    v.wait_ge(cVE.h, cVE.n)
                    if i >= 1:
                        v.wait_ge(cGP.h, PQ_mult[i - 1])
                    cVE.inc(v.tensor_scalar(qr[:, :F], qh[:, par, :F],
                                            MAGIC, -MAGIC, OP.add, OP.add))



            @block.tensor
            def _(te):
                gidx = 0
                for i, t in enumerate(tiles):
                    te.wait_ge(cGP.h, PQ_mult[i])
                    for grp in range(tF(t) // 512):
                        if gidx >= 2:
                            te.wait_ge(cAC.h, s0["cAC"] + gidx - 1)
                        pb = bB(2 + gidx % 2)
                        last = None
                        for j in range(4):
                            dt_i = grp * 4 + j
                            last = te.matmul(pb[:, j * 128:(j + 1) * 128],
                                             qh[:, i % 2, dt_i * 128:(dt_i + 1) * 128],
                                             ident_bf[:], start=True, stop=True)
                        cPE.inc(last)
                        gidx += 1

            @block.scalar
            def _(sc):
                gidx = 0

                def evac_tile(pi):
                    nonlocal gidx
                    for _ in range(tF(tiles[pi]) // 512):
                        sc.wait_ge(cPE.h, s0["cPE"] + gidx + 1)
                        if gidx >= 2:
                            SST0.wait(sc, SST0.tok(gidx - 2))
                        cAC.inc(sc.activation(out=qstage[:, gidx % 2, :],
                                              in_=bB(2 + gidx % 2), func=AF.Copy))
                        gidx += 1

                for i, t in enumerate(tiles):
                    if i >= 2:
                        sc.wait_ge(cGP.h, PQ_cast[i - 2])
                    src = (wgu if t[0] == "gu" else wp)[t[1] * 128:(t[1] + 1) * 128, :]
                    SQW.issue(sc, qwf[:, i % 2, :tF(t)], src, i)
                    if i >= 2:
                        evac_tile(i - 2)
                for pi in (NT - 2, NT - 1):
                    evac_tile(pi)

            @block.sync
            def _(sy):
                gidx = 0
                for i, t in enumerate(tiles):
                    for grp in range(tF(t) // 512):
                        sy.wait_ge(cAC.h, s0["cAC"] + gidx + 1)
                        dst = (ggu_in if t[0] == "gu" else gp_in)[
                            grp * 512:(grp + 1) * 512, t[1] * 128:(t[1] + 1) * 128]
                        SST0.issue(
                            sy, dst.rearrange("(a p) c -> p a c", p=128),
                            qstage[:, gidx % 2, :].rearrange("p (a c) -> p a c", c=128),
                            gidx)
                        gidx += 1

            st_p0 = [s.n for s in SST0.sems]

            @block.gpsimd
            def _(g):
                for s, v in zip(SST0.sems, st_p0):
                    g.wait_ge(s.h, v)
                cCC.inc(g.collective_compute("AllGather", OP.bypass, replica_groups=RG,
                                             ins=[ggu_in.opt()], outs=[ggu_g.opt()]), 1)
                cCC.inc(g.collective_compute("AllGather", OP.bypass, replica_groups=RG,
                                             ins=[gp_in.opt()], outs=[gp_g.opt()]), 1)
        cc_ffn = cCC.n

        s2 = snap()
        with contextlib.ExitStack() as e2:
            xg = e2.enter_context(nc.sbuf_tensor("xg", [128, 2, DT, 512], BF16))
            SXG = PStream(nc, es, sems, "SXG")
            SXH = PStream(nc, es, sems, "SXH")

            @block.sync
            def _(sy):
                sy.wait_ge(cCC.h, cc_xnt)
                for tp in range(16):
                    if tp >= 2:
                        sy.wait_ge(cPE.h, s2["cPE"] + (tp - 1) * 8)
                    r = tp // 2
                    coff = (tp % 2) * 512
                    src = xnt_g[r * 2048:(r + 1) * 2048, coff:coff + 256]
                    SXG.issue(sy, xg[:, tp % 2, :, 0:256],
                              src.rearrange("(a p) t -> p a t", p=128), tp)

            @block.gpsimd
            def _(g):
                g.wait_ge(cCC.h, cc_xnt)
                for tp in range(16):
                    if tp >= 2:
                        g.wait_ge(cPE.h, s2["cPE"] + (tp - 1) * 8)
                    r = tp // 2
                    coff = (tp % 2) * 512 + 256
                    src = xnt_g[r * 2048:(r + 1) * 2048, coff:coff + 256]
                    SXH.issue(g, xg[:, tp % 2, :, 256:512],
                              src.rearrange("(a p) t -> p a t", p=128), tp)

            @block.tensor
            def _(te):
                te.wait_ge(dG.h, dg_wqkv)
                for tp in range(16):
                    par = tp % 2
                    SXG.wait(te, SXG.tok(tp))
                    SXH.wait(te, SXH.tok(tp))
                    for mt in range(4):
                        if tp > 0 or mt >= 2:
                            pj, pmt = (tp, mt - 2) if mt >= 2 else (tp - 1, mt + 2)
                            te.wait_ge(cAC.h, s2["cAC"] + pj * 8 + pmt + 1)
                        pb = bA(mt % 2)
                        for dt_i in range(DT):
                            te.matmul(pb, wqkv_bf[:, dt_i, mt * 128:(mt + 1) * 128],
                                      xg[:, par, dt_i, :], start=(dt_i == 0), stop=False)
                        cPE.inc(te.matmul(pb, bqkv_bf[0:1, mt * 128:(mt + 1) * 128],
                                          ones_row[0:1, :], start=False, stop=True))
                    for vt in range(4):
                        if tp > 0 or vt >= 2:
                            pj, pvt = (tp, vt - 2) if vt >= 2 else (tp - 1, vt + 2)
                            te.wait_ge(cAC.h, s2["cAC"] + pj * 8 + 4 + pvt + 1)
                        pb = bA(2 + vt % 2)[:, 0:256]
                        for dt_i in range(DT):
                            te.matmul(pb, xg[:, par, dt_i, vt * 128:(vt + 1) * 128],
                                      wqkv_bf[:, dt_i, 512:768],
                                      start=(dt_i == 0), stop=False)
                        cPE.inc(te.matmul(pb, ones_row[0:1, 0:128],
                                          bqkv_bf[0:1, 512:768], start=False, stop=True))

            @block.scalar
            def _(sc):
                for tp in range(16):
                    for mt in range(4):
                        sc.wait_ge(cPE.h, s2["cPE"] + tp * 8 + mt + 1)
                        dstt = QT if mt < 2 else KT
                        scl = QK_SCALE if mt < 2 else 1.0
                        cAC.inc(sc.activation(
                            out=dstt[:, mt % 2, tp * 512:(tp + 1) * 512],
                            in_=bA(mt % 2), func=AF.Copy, scale=scl))
                    for vt in range(4):
                        sc.wait_ge(cPE.h, s2["cPE"] + tp * 8 + 4 + vt + 1)
                        cAC.inc(sc.activation(out=Vsb[:, tp * 4 + vt, :],
                                              in_=bA(2 + vt % 2)[:, 0:256], func=AF.Copy))
        s4 = snap()
        with contextlib.ExitStack() as e4:
            masksT = e4.enter_context(nc.sbuf_tensor("masksT", [128, 4, 512], F32))
            PT_sb = e4.enter_context(nc.sbuf_tensor("PT_sb", [128, 4, 1024], BF16))
            OT_raw = e4.enter_context(nc.sbuf_tensor("OT_raw", [128, 2, 512], BF16))
            a2a_st = e4.enter_context(nc.sbuf_tensor("a2a_st", [128, 2, 4, 128], BF16))
            m0 = e4.enter_context(nc.sbuf_tensor("m0", [128, 2, 4], F32))
            m0s_sb = e4.enter_context(nc.sbuf_tensor("m0s_sb", [1, 2], F32))
            nm0s = e4.enter_context(nc.sbuf_tensor("nm0s", [128, 32], F32))
            lrow_sb = e4.enter_context(nc.sbuf_tensor("lrow_sb", [1, 512], F32))
            onef = e4.enter_context(nc.sbuf_tensor("onef", [1, 1], F32))
            linv = e4.enter_context(nc.sbuf_tensor("linv", [128, 4], F32))
            SAS = PStream(nc, es, sems, "SAS")

            NHC = 2 * p4_nch
            hcs = [(head, chunk) for head in range(2) for chunk in range(p4_nch)]

            peL, acL, veL = {}, {}, {}
            pe_n = [s4["cPE"]]
            ac_n = [s4["cAC"]]
            ve_n = [s4["cVE"]]

            def _mk(counter, lab_dict):
                def _add(lab, k=1):
                    counter[0] += k
                    lab_dict[lab] = counter[0]
                return _add

            PEp, ACp, VEp = _mk(pe_n, peL), _mk(ac_n, acL), _mk(ve_n, veL)
            for hc in range(NHC + 2):
                if 1 <= hc <= NHC:
                    PEp(("pT1", hc - 1))
                if 2 <= hc <= NHC + 1:
                    PEp(("pTb", hc - 2))
                if hc < NHC:
                    PEp(("pSq", hc))
                if hc < NHC:
                    VEp(("pr1", hc))
                if 1 <= hc <= NHC:
                    VEp(("pr2", hc - 1))
                if 2 <= hc <= NHC + 1:
                    VEp(("pcp", hc - 2))
            for hc, (head, chunk) in enumerate(hcs):
                nu = 2 * (chunk + 1)
                s_order = [1, 0] + list(range(2, nu))
                for i, u in enumerate(s_order):
                    PEp(("ST", hc, u))
                    if i >= 2:
                        PEp(("PVh", hc, s_order[i - 2]))
                for t in (s_order[nu - 2], s_order[nu - 1]):
                    PEp(("PVh", hc, t))
                PEp(("otm", hc))
                PEp(("lrowT", hc))
                for u in range(nu):
                    ACp(("exph", hc, u))
                ACp(("otev", hc))
                for u in range(nu):
                    if u // 2 == chunk:
                        VEp(("maskT", hc, u), 2)
                VEp(("lrowev", hc))
                VEp(("recip", hc))
                VEp(("scale", hc), 4)

            def lastPV(hc):
                chunk = hcs[hc][1]
                return ("PVh", hc, 0) if chunk == 0 else ("PVh", hc, 2 * chunk + 1)

            @block.gpsimd
            def _(g):
                barrier(g, s4)
                for kc in range(4):
                    cGP.inc(g.memset(masksT[:, kc, :], 0.0))
                    g.wait_ge(cGP.h, cGP.n)
                    cGP.inc(g.affine_select(out=masksT[:, kc, :], in_=masksT[:, kc, :],
                                            compare_op=OP.is_ge, fill=NEG_BIG,
                                            base=-kc * 128, pattern=[[1, 512]],
                                            channel_multiplier=-1))
                cGP.inc(g.memset(onef[:], 1.0))

            gp_masks = cGP.n

            @block.tensor
            def _(te):
                barrier(te, s4)
                for it in range(NHC + 2):
                    if 1 <= it <= NHC:
                        k = it - 1
                        tgt = pA if k % 2 == 0 else pB
                        te.wait_ge(cVE.h, veL[("pr1", k)])
                        cPE.inc(te.matmul(tgt[0:1, 0:128], m0[:, k % 2, 0:1],
                                          ident_f[:], start=True, stop=True))
                    if 2 <= it <= NHC + 1:
                        k = it - 2
                        tgt = pB if k % 2 == 0 else pA
                        te.wait_ge(cVE.h, veL[("pr2", k)])
                        if k + 1 < NHC:
                            te.wait_ge(cVE.h, veL[("pr1", k + 1)])
                        cPE.inc(te.matmul(tgt[:, 192:193], ones_rowf[0:1, :],
                                          m0s_sb[0:1, k % 2:k % 2 + 1],
                                          start=True, stop=True))
                    if it < NHC:
                        k = it
                        head, chunk = hcs[k]
                        qb = chunk * 512
                        tgt = pA if k % 2 == 0 else pB
                        if k >= 3:
                            te.wait_ge(cVE.h, veL[("pcp", k - 3)])
                        last = None
                        for qt in range(4):
                            last = te.matmul(
                                tgt[:, qt * 512:(qt + 1) * 512],
                                QT[:, head, qb + qt * 128:qb + (qt + 1) * 128],
                                KT[:, head, 0:512], start=True, stop=True)
                        cPE.inc(last)
                te.wait_ge(cGP.h, gp_masks)
                for hc, (head, chunk) in enumerate(hcs):
                    qb = chunk * 512
                    nu = 2 * (chunk + 1)
                    s_order = [1, 0] + list(range(2, nu))
                    ob = bB(2 + hc % 2)

                    def emit_ST(u):
                        kb = u // 2
                        h = u % 2
                        p = u % 2
                        if u == 0:
                            if hc > 0:
                                te.wait_ge(cVE.h, veL[("scale", hc - 1)])
                        elif u == 1:
                            if hc > 0:
                                te.wait_ge(cAC.h, acL[("otev", hc - 1)])
                            else:
                                te.wait_ge(cVE.h, veL[("pcp", NHC - 1)])
                        else:
                            te.wait_ge(cAC.h, acL[("exph", hc, u - 2)])
                        last = None
                        for j in range(2):
                            kc = 2 * h + j
                            koff = kb * 512 + kc * 128
                            last = te.matmul(
                                pA[:, p * 1024 + j * 512:p * 1024 + (j + 1) * 512],
                                KT[:, head, koff:koff + 128],
                                QT[:, head, qb:qb + 512],
                                start=True, stop=True)
                        cPE.inc(last)

                    def emit_PVh(u, first, last_f):
                        kb = u // 2
                        h = u % 2
                        p = u % 4
                        te.wait_ge(cAC.h, acL[("exph", hc, u)])
                        if first and hc >= 2:
                            te.wait_ge(cAC.h, acL[("otev", hc - 2)])
                        last = None
                        for j in range(2):
                            kc = 2 * h + j
                            te.matmul(ob, Vsb[:, kb * 4 + kc,
                                              head * 128:(head + 1) * 128],
                                      PT_sb[:, p, j * 512:(j + 1) * 512],
                                      start=(first and j == 0),
                                      stop=(last_f and j == 1),
                                      skip_group_check=True)
                        for j in range(2):
                            last = te.matmul(
                                bB(0)[0:1, 0:512], ones_col[:],
                                PT_sb[:, p, j * 512:(j + 1) * 512],
                                start=(first and j == 0),
                                stop=(last_f and j == 1),
                                skip_group_check=True)
                        cPE.inc(last)

                    npv = 0
                    for i, u in enumerate(s_order):
                        emit_ST(u)
                        if i >= 2:
                            emit_PVh(s_order[i - 2], npv == 0, npv == nu - 1)
                            npv += 1
                    for t in (s_order[nu - 2], s_order[nu - 1]):
                        emit_PVh(t, npv == 0, npv == nu - 1)
                        npv += 1
                    te.wait_ge(cAC.h, acL[("otev", hc)])
                    last = None
                    for qt in range(4):
                        last = te.matmul(pA[:, qt * 128:(qt + 1) * 128],
                                         OT_raw[:, hc % 2, qt * 128:(qt + 1) * 128],
                                         ident_bf[:], start=True, stop=True)
                    cPE.inc(last)
                    te.wait_ge(cVE.h, veL[("lrowev", hc)])
                    last = None
                    for qt in range(4):
                        last = te.matmul(bB(1)[:, qt:qt + 1],
                                         lrow_sb[0:1, qt * 128:(qt + 1) * 128],
                                         onef[0:1, 0:1], start=True, stop=True)
                    cPE.inc(last)

            @block.scalar
            def _(sc):
                barrier(sc, s4)
                sc.wait_ge(cVE.h, veL[("pcp", NHC - 1)])
                for hc, (head, chunk) in enumerate(hcs):
                    nu = 2 * (chunk + 1)
                    for u in range(nu):
                        p = u % 4
                        sc.wait_ge(cPE.h, peL[("ST", hc, u)])
                        if u // 2 == chunk:
                            sc.wait_ge(cVE.h, veL[("maskT", hc, u)])
                        if u < 4:
                            if hc > 0:
                                sc.wait_ge(cPE.h, peL[lastPV(hc - 1)])
                        else:
                            sc.wait_ge(cPE.h, peL[("PVh", hc, u - 4)])
                        cAC.inc(sc.activation(out=PT_sb[:, p, 0:1024],
                                              in_=pA[:, (u % 2) * 1024:
                                                      (u % 2 + 1) * 1024],
                                              func=AF.Exp, bias=nm0s[:, hc:hc + 1],
                                              scale=1.0))
                    sc.wait_ge(cPE.h, peL[lastPV(hc)])
                    cAC.inc(sc.activation(out=OT_raw[:, hc % 2, :],
                                          in_=bB(2 + hc % 2), func=AF.Copy))

            @block.vector
            def _(v):
                barrier(v, s4)
                for it in range(NHC + 2):
                    if it < NHC:
                        k = it
                        tgt = pA if k % 2 == 0 else pB
                        v.wait_ge(cPE.h, peL[("pSq", k)])
                        cVE.inc(v.reduce_max(m0[:, k % 2, 0:1], tgt[:, :],
                                             AX.X, negate=True))
                    if 1 <= it <= NHC:
                        k = it - 1
                        tgt = pA if k % 2 == 0 else pB
                        v.wait_ge(cPE.h, peL[("pT1", k)])
                        v.wait_ge(cVE.h, cVE.n)
                        cVE.inc(v.tensor_reduce(m0s_sb[0:1, k % 2:k % 2 + 1],
                                                tgt[0:1, 0:128], AX.X, OP.min))
                    if 2 <= it <= NHC + 1:
                        k = it - 2
                        tgt = pB if k % 2 == 0 else pA
                        v.wait_ge(cPE.h, peL[("pTb", k)])
                        v.wait_ge(cVE.h, cVE.n)
                        cVE.inc(v.tensor_copy(nm0s[:, k:k + 1], tgt[:, 192:193]))
                v.wait_ge(cGP.h, gp_masks)
                for hc, (head, chunk) in enumerate(hcs):
                    nu = 2 * (chunk + 1)
                    for u in range(nu):
                        if u // 2 == chunk:
                            p = u % 2
                            v.wait_ge(cPE.h, peL[("ST", hc, u)])
                            for j in range(2):
                                kc = 2 * (u % 2) + j
                                cVE.inc(v.tensor_tensor(
                                    pA[:, p * 1024 + j * 512:p * 1024 + (j + 1) * 512],
                                    pA[:, p * 1024 + j * 512:p * 1024 + (j + 1) * 512],
                                    masksT[:, kc, :], OP.add))
                    v.wait_ge(cPE.h, peL[lastPV(hc)])
                    v.wait_ge(cVE.h, cVE.n)
                    cVE.inc(v.tensor_copy(lrow_sb[:], bB(0)[0:1, 0:512]))
                    v.wait_ge(cPE.h, peL[("otm", hc)])
                    v.wait_ge(cPE.h, peL[("lrowT", hc)])
                    if hc >= 2:
                        SAS.wait(v, SAS.tok(hc - 2))
                    v.wait_ge(cVE.h, cVE.n)
                    cVE.inc(v.reciprocal(linv[:], bB(1)[:, 0:4]))
                    v.wait_ge(cVE.h, cVE.n)
                    for qt in range(4):
                        cVE.inc(v.tensor_scalar(a2a_st[:, hc % 2, qt, :],
                                                pA[:, qt * 128:(qt + 1) * 128],
                                                linv[:, qt:qt + 1], None, OP.mult))

            @block.sync
            def _(sy):
                barrier(sy, s4)
                for hc, (head, chunk) in enumerate(hcs):
                    sy.wait_ge(cVE.h, veL[("scale", hc)])
                    dst = a2a_in[chunk * 512:(chunk + 1) * 512,
                                 head * 128:(head + 1) * 128]
                    SAS.issue(sy, dst.rearrange("(a p) c -> p a c", p=128),
                              a2a_st[:, hc % 2, :, :], hc)

            st_a2a = [s.n for s in SAS.sems]

            @block.gpsimd
            def _(g):
                for s, v in zip(SAS.sems, st_a2a):
                    g.wait_ge(s.h, v)
                cCC.inc(g.collective_compute("AllGather", OP.bypass, replica_groups=RG,
                                             ins=[a2a_in.opt()], outs=[ag_o.opt()]), 1)
                dG.inc(g.dma_start(out=oofs_sb[:], in_=oofs[:, :]), 16)
                g.wait_ge(cCC.h, cCC.n)
                g.wait_ge(dG.h, dG.n)
                with g.register("vreg") as vreg:
                    for i in range(NCORES):
                        g.reg_load(vreg, oofs_sb[0:1, i:i + 1])
                        v = g.snap(vreg, min_val=0, max_val=NCORES * SEQ - TOWN)
                        dG.inc(g.dma_start(out=a2a_out[i * TOWN:(i + 1) * TOWN, :],
                                           in_=ag_o[bass.ds(v, TOWN), :]), 16)

        exB.close()
        cc_a2a = cCC.n

        s3 = snap()
        with contextlib.ExitStack() as e3:
            xnt = e3.enter_context(nc.sbuf_tensor("xnt3", [128, DT, 1024], BF16))
            SXN = PStream(nc, es, sems, "SXN")
            ggs = e3.enter_context(nc.sbuf_tensor("ggs", [128, 2, 2, DT, 256], BF16))
            hst = e3.enter_context(nc.sbuf_tensor("hst", [128, 2, 1024], BF16))
            gssb = e3.enter_context(nc.sbuf_tensor("gssb", [128, 2, 512], F32))
            hsq = e3.enter_context(nc.sbuf_tensor("hsq", [128, 2, 1024], BF16))
            rrow = e3.enter_context(nc.sbuf_tensor("rrow", [1, TOWN], F32))
            rrow2 = e3.enter_context(nc.sbuf_tensor("rrow2", [1, TOWN], F32))
            SGG = PStream(nc, es, sems, "SGG")
            SHS = PStream(nc, es, sems, "SHS")

            def pe_hp(hp):
                return s3["cPE"] + (hp + 1) * 5

            @block.sync
            def _(sy):
                barrier(sy, s3, skip=("cCC", "dG"))
                sy.wait_ge(cCC.h, cc_ffn)
                SXN.issue(sy, xnt[:],
                          xnt_in.rearrange("(a p) t -> p a t", p=128), 0)

                def st_h(hp):
                    sy.wait_ge(cVE.h, s3["cVE"] + (hp + 1) * 2)
                    SHS.issue(sy, h_dram[hp * 128:(hp + 1) * 128, :],
                              hst[:, hp % 2, :], hp)

                for hq in range(32):
                    if hq >= 2:
                        sy.wait_ge(cPE.h, pe_hp(2 * hq - 3))
                    r = hq // 8
                    hcol = (hq * 256) % 2048
                    for gi, blk in enumerate([r, 4 + r]):
                        src = ggu_g[blk * 2048:(blk + 1) * 2048, hcol:hcol + 256]
                        SGG.issue(sy, ggs[:, hq % 2, gi, :, :],
                                  src.rearrange("(a p) c -> p a c", p=128), hq)
                    if hq >= 1:
                        st_h(2 * hq - 2)
                        st_h(2 * hq - 1)
                for hp in (62, 63):
                    st_h(hp)

            @block.tensor
            def _(te):
                barrier(te, s3, skip=("cCC", "dG"))
                SXN.wait(te, SXN.tok(0))
                for hp in range(64):
                    hq, hloc = hp // 2, hp % 2
                    SGG.wait(te, SGG.tok(hq, per_grp=2))
                    for th in range(2):
                        u = hp * 2 + th
                        if u >= 2:
                            v_ = u - 2
                            te.wait_ge(cAC.h, s3["cAC"] + (v_ // 2) * 3 + v_ % 2 + 1)
                            te.wait_ge(cVE.h, s3["cVE"] + u - 1)
                        for gi, pb in [(0, bA(u % 2)), (1, bA(2 + u % 2))]:
                            last = None
                            for dt_i in range(DT):
                                last = te.matmul(
                                    pb, ggs[:, hq % 2, gi, dt_i,
                                            hloc * 128:(hloc + 1) * 128],
                                    xnt[:, dt_i, th * 512:(th + 1) * 512],
                                    start=(dt_i == 0), stop=(dt_i == DT - 1))
                            cPE.inc(last)
                    te.wait_ge(cAC.h, s3["cAC"] + hp * 3 + 3)
                    last = None
                    for th in range(2):
                        last = te.matmul(bB(th)[0:1, :], ones_col[:],
                                         hsq[:, hp % 2, th * 512:(th + 1) * 512],
                                         start=(hp == 0), stop=(hp == 63),
                                         skip_group_check=True)
                    cPE.inc(last)

            @block.scalar
            def _(sc):
                barrier(sc, s3, skip=("cCC", "dG"))
                for hp in range(64):
                    for th in range(2):
                        u = hp * 2 + th
                        sc.wait_ge(cPE.h, s3["cPE"] + hp * 5 + th * 2 + 1)
                        if u >= 2:
                            sc.wait_ge(cVE.h, s3["cVE"] + u - 1)
                        cAC.inc(sc.activation(out=gssb[:, u % 2, :], in_=bA(u % 2),
                                              func=AF.Silu))
                    sc.wait_ge(cVE.h, s3["cVE"] + (hp + 1) * 2)
                    cAC.inc(sc.activation(out=hsq[:, hp % 2, :], in_=hst[:, hp % 2, :],
                                          func=AF.Square))

            @block.vector
            def _(v):
                barrier(v, s3, skip=("cCC", "dG"))
                for hp in range(64):
                    for th in range(2):
                        u = hp * 2 + th
                        v.wait_ge(cAC.h, s3["cAC"] + hp * 3 + th + 1)
                        v.wait_ge(cPE.h, s3["cPE"] + hp * 5 + th * 2 + 2)
                        if hp >= 2 and th == 0:
                            SHS.wait(v, SHS.tok(hp - 2))
                        cVE.inc(v.tensor_tensor(hst[:, hp % 2, th * 512:(th + 1) * 512],
                                                gssb[:, u % 2, :], bA(2 + u % 2),
                                                OP.mult))
                v.wait_ge(cPE.h, pe_hp(63))
                for th in range(2):
                    cVE.inc(v.tensor_scalar(rrow[0:1, th * 512:(th + 1) * 512],
                                            bB(th)[0:1, :], 1.0 / HFFN, RMS_EPS,
                                            OP.mult, OP.add))

            @block.scalar
            def _(sc):
                sc.wait_ge(cVE.h, s3["cVE"] + 130)
                cAC.inc(sc.activation(out=rrow2[:], in_=rrow[:], func=AF.Sqrt))

            @block.vector
            def _(v):
                v.wait_ge(cAC.h, s3["cAC"] + 193)
                cVE.inc(v.reciprocal(rrow[:], rrow2[:]))
                v.wait_ge(cVE.h, cVE.n)

            @block.tensor
            def _(te):
                te.wait_ge(cVE.h, s3["cVE"] + 131)
                for th in range(2):
                    last = te.matmul(bB(2 + th), ones_rowf[0:1, :],
                                     rrow[0:1, th * 512:(th + 1) * 512],
                                     start=True, stop=True)
                cPE.inc(last)

            @block.vector
            def _(v):
                v.wait_ge(cPE.h, pe_hp(63) + 1)
                for th in range(2):
                    cVE.inc(v.tensor_copy(rstd_bc[:, th * 512:(th + 1) * 512],
                                          bB(2 + th)))
        s5 = snap()
        with contextlib.ExitStack() as e5:
            ot_in = e5.enter_context(nc.sbuf_tensor("ot_in", [128, 2, 2048], BF16))
            OTs = e5.enter_context(nc.sbuf_tensor("OTs", [128, DT, 1024], BF16))
            wo_b = e5.enter_context(nc.sbuf_tensor("wo_b", [128, 2, DT, 512], BF16))
            gp_b = e5.enter_context(nc.sbuf_tensor("gp_b", [128, 2, 4, 512], BF16))
            h_b = e5.enter_context(nc.sbuf_tensor("h_b", [128, 2, 4, 1024], BF16))
            hn_b = e5.enter_context(nc.sbuf_tensor("hn_b", [128, 2, 4, 1024], BF16))
            fst = e5.enter_context(nc.sbuf_tensor("fst", [128, 8, 512], F32))
            x_b = e5.enter_context(nc.sbuf_tensor("x_b", [128, 2, 512], F32))
            o_st = e5.enter_context(nc.sbuf_tensor("o_st", [128, 2, 512], F32))
            bout_bf = e5.enter_context(nc.sbuf_tensor("bout_bf", [1, D], BF16))
            SOT = PStream(nc, es, sems, "SOT")
            SGP = PStream(nc, es, sems, "SGP")
            SXB = PStream(nc, es, sems, "SXB")
            SOS = PStream(nc, es, sems, "SOS")
            SWO = PStream(nc, es, sems, "SWO")
            SGQ = PStream(nc, es, sems, "SGQ")

            @block.sync
            def _(sy):
                barrier(sy, s5)
                sy.wait_ge(cCC.h, cc_a2a)
                for tt in range(8):
                    if tt >= 2:
                        sy.wait_ge(cPE.h, s5["cPE"] + (tt - 1) * 4)
                    src = a2a_out.rearrange("(i q) c -> i q c", q=1024)[
                        :, tt * 128:(tt + 1) * 128, :].transpose([1, 0, 2])
                    SOT.issue(sy, ot_in[:, tt % 2, :].rearrange("p (i c) -> p i c",
                                                                c=256), src, tt)

            @block.tensor
            def _(te):
                barrier(te, s5)
                for tt in range(8):
                    SOT.wait(te, SOT.tok(tt))
                    for grp in range(4):
                        gidx = tt * 4 + grp
                        if gidx >= 2:
                            te.wait_ge(cAC.h, s5["cAC"] + gidx - 1)
                        last = None
                        for j in range(4):
                            dt_i = grp * 4 + j
                            last = te.matmul(bB(grp % 2)[:, j * 128:(j + 1) * 128],
                                             ot_in[:, tt % 2,
                                                   dt_i * 128:(dt_i + 1) * 128],
                                             ident_bf[:], start=True, stop=True)
                        cPE.inc(last)

            @block.scalar
            def _(sc):
                barrier(sc, s5)
                for gidx in range(32):
                    tt, grp = gidx // 4, gidx % 4
                    sc.wait_ge(cPE.h, s5["cPE"] + gidx + 1)
                    cAC.inc(sc.activation(
                        out=OTs[:, grp * 4:(grp + 1) * 4, tt * 128:(tt + 1) * 128],
                        in_=bB(grp % 2), func=AF.Copy))

            pe_ot = s5["cPE"] + 32
            ac_ot = s5["cAC"] + 32
            s5b = snap()

            pe_z = [[0] * 8 for _ in range(4)]
            pe_y = [[0] * 64 for _ in range(4)]
            pe_T = [0] * 4
            ve_fev = [0] * 4
            ve_xadd = [[0] * 8 for _ in range(4)]
            ve_hn = [[0] * 16 for _ in range(4)]
            pe = pe_ot
            ve = s5b["cVE"]
            for dg in range(4):
                for hq4 in range(16):
                    ve += 1; ve_hn[dg][hq4] = ve
                for b in range(8):
                    pe += 1; pe_z[dg][b] = pe
                for hc in range(64):
                    pe += 1; pe_y[dg][hc] = pe
                for b in range(8):
                    ve += 1
                ve_fev[dg] = ve
                pe += 1; pe_T[dg] = pe
                for tt in range(8):
                    ve += 1; ve_xadd[dg][tt] = ve

            @block.gpsimd
            def _(g):
                barrier(g, s5b)
                dG.inc(g.dma_start(out=bout_bf[:], in_=bout[:, :]), 16)
                for dg in range(4):
                    if dg >= 2:
                        g.wait_ge(cPE.h, pe_z[dg - 2][7])
                    src = wout[:, dg * 512:(dg + 1) * 512]
                    SWO.issue(g, wo_b[:, dg % 2, :, :],
                              src.rearrange("(a p) c -> p a c", p=128), dg)

            @block.sync
            def _(sy):
                for dg in range(4):
                    for hq4 in range(16):
                        if dg > 0 or hq4 >= 2:
                            pg, ph = (dg, hq4 - 2) if hq4 >= 2 else (dg - 1, 14 + hq4)
                            sy.wait_ge(cPE.h, pe_y[pg][ph * 4 + 3])
                            sy.wait_ge(cVE.h, ve_hn[pg][ph])
                        gidx = dg * 16 + hq4
                        SGP.issue(sy, h_b[:, hq4 % 2, :, :],
                                  h_dram[hq4 * 512:(hq4 + 1) * 512, :].rearrange(
                                      "(a p) c -> p a c", p=128), gidx)
                    def ld_x(tt):
                        if dg > 0 or tt >= 2:
                            pg, pt = (dg, tt - 2) if tt >= 2 else (dg - 1, 6 + tt)
                            sy.wait_ge(cVE.h, ve_xadd[pg][pt])
                        SXB.issue(sy, x_b[:, tt % 2, :],
                                  x_own[tt * 128:(tt + 1) * 128,
                                        dg * 512:(dg + 1) * 512], dg * 8 + tt)

                    def st_o(tt):
                        sy.wait_ge(cVE.h, ve_xadd[dg][tt])
                        SOS.issue(sy,
                                  out[tt * 128:(tt + 1) * 128,
                                      dg * 512:(dg + 1) * 512],
                                  o_st[:, tt % 2, :], dg * 8 + tt)

                    ld_x(0)
                    ld_x(1)
                    for tt in range(2, 8):
                        st_o(tt - 2)
                        ld_x(tt)
                    st_o(6)
                    st_o(7)
                for s in SOS.sems:
                    sy.wait_ge(s.h, s.n)
                sy.wait_ge(dL.h, dL.n)

            @block.scalar
            def _(sc):
                for dg in range(4):
                    for hq4 in range(16):
                        if dg > 0 or hq4 >= 2:
                            pg, ph = (dg, hq4 - 2) if hq4 >= 2 else (dg - 1, 14 + hq4)
                            sc.wait_ge(cPE.h, pe_y[pg][ph * 4 + 3])
                            sc.wait_ge(cVE.h, ve_hn[pg][ph])
                        gidx = dg * 16 + hq4
                        SGQ.issue(sc, gp_b[:, hq4 % 2, :, :],
                                  gp_g[hq4 * 512:(hq4 + 1) * 512,
                                       dg * 512:(dg + 1) * 512].rearrange(
                                           "(a p) c -> p a c", p=128), gidx)

            @block.tensor
            def _(te):
                te.wait_ge(cAC.h, ac_ot)
                te.wait_ge(dG.h, s5b["dG"] + 16)
                for dg in range(4):
                    SWO.wait(te, SWO.tok(dg))
                    for b in range(8):
                        mt, th = b // 2, b % 2
                        pb = bA(b % 4) if b < 4 else bB(b % 4)
                        if dg > 0:
                            te.wait_ge(cVE.h, ve_xadd[dg - 1][b])
                        for dt_i in range(DT):
                            te.matmul(pb, wo_b[:, dg % 2, dt_i, mt * 128:(mt + 1) * 128],
                                      OTs[:, dt_i, th * 512:(th + 1) * 512],
                                      start=(dt_i == 0), stop=False,
                                      skip_group_check=True)
                        cPE.inc(te.matmul(
                            pb,
                            bout_bf[0:1, dg * 512 + mt * 128:dg * 512 + (mt + 1) * 128],
                            ones_row[0:1, :], start=False, stop=False,
                            skip_group_check=True))
                    for hc in range(64):
                        hq4, hs = hc // 4, hc % 4
                        if hs == 0:
                            te.wait_ge(cVE.h, ve_hn[dg][hq4])
                            SGQ.wait(te, SGQ.tok(dg * 16 + hq4))
                        last = None
                        for b in range(8):
                            mt, th = b // 2, b % 2
                            pb = bA(b % 4) if b < 4 else bB(b % 4)
                            last = te.matmul(
                                pb, gp_b[:, hq4 % 2, hs, mt * 128:(mt + 1) * 128],
                                hn_b[:, hq4 % 2, hs, th * 512:(th + 1) * 512],
                                start=False, stop=(hc == 63),
                                skip_group_check=True)
                        cPE.inc(last)
                    te.wait_ge(cVE.h, ve_fev[dg])
                    last = None
                    for b in range(8):
                        mt, th = b // 2, b % 2
                        for tsub in range(4):
                            j = th * 4 + tsub
                            tb = bA(j % 4) if j < 4 else bB(j % 4)
                            last = te.matmul(tb[:, mt * 128:(mt + 1) * 128],
                                             fst[:, b, tsub * 128:(tsub + 1) * 128],
                                             ident_f[:], start=True, stop=True,
                                             skip_group_check=True)
                    cPE.inc(last)

            @block.vector
            def _(v):
                for dg in range(4):
                    for hq4 in range(16):
                        SGP.wait(v, SGP.tok(dg * 16 + hq4))
                        if dg > 0 or hq4 >= 2:
                            pg, ph = (dg, hq4 - 2) if hq4 >= 2 else (dg - 1, 14 + hq4)
                            v.wait_ge(cPE.h, pe_y[pg][ph * 4 + 3])
                        cVE.inc(v.tensor_tensor(
                            hn_b[:, hq4 % 2, :, :], h_b[:, hq4 % 2, :, :],
                            rstd_bc[:].unsqueeze(1).broadcast_to([128, 4, 1024]),
                            OP.mult))
                    v.wait_ge(cPE.h, pe_y[dg][63])
                    for b in range(8):
                        pb = bA(b % 4) if b < 4 else bB(b % 4)
                        cVE.inc(v.tensor_copy(fst[:, b, :], pb))
                    v.wait_ge(cPE.h, pe_T[dg])
                    for tt in range(8):
                        tb = bA(tt % 4) if tt < 4 else bB(tt % 4)
                        SXB.wait(v, SXB.tok(dg * 8 + tt))
                        if dg > 0 or tt >= 2:
                            pg, pt = (dg, tt - 2) if tt >= 2 else (dg - 1, 6 + tt)
                            SOS.wait(v, SOS.tok(pg * 8 + pt))
                        cVE.inc(v.tensor_tensor(o_st[:, tt % 2, :], x_b[:, tt % 2, :],
                                                tb, OP.add))

    return nc


def _finish_dbg(nc, block, sems, snp, dbg_aps, dbg, srcs=None, sb_srcs=None):
    dL = sems["dL"]

    @block.sync
    def _(sy):
        for nm, v in snp.items():
            if v > 0:
                sy.wait_ge(sems[nm].h, v)
        for name in dbg:
            if srcs and name in srcs:
                dL.inc(sy.dma_start(out=dbg_aps[name][:, :], in_=srcs[name][:, :]), 16)
            elif sb_srcs and name in sb_srcs:
                ap = sb_srcs[name]
                d = dbg_aps[name]
                if len(ap.shape) == 3:
                    d3 = d.rearrange("(a p) c -> p a c", p=128)
                    dL.inc(sy.dma_start(out=d3, in_=ap), 16)
                else:
                    dL.inc(sy.dma_start(out=d[:, :], in_=ap), 16)
        sy.wait_ge(dL.h, dL.n)


_NC_CACHE = {}


def _get_nc(phases=99, dbg=()):
    key = (phases, tuple(dbg))
    if key not in _NC_CACHE:
        _NC_CACHE[key] = build_nc(phases, dbg)
    return _NC_CACHE[key]


def make_in_maps(L0_attn_x, L0_attn_W_qkv, L0_attn_b_qkv, L0_attn_W_out,
                 L0_attn_b_out, L0_ffn_w_gate_up, L0_ffn_w_proj):
    x = np.ascontiguousarray(np.asarray(L0_attn_x).reshape(SEQ, D))
    Wqkv = np.asarray(L0_attn_W_qkv)
    bq = np.asarray(L0_attn_b_qkv)
    Wout = np.ascontiguousarray(np.asarray(L0_attn_W_out))
    bo = np.ascontiguousarray(np.asarray(L0_attn_b_out).reshape(1, D))
    wgu_full = np.asarray(L0_ffn_w_gate_up)
    wp_full = np.asarray(L0_ffn_w_proj)
    in_maps = []
    for c in range(NCORES):
        cols = [Wqkv[:, p * D + c * 256:p * D + (c + 1) * 256] for p in range(3)]
        wq = np.ascontiguousarray(np.concatenate(cols, axis=1))
        bqc = np.ascontiguousarray(
            np.concatenate([bq[p * D + c * 256:p * D + (c + 1) * 256]
                            for p in range(3)]).reshape(1, 768))
        in_maps.append({
            "x_own": np.ascontiguousarray(x[c * TOWN:(c + 1) * TOWN]),
            "oofs": np.array([[i * SEQ + c * TOWN for i in range(NCORES)]], np.int32),
            "wqkv": wq, "bqkv": bqc, "wout": Wout, "bout": bo,
            "wgu": np.ascontiguousarray(wgu_full[c * 2048:(c + 1) * 2048]),
            "wp": np.ascontiguousarray(wp_full[:, c * 1024:(c + 1) * 1024]),
        })
    return in_maps


def kernel(**inputs):
    nc = _get_nc()
    in_maps = make_in_maps(**inputs)
    res = run_bass_kernel_spmd(nc, in_maps, core_ids=list(range(NCORES)))
    outp = np.concatenate([res.results[c]["out"] for c in range(NCORES)], axis=0)
    return outp.reshape(1, SEQ, D)


# version 20
# speedup vs baseline: 1.0364x; 1.0364x over previous
import contextlib
import numpy as np
import concourse.bass as bass
import concourse.mybir as mybir
from concourse.bass_utils import run_bass_kernel_spmd

F32 = mybir.dt.float32
BF16 = mybir.dt.bfloat16
AF = mybir.ActivationFunctionType
OP = mybir.AluOpType
AX = mybir.AxisListType

NCORES = 8
D = 2048
SEQ = 8192
TOWN = 1024
NCH = 16
DT = 16
HFFN = 8192
MAGIC = 12582912.0
QK_SCALE = 0.08838834764831845
LN_EPS = 1e-5
RMS_EPS = float(np.finfo(np.float32).eps)
NEG_BIG = -3.0e38
RG = [list(range(NCORES))]


class Sem:
    def __init__(self, h, name):
        self.h = h
        self.name = name
        self.n = 0

    def inc(self, instr, k=1):
        instr.then_inc(self.h, k)
        self.n += k
        return self.n


class PStream:

    def __init__(self, nc, scope, sems, name):
        self.sems = []
        for i in range(2):
            nm = f"{name}{i}"
            s = Sem(scope.enter_context(nc.semaphore(nm)), nm)
            sems[nm] = s
            self.sems.append(s)

    def issue(self, q, out, in_, grp):
        s = self.sems[grp % 2]
        s.inc(q.dma_start(out=out, in_=in_), 16)
        return (grp % 2, s.n)

    def tok(self, grp, per_grp=1):
        return (grp % 2, (grp // 2 + 1) * 16 * per_grp)

    def wait(self, eng, tok):
        eng.wait_ge(self.sems[tok[0]].h, tok[1])


def build_nc(phases=99, dbg=(), p4_nch=NCH, p4_tail=True):
    nc = bass.Bass("TRN2", target_bir_lowering=False, debug=False)

    x_own = nc.dram_tensor("x_own", [TOWN, D], F32, kind="ExternalInput").ap()
    wqkv = nc.dram_tensor("wqkv", [D, 768], F32, kind="ExternalInput").ap()
    bqkv = nc.dram_tensor("bqkv", [1, 768], F32, kind="ExternalInput").ap()
    wout = nc.dram_tensor("wout", [D, D], F32, kind="ExternalInput").ap()
    bout = nc.dram_tensor("bout", [1, D], F32, kind="ExternalInput").ap()
    wgu = nc.dram_tensor("wgu", [2048, 2048], F32, kind="ExternalInput").ap()
    wp = nc.dram_tensor("wp", [2048, 1024], F32, kind="ExternalInput").ap()
    out = nc.dram_tensor("out", [TOWN, D], F32, kind="ExternalOutput").ap()

    ggu_in = nc.dram_tensor("ggu_in", [2048, 2048], BF16).ap()
    ggu_g = nc.dram_tensor("ggu_g", [16384, 2048], BF16, addr_space="Shared").ap()
    gp_in = nc.dram_tensor("gp_in", [1024, 2048], BF16).ap()
    gp_g = nc.dram_tensor("gp_g", [8192, 2048], BF16, addr_space="Shared").ap()
    xnt_in = nc.dram_tensor("xnt_in", [2048, 1024], BF16).ap()
    xnt_g = nc.dram_tensor("xnt_g", [16384, 1024], BF16, addr_space="Shared").ap()
    h_dram = nc.dram_tensor("h_dram", [HFFN, TOWN], BF16).ap()
    hn_dram = nc.dram_tensor("hn_dram", [HFFN, TOWN], BF16).ap()
    a2a_in = nc.dram_tensor("a2a_in", [SEQ, 256], BF16).ap()
    ag_o = nc.dram_tensor("ag_o", [NCORES * SEQ, 256], BF16, addr_space="Shared").ap()
    a2a_out = nc.dram_tensor("a2a_out", [SEQ, 256], BF16).ap()
    oofs = nc.dram_tensor("oofs", [1, NCORES], mybir.dt.int32, kind="ExternalInput").ap()

    dbg_specs = {
        "ggu_g": ([16384, 2048], BF16), "gp_g": ([8192, 2048], BF16),
        "xnt_g": ([16384, 1024], BF16),
        "qt_d": ([256, SEQ], BF16), "kt_d": ([256, SEQ], BF16),
        "v_d": ([SEQ, 256], BF16),
        "h_d": ([HFFN, TOWN], BF16), "rstd_d": ([128, TOWN], F32),
        "a2a_d": ([SEQ, 256], BF16), "hn_d": ([HFFN, TOWN], BF16),
    }
    dbg_aps = {name: nc.dram_tensor("d_" + name, *dbg_specs[name],
                                    kind="ExternalOutput").ap()
               for name in dbg}

    es = contextlib.ExitStack()
    with es:
        ident_bf = es.enter_context(nc.sbuf_tensor("ident_bf", [128, 128], BF16))
        ident_f = es.enter_context(nc.sbuf_tensor("ident_f", [128, 128], F32))
        ones_row = es.enter_context(nc.sbuf_tensor("ones_row", [1, 512], BF16))
        ones_col = es.enter_context(nc.sbuf_tensor("ones_col", [128, 1], BF16))
        ones_rowf = es.enter_context(nc.sbuf_tensor("ones_rowf", [1, 128], F32))
        rstd_bc = es.enter_context(nc.sbuf_tensor("rstd_bc", [128, TOWN], F32))
        oofs_sb = es.enter_context(nc.sbuf_tensor("oofs_sb", [1, NCORES], mybir.dt.int32))

        pA = es.enter_context(nc.psum_tensor("pA", [128, 2048], F32))
        pB = es.enter_context(nc.psum_tensor("pB", [128, 2048], F32))

        def bA(i):
            return pA[:, i * 512:(i + 1) * 512]

        def bB(i):
            return pB[:, i * 512:(i + 1) * 512]

        sems = {}
        for nm in ["dL", "dG", "cPE", "cAC", "cVE", "cGP", "cCC"]:
            sems[nm] = Sem(es.enter_context(nc.semaphore(nm)), nm)
        dL, dG = sems["dL"], sems["dG"]
        cPE, cAC, cVE, cGP, cCC = (sems[k] for k in ["cPE", "cAC", "cVE", "cGP", "cCC"])

        def snap():
            return {nm: s.n for nm, s in sems.items()}

        block = es.enter_context(nc.Block())

        def barrier(eng, snp, skip=()):
            for nm, v in snp.items():
                if v > 0 and nm not in skip:
                    eng.wait_ge(sems[nm].h, v)

        @block.gpsimd
        def _(g):
            for t_ in (ident_bf, ident_f):
                cGP.inc(g.memset(t_[:], 0.0))
                g.wait_ge(cGP.h, cGP.n)
                cGP.inc(g.affine_select(out=t_[:], in_=t_[:], compare_op=OP.not_equal,
                                        fill=1.0, base=0, pattern=[[-1, 128]],
                                        channel_multiplier=1))
            cGP.inc(g.memset(ones_row[:], 1.0))
            cGP.inc(g.memset(ones_col[:], 1.0))
            cGP.inc(g.memset(ones_rowf[:], 1.0))

        qw = es.enter_context(nc.sbuf_tensor("qw", [128, 2, 2048], BF16))
        qh = es.enter_context(nc.sbuf_tensor("qh", [128, 2, 2048], BF16))
        qg = es.enter_context(nc.sbuf_tensor("qg", [128, 2, 32], F32))
        qsb = es.enter_context(nc.sbuf_tensor("qsb", [128, 2, 32], BF16))
        qgr = es.enter_context(nc.sbuf_tensor("qgr", [128, 2, 32], F32))
        qstage = es.enter_context(nc.sbuf_tensor("qstage", [128, 2, 512], BF16))
        qr = es.enter_context(nc.sbuf_tensor("qr", [128, 2, 2048], F32))

        s1 = snap()
        with contextlib.ExitStack() as e1:
            xnt = e1.enter_context(nc.sbuf_tensor("xnt", [128, DT, 1024], BF16))
            xt = e1.enter_context(nc.sbuf_tensor("xt", [128, 2, 2048], F32))
            sqs = e1.enter_context(nc.sbuf_tensor("sqs", [128, 2048], F32))
            xn_tm = e1.enter_context(nc.sbuf_tensor("xn_tm", [128, 2, 2048], BF16))
            stt = e1.enter_context(nc.sbuf_tensor("stt", [128, 2, 8], F32))
            xn_f = e1.enter_context(nc.sbuf_tensor("xn_f", [128, 2048], F32))
            NA = 6
            NV1 = 9
            SX1 = PStream(nc, es, sems, "SX1")

            @block.sync
            def _(sy):
                barrier(sy, s1)
                for i in range(8):
                    if i >= 2:
                        sy.wait_ge(cVE.h, s1["cVE"] + (i - 1) * NV1)
                        sy.wait_ge(cAC.h, s1["cAC"] + (i - 2) * NA + 1)
                    SX1.issue(sy, xt[:, i % 2, :], x_own[i * 128:(i + 1) * 128, :], i)

            @block.scalar
            def _(sc):
                barrier(sc, s1)
                for i in range(8):
                    par = i % 2
                    SX1.wait(sc, SX1.tok(i))
                    cAC.inc(sc.activation(out=sqs[:], in_=xt[:, par, :], func=AF.Square,
                                          accum_out=stt[:, par, 6:7]))
                    sc.wait_ge(cVE.h, s1["cVE"] + i * NV1 + 5)
                    cAC.inc(sc.activation(out=stt[:, par, 2:3], in_=stt[:, par, 1:2],
                                          func=AF.Sqrt))
                    for grp in range(4):
                        sc.wait_ge(cPE.h, s1["cPE"] + i * 4 + grp + 1)
                        cAC.inc(sc.activation(
                            out=xnt[:, grp * 4:(grp + 1) * 4, i * 128:(i + 1) * 128],
                            in_=bB(grp % 2), func=AF.Copy))

            @block.vector
            def _(v):
                barrier(v, s1)
                for i in range(8):
                    par = i % 2
                    SX1.wait(v, SX1.tok(i))
                    cVE.inc(v.reduce_sum(stt[:, par, 0:1], xt[:, par, :], AX.X))
                    v.wait_ge(cAC.h, s1["cAC"] + i * NA + 1)
                    v.wait_ge(cVE.h, cVE.n)
                    cVE.inc(v.tensor_scalar(stt[:, par, 4:5], stt[:, par, 0:1],
                                            -1.0 / D, None, OP.mult))
                    v.wait_ge(cVE.h, cVE.n)
                    cVE.inc(v.tensor_tensor(stt[:, par, 7:8], stt[:, par, 4:5],
                                            stt[:, par, 4:5], OP.mult))
                    v.wait_ge(cVE.h, cVE.n)
                    cVE.inc(v.tensor_scalar(stt[:, par, 1:2], stt[:, par, 6:7],
                                            1.0 / D, LN_EPS, OP.mult, OP.add))
                    v.wait_ge(cVE.h, cVE.n)
                    cVE.inc(v.tensor_tensor(stt[:, par, 1:2], stt[:, par, 1:2],
                                            stt[:, par, 7:8], OP.subtract))
                    v.wait_ge(cAC.h, s1["cAC"] + i * NA + 2)
                    cVE.inc(v.reciprocal(stt[:, par, 3:4], stt[:, par, 2:3]))
                    v.wait_ge(cVE.h, cVE.n)
                    cVE.inc(v.tensor_tensor(stt[:, par, 5:6], stt[:, par, 4:5],
                                            stt[:, par, 3:4], OP.mult))
                    v.wait_ge(cVE.h, cVE.n)
                    cVE.inc(v.tensor_scalar(xn_f[:], xt[:, par, :],
                                            stt[:, par, 3:4], stt[:, par, 5:6],
                                            OP.mult, OP.add))
                    v.wait_ge(cVE.h, cVE.n)
                    if i >= 2:
                        v.wait_ge(cPE.h, s1["cPE"] + (i - 1) * 4)
                    cVE.inc(v.tensor_copy(xn_tm[:, par, :], xn_f[:]))

            @block.tensor
            def _(te):
                barrier(te, s1)
                for i in range(8):
                    te.wait_ge(cVE.h, s1["cVE"] + (i + 1) * NV1)
                    for grp in range(4):
                        gidx = i * 4 + grp
                        if gidx >= 2:
                            pi, pg = (gidx - 2) // 4, (gidx - 2) % 4
                            te.wait_ge(cAC.h, s1["cAC"] + pi * NA + 2 + pg + 1)
                        last = None
                        for j in range(4):
                            dt_i = grp * 4 + j
                            last = te.matmul(bB(grp % 2)[:, j * 128:(j + 1) * 128],
                                             xn_tm[:, i % 2, dt_i * 128:(dt_i + 1) * 128],
                                             ident_bf[:], start=True, stop=True)
                        cPE.inc(last)

            ac_p1 = s1["cAC"] + 8 * NA

            @block.sync
            def _(sy):
                sy.wait_ge(cAC.h, ac_p1)
                dL.inc(sy.dma_start(out=xnt_in.rearrange("(a p) t -> p a t", p=128),
                                    in_=xnt[:]), 16)

            st_p1 = dL.n

            @block.gpsimd
            def _(g):
                g.wait_ge(dL.h, st_p1)
                cCC.inc(g.collective_compute("AllGather", OP.bypass, replica_groups=RG,
                                             ins=[xnt_in.opt()], outs=[xnt_g.opt()]), 1)
        cc_xnt = cCC.n

        s0 = snap()
        if True:

            tiles = [("gu", i) for i in range(16)] + [("wp", i) for i in range(16)]
            NV = 8
            NVQ = 6
            SQW = PStream(nc, es, sems, "SQW")
            SST0 = PStream(nc, es, sems, "SST0")

            def tF(t):
                return 2048 if t[0] == "gu" else 1024

            pe_groups_after = []
            tot = 0
            for t in tiles:
                tot += tF(t) // 512
                pe_groups_after.append(tot)
            NBG = tot

            def pool_quant(g, i):
                t = tiles[i]
                par = i % 2
                F = tF(t)
                G = F // 64
                h3 = qh[:, par, :F].rearrange("p (g k) -> p g k", k=64)
                sb = qsb[:, par, :G].unsqueeze(2).broadcast_to([128, G, 64])
                g.wait_ge(cVE.h, s0["cVE"] + (i + 1) * NVQ)
                cGP.inc(g.tensor_scalar(qr[:, par, :F], qr[:, par, :F],
                                        1.0, -1.0, OP.min, OP.max))
                g.wait_ge(cGP.h, cGP.n)
                if i >= 2:
                    g.wait_ge(cPE.h, s0["cPE"] + pe_groups_after[i - 2])
                cGP.inc(g.tensor_tensor(
                    h3, qr[:, par, :F].rearrange("p (g k) -> p g k", k=64),
                    sb, OP.mult))

            @block.gpsimd
            def _(g):
                for i, t in enumerate(tiles):
                    if i >= 2:
                        g.wait_ge(cVE.h, s0["cVE"] + (i - 1) * NVQ)
                    src = (wgu if t[0] == "gu" else wp)[t[1] * 128:(t[1] + 1) * 128, :]
                    SQW.issue(g, qw[:, i % 2, :tF(t)], src, i)
                    if i >= 1:
                        pool_quant(g, i - 1)
                pool_quant(g, len(tiles) - 1)

            @block.vector
            def _(v):
                for i, t in enumerate(tiles):
                    par = i % 2
                    F = tF(t)
                    G = F // 64
                    SQW.wait(v, SQW.tok(i))
                    if i >= 2:
                        v.wait_ge(cGP.h, s0["cGP"] + (i - 1) * 2)
                        v.wait_ge(cPE.h, s0["cPE"] + pe_groups_after[i - 2])
                    w3 = qw[:, par, :F].rearrange("p (g k) -> p g k", k=64)
                    h3 = qh[:, par, :F].rearrange("p (g k) -> p g k", k=64)
                    cVE.inc(v.tensor_reduce(qg[:, par, :G], w3, AX.X, OP.add,
                                            apply_absolute_value=True))
                    v.wait_ge(cVE.h, cVE.n)
                    cVE.inc(v.tensor_scalar(qg[:, par, :G], qg[:, par, :G],
                                            1.0 / 64.0, 1e-8, OP.mult, OP.max))
                    v.wait_ge(cVE.h, cVE.n)
                    cVE.inc(v.tensor_copy(qsb[:, par, :G], qg[:, par, :G]))
                    v.wait_ge(cVE.h, cVE.n)
                    cVE.inc(v.reciprocal(qgr[:, par, :G], qsb[:, par, :G]))
                    v.wait_ge(cVE.h, cVE.n)
                    rb = qgr[:, par, :G].unsqueeze(2).broadcast_to([128, G, 64])
                    cVE.inc(v.tensor_tensor(h3, w3, rb, OP.mult))
                    v.wait_ge(cVE.h, cVE.n)
                    if i >= 2:
                        v.wait_ge(cGP.h, s0["cGP"] + (i - 2) * 2 + 1)
                    cVE.inc(v.tensor_scalar(qr[:, par, :F], qh[:, par, :F],
                                            MAGIC, -MAGIC, OP.add, OP.add))



            @block.tensor
            def _(te):
                gidx = 0
                for i, t in enumerate(tiles):
                    te.wait_ge(cGP.h, s0["cGP"] + (i + 1) * 2)
                    for grp in range(tF(t) // 512):
                        if gidx >= 2:
                            te.wait_ge(cAC.h, s0["cAC"] + gidx - 1)
                        pb = bB(2 + gidx % 2)
                        last = None
                        for j in range(4):
                            dt_i = grp * 4 + j
                            last = te.matmul(pb[:, j * 128:(j + 1) * 128],
                                             qh[:, i % 2, dt_i * 128:(dt_i + 1) * 128],
                                             ident_bf[:], start=True, stop=True)
                        cPE.inc(last)
                        gidx += 1

            @block.scalar
            def _(sc):
                for gidx in range(NBG):
                    sc.wait_ge(cPE.h, s0["cPE"] + gidx + 1)
                    if gidx >= 2:
                        SST0.wait(sc, SST0.tok(gidx - 2))
                    cAC.inc(sc.activation(out=qstage[:, gidx % 2, :],
                                          in_=bB(2 + gidx % 2), func=AF.Copy))

            @block.sync
            def _(sy):
                gidx = 0
                for i, t in enumerate(tiles):
                    for grp in range(tF(t) // 512):
                        sy.wait_ge(cAC.h, s0["cAC"] + gidx + 1)
                        dst = (ggu_in if t[0] == "gu" else gp_in)[
                            grp * 512:(grp + 1) * 512, t[1] * 128:(t[1] + 1) * 128]
                        SST0.issue(
                            sy, dst.rearrange("(a p) c -> p a c", p=128),
                            qstage[:, gidx % 2, :].rearrange("p (a c) -> p a c", c=128),
                            gidx)
                        gidx += 1

            st_p0 = [s.n for s in SST0.sems]

            @block.gpsimd
            def _(g):
                for s, v in zip(SST0.sems, st_p0):
                    g.wait_ge(s.h, v)
                cCC.inc(g.collective_compute("AllGather", OP.bypass, replica_groups=RG,
                                             ins=[ggu_in.opt()], outs=[ggu_g.opt()]), 1)
                cCC.inc(g.collective_compute("AllGather", OP.bypass, replica_groups=RG,
                                             ins=[gp_in.opt()], outs=[gp_g.opt()]), 1)
        cc_ffn = cCC.n

        s2 = snap()
        exB = es.enter_context(contextlib.ExitStack())
        QT = exB.enter_context(nc.sbuf_tensor("QT", [128, 2, SEQ], BF16))
        KT = exB.enter_context(nc.sbuf_tensor("KT", [128, 2, SEQ], BF16))
        Vsb = exB.enter_context(nc.sbuf_tensor("Vsb", [128, 64, 256], BF16))
        with contextlib.ExitStack() as e2:
            xg = e2.enter_context(nc.sbuf_tensor("xg", [128, 2, DT, 512], BF16))
            wqkv_bf = e2.enter_context(nc.sbuf_tensor("wqkv_bf", [128, DT, 768], BF16))
            bqkv_bf = e2.enter_context(nc.sbuf_tensor("bqkv_bf", [1, 768], BF16))
            SXG = PStream(nc, es, sems, "SXG")
            SXH = PStream(nc, es, sems, "SXH")

            @block.gpsimd
            def _(g):
                dG.inc(g.dma_start(out=bqkv_bf[:], in_=bqkv[:, :]), 16)
                dG.inc(g.dma_start(out=wqkv_bf[:],
                                   in_=wqkv.rearrange("(a p) c -> p a c", p=128)), 16)

            dg_wqkv = dG.n

            @block.sync
            def _(sy):
                sy.wait_ge(cCC.h, cc_xnt)
                for tp in range(16):
                    if tp >= 2:
                        sy.wait_ge(cPE.h, s2["cPE"] + (tp - 1) * 8)
                    r = tp // 2
                    coff = (tp % 2) * 512
                    src = xnt_g[r * 2048:(r + 1) * 2048, coff:coff + 256]
                    SXG.issue(sy, xg[:, tp % 2, :, 0:256],
                              src.rearrange("(a p) t -> p a t", p=128), tp)

            @block.gpsimd
            def _(g):
                g.wait_ge(cCC.h, cc_xnt)
                for tp in range(16):
                    if tp >= 2:
                        g.wait_ge(cPE.h, s2["cPE"] + (tp - 1) * 8)
                    r = tp // 2
                    coff = (tp % 2) * 512 + 256
                    src = xnt_g[r * 2048:(r + 1) * 2048, coff:coff + 256]
                    SXH.issue(g, xg[:, tp % 2, :, 256:512],
                              src.rearrange("(a p) t -> p a t", p=128), tp)

            @block.tensor
            def _(te):
                te.wait_ge(dG.h, dg_wqkv)
                for tp in range(16):
                    par = tp % 2
                    SXG.wait(te, SXG.tok(tp))
                    SXH.wait(te, SXH.tok(tp))
                    for mt in range(4):
                        if tp > 0 or mt >= 2:
                            pj, pmt = (tp, mt - 2) if mt >= 2 else (tp - 1, mt + 2)
                            te.wait_ge(cAC.h, s2["cAC"] + pj * 8 + pmt + 1)
                        pb = bA(mt % 2)
                        for dt_i in range(DT):
                            te.matmul(pb, wqkv_bf[:, dt_i, mt * 128:(mt + 1) * 128],
                                      xg[:, par, dt_i, :], start=(dt_i == 0), stop=False)
                        cPE.inc(te.matmul(pb, bqkv_bf[0:1, mt * 128:(mt + 1) * 128],
                                          ones_row[0:1, :], start=False, stop=True))
                    for vt in range(4):
                        if tp > 0 or vt >= 2:
                            pj, pvt = (tp, vt - 2) if vt >= 2 else (tp - 1, vt + 2)
                            te.wait_ge(cAC.h, s2["cAC"] + pj * 8 + 4 + pvt + 1)
                        pb = bA(2 + vt % 2)[:, 0:256]
                        for dt_i in range(DT):
                            te.matmul(pb, xg[:, par, dt_i, vt * 128:(vt + 1) * 128],
                                      wqkv_bf[:, dt_i, 512:768],
                                      start=(dt_i == 0), stop=False)
                        cPE.inc(te.matmul(pb, ones_row[0:1, 0:128],
                                          bqkv_bf[0:1, 512:768], start=False, stop=True))

            @block.scalar
            def _(sc):
                for tp in range(16):
                    for mt in range(4):
                        sc.wait_ge(cPE.h, s2["cPE"] + tp * 8 + mt + 1)
                        dstt = QT if mt < 2 else KT
                        scl = QK_SCALE if mt < 2 else 1.0
                        cAC.inc(sc.activation(
                            out=dstt[:, mt % 2, tp * 512:(tp + 1) * 512],
                            in_=bA(mt % 2), func=AF.Copy, scale=scl))
                    for vt in range(4):
                        sc.wait_ge(cPE.h, s2["cPE"] + tp * 8 + 4 + vt + 1)
                        cAC.inc(sc.activation(out=Vsb[:, tp * 4 + vt, :],
                                              in_=bA(2 + vt % 2)[:, 0:256], func=AF.Copy))
        s4 = snap()
        with contextlib.ExitStack() as e4:
            masksT = e4.enter_context(nc.sbuf_tensor("masksT", [128, 4, 512], F32))
            PT_sb = e4.enter_context(nc.sbuf_tensor("PT_sb", [128, 4, 1024], BF16))
            OT_raw = e4.enter_context(nc.sbuf_tensor("OT_raw", [128, 2, 512], BF16))
            a2a_st = e4.enter_context(nc.sbuf_tensor("a2a_st", [128, 2, 4, 128], BF16))
            m0 = e4.enter_context(nc.sbuf_tensor("m0", [128, 2, 4], F32))
            m0s_sb = e4.enter_context(nc.sbuf_tensor("m0s_sb", [1, 2], F32))
            nm0s = e4.enter_context(nc.sbuf_tensor("nm0s", [128, 32], F32))
            lrow_sb = e4.enter_context(nc.sbuf_tensor("lrow_sb", [1, 512], F32))
            onef = e4.enter_context(nc.sbuf_tensor("onef", [1, 1], F32))
            linv = e4.enter_context(nc.sbuf_tensor("linv", [128, 4], F32))
            SAS = PStream(nc, es, sems, "SAS")

            NHC = 2 * p4_nch
            hcs = [(head, chunk) for head in range(2) for chunk in range(p4_nch)]

            peL, acL, veL = {}, {}, {}
            pe_n = [s4["cPE"]]
            ac_n = [s4["cAC"]]
            ve_n = [s4["cVE"]]

            def _mk(counter, lab_dict):
                def _add(lab, k=1):
                    counter[0] += k
                    lab_dict[lab] = counter[0]
                return _add

            PEp, ACp, VEp = _mk(pe_n, peL), _mk(ac_n, acL), _mk(ve_n, veL)
            for hc in range(NHC + 2):
                if 1 <= hc <= NHC:
                    PEp(("pT1", hc - 1))
                if 2 <= hc <= NHC + 1:
                    PEp(("pTb", hc - 2))
                if hc < NHC:
                    PEp(("pSq", hc))
                if hc < NHC:
                    VEp(("pr1", hc))
                if 1 <= hc <= NHC:
                    VEp(("pr2", hc - 1))
                if 2 <= hc <= NHC + 1:
                    VEp(("pcp", hc - 2))
            for hc, (head, chunk) in enumerate(hcs):
                nu = 2 * (chunk + 1)
                s_order = [1, 0] + list(range(2, nu))
                for i, u in enumerate(s_order):
                    PEp(("ST", hc, u))
                    if i >= 2:
                        PEp(("PVh", hc, s_order[i - 2]))
                for t in (s_order[nu - 2], s_order[nu - 1]):
                    PEp(("PVh", hc, t))
                PEp(("otm", hc))
                PEp(("lrowT", hc))
                for u in range(nu):
                    ACp(("exph", hc, u))
                ACp(("otev", hc))
                for u in range(nu):
                    if u // 2 == chunk:
                        VEp(("maskT", hc, u), 2)
                VEp(("lrowev", hc))
                VEp(("recip", hc))
                VEp(("scale", hc), 4)

            def lastPV(hc):
                chunk = hcs[hc][1]
                return ("PVh", hc, 0) if chunk == 0 else ("PVh", hc, 2 * chunk + 1)

            @block.gpsimd
            def _(g):
                barrier(g, s4)
                for kc in range(4):
                    cGP.inc(g.memset(masksT[:, kc, :], 0.0))
                    g.wait_ge(cGP.h, cGP.n)
                    cGP.inc(g.affine_select(out=masksT[:, kc, :], in_=masksT[:, kc, :],
                                            compare_op=OP.is_ge, fill=NEG_BIG,
                                            base=-kc * 128, pattern=[[1, 512]],
                                            channel_multiplier=-1))
                cGP.inc(g.memset(onef[:], 1.0))

            gp_masks = cGP.n

            @block.tensor
            def _(te):
                barrier(te, s4)
                for it in range(NHC + 2):
                    if 1 <= it <= NHC:
                        k = it - 1
                        tgt = pA if k % 2 == 0 else pB
                        te.wait_ge(cVE.h, veL[("pr1", k)])
                        cPE.inc(te.matmul(tgt[0:1, 0:128], m0[:, k % 2, 0:1],
                                          ident_f[:], start=True, stop=True))
                    if 2 <= it <= NHC + 1:
                        k = it - 2
                        tgt = pB if k % 2 == 0 else pA
                        te.wait_ge(cVE.h, veL[("pr2", k)])
                        if k + 1 < NHC:
                            te.wait_ge(cVE.h, veL[("pr1", k + 1)])
                        cPE.inc(te.matmul(tgt[:, 192:193], ones_rowf[0:1, :],
                                          m0s_sb[0:1, k % 2:k % 2 + 1],
                                          start=True, stop=True))
                    if it < NHC:
                        k = it
                        head, chunk = hcs[k]
                        qb = chunk * 512
                        tgt = pA if k % 2 == 0 else pB
                        if k >= 3:
                            te.wait_ge(cVE.h, veL[("pcp", k - 3)])
                        last = None
                        for qt in range(4):
                            last = te.matmul(
                                tgt[:, qt * 512:(qt + 1) * 512],
                                QT[:, head, qb + qt * 128:qb + (qt + 1) * 128],
                                KT[:, head, 0:512], start=True, stop=True)
                        cPE.inc(last)
                te.wait_ge(cGP.h, gp_masks)
                for hc, (head, chunk) in enumerate(hcs):
                    qb = chunk * 512
                    nu = 2 * (chunk + 1)
                    s_order = [1, 0] + list(range(2, nu))
                    ob = bB(2 + hc % 2)

                    def emit_ST(u):
                        kb = u // 2
                        h = u % 2
                        p = u % 2
                        if u == 0:
                            if hc > 0:
                                te.wait_ge(cVE.h, veL[("scale", hc - 1)])
                        elif u == 1:
                            if hc > 0:
                                te.wait_ge(cAC.h, acL[("otev", hc - 1)])
                            else:
                                te.wait_ge(cVE.h, veL[("pcp", NHC - 1)])
                        else:
                            te.wait_ge(cAC.h, acL[("exph", hc, u - 2)])
                        last = None
                        for j in range(2):
                            kc = 2 * h + j
                            koff = kb * 512 + kc * 128
                            last = te.matmul(
                                pA[:, p * 1024 + j * 512:p * 1024 + (j + 1) * 512],
                                KT[:, head, koff:koff + 128],
                                QT[:, head, qb:qb + 512],
                                start=True, stop=True)
                        cPE.inc(last)

                    def emit_PVh(u, first, last_f):
                        kb = u // 2
                        h = u % 2
                        p = u % 4
                        te.wait_ge(cAC.h, acL[("exph", hc, u)])
                        if first and hc >= 2:
                            te.wait_ge(cAC.h, acL[("otev", hc - 2)])
                        last = None
                        for j in range(2):
                            kc = 2 * h + j
                            te.matmul(ob, Vsb[:, kb * 4 + kc,
                                              head * 128:(head + 1) * 128],
                                      PT_sb[:, p, j * 512:(j + 1) * 512],
                                      start=(first and j == 0),
                                      stop=(last_f and j == 1),
                                      skip_group_check=True)
                        for j in range(2):
                            last = te.matmul(
                                bB(0)[0:1, 0:512], ones_col[:],
                                PT_sb[:, p, j * 512:(j + 1) * 512],
                                start=(first and j == 0),
                                stop=(last_f and j == 1),
                                skip_group_check=True)
                        cPE.inc(last)

                    npv = 0
                    for i, u in enumerate(s_order):
                        emit_ST(u)
                        if i >= 2:
                            emit_PVh(s_order[i - 2], npv == 0, npv == nu - 1)
                            npv += 1
                    for t in (s_order[nu - 2], s_order[nu - 1]):
                        emit_PVh(t, npv == 0, npv == nu - 1)
                        npv += 1
                    te.wait_ge(cAC.h, acL[("otev", hc)])
                    last = None
                    for qt in range(4):
                        last = te.matmul(pA[:, qt * 128:(qt + 1) * 128],
                                         OT_raw[:, hc % 2, qt * 128:(qt + 1) * 128],
                                         ident_bf[:], start=True, stop=True)
                    cPE.inc(last)
                    te.wait_ge(cVE.h, veL[("lrowev", hc)])
                    last = None
                    for qt in range(4):
                        last = te.matmul(bB(1)[:, qt:qt + 1],
                                         lrow_sb[0:1, qt * 128:(qt + 1) * 128],
                                         onef[0:1, 0:1], start=True, stop=True)
                    cPE.inc(last)

            @block.scalar
            def _(sc):
                barrier(sc, s4)
                sc.wait_ge(cVE.h, veL[("pcp", NHC - 1)])
                for hc, (head, chunk) in enumerate(hcs):
                    nu = 2 * (chunk + 1)
                    for u in range(nu):
                        p = u % 4
                        sc.wait_ge(cPE.h, peL[("ST", hc, u)])
                        if u // 2 == chunk:
                            sc.wait_ge(cVE.h, veL[("maskT", hc, u)])
                        if u < 4:
                            if hc > 0:
                                sc.wait_ge(cPE.h, peL[lastPV(hc - 1)])
                        else:
                            sc.wait_ge(cPE.h, peL[("PVh", hc, u - 4)])
                        cAC.inc(sc.activation(out=PT_sb[:, p, 0:1024],
                                              in_=pA[:, (u % 2) * 1024:
                                                      (u % 2 + 1) * 1024],
                                              func=AF.Exp, bias=nm0s[:, hc:hc + 1],
                                              scale=1.0))
                    sc.wait_ge(cPE.h, peL[lastPV(hc)])
                    cAC.inc(sc.activation(out=OT_raw[:, hc % 2, :],
                                          in_=bB(2 + hc % 2), func=AF.Copy))

            @block.vector
            def _(v):
                barrier(v, s4)
                for it in range(NHC + 2):
                    if it < NHC:
                        k = it
                        tgt = pA if k % 2 == 0 else pB
                        v.wait_ge(cPE.h, peL[("pSq", k)])
                        cVE.inc(v.reduce_max(m0[:, k % 2, 0:1], tgt[:, :],
                                             AX.X, negate=True))
                    if 1 <= it <= NHC:
                        k = it - 1
                        tgt = pA if k % 2 == 0 else pB
                        v.wait_ge(cPE.h, peL[("pT1", k)])
                        v.wait_ge(cVE.h, cVE.n)
                        cVE.inc(v.tensor_reduce(m0s_sb[0:1, k % 2:k % 2 + 1],
                                                tgt[0:1, 0:128], AX.X, OP.min))
                    if 2 <= it <= NHC + 1:
                        k = it - 2
                        tgt = pB if k % 2 == 0 else pA
                        v.wait_ge(cPE.h, peL[("pTb", k)])
                        v.wait_ge(cVE.h, cVE.n)
                        cVE.inc(v.tensor_copy(nm0s[:, k:k + 1], tgt[:, 192:193]))
                v.wait_ge(cGP.h, gp_masks)
                for hc, (head, chunk) in enumerate(hcs):
                    nu = 2 * (chunk + 1)
                    for u in range(nu):
                        if u // 2 == chunk:
                            p = u % 2
                            v.wait_ge(cPE.h, peL[("ST", hc, u)])
                            for j in range(2):
                                kc = 2 * (u % 2) + j
                                cVE.inc(v.tensor_tensor(
                                    pA[:, p * 1024 + j * 512:p * 1024 + (j + 1) * 512],
                                    pA[:, p * 1024 + j * 512:p * 1024 + (j + 1) * 512],
                                    masksT[:, kc, :], OP.add))
                    v.wait_ge(cPE.h, peL[lastPV(hc)])
                    v.wait_ge(cVE.h, cVE.n)
                    cVE.inc(v.tensor_copy(lrow_sb[:], bB(0)[0:1, 0:512]))
                    v.wait_ge(cPE.h, peL[("otm", hc)])
                    v.wait_ge(cPE.h, peL[("lrowT", hc)])
                    if hc >= 2:
                        SAS.wait(v, SAS.tok(hc - 2))
                    v.wait_ge(cVE.h, cVE.n)
                    cVE.inc(v.reciprocal(linv[:], bB(1)[:, 0:4]))
                    v.wait_ge(cVE.h, cVE.n)
                    for qt in range(4):
                        cVE.inc(v.tensor_scalar(a2a_st[:, hc % 2, qt, :],
                                                pA[:, qt * 128:(qt + 1) * 128],
                                                linv[:, qt:qt + 1], None, OP.mult))

            @block.sync
            def _(sy):
                barrier(sy, s4)
                for hc, (head, chunk) in enumerate(hcs):
                    sy.wait_ge(cVE.h, veL[("scale", hc)])
                    dst = a2a_in[chunk * 512:(chunk + 1) * 512,
                                 head * 128:(head + 1) * 128]
                    SAS.issue(sy, dst.rearrange("(a p) c -> p a c", p=128),
                              a2a_st[:, hc % 2, :, :], hc)

            st_a2a = [s.n for s in SAS.sems]

            @block.gpsimd
            def _(g):
                for s, v in zip(SAS.sems, st_a2a):
                    g.wait_ge(s.h, v)
                cCC.inc(g.collective_compute("AllGather", OP.bypass, replica_groups=RG,
                                             ins=[a2a_in.opt()], outs=[ag_o.opt()]), 1)
                dG.inc(g.dma_start(out=oofs_sb[:], in_=oofs[:, :]), 16)
                g.wait_ge(cCC.h, cCC.n)
                g.wait_ge(dG.h, dG.n)
                with g.register("vreg") as vreg:
                    for i in range(NCORES):
                        g.reg_load(vreg, oofs_sb[0:1, i:i + 1])
                        v = g.snap(vreg, min_val=0, max_val=NCORES * SEQ - TOWN)
                        dG.inc(g.dma_start(out=a2a_out[i * TOWN:(i + 1) * TOWN, :],
                                           in_=ag_o[bass.ds(v, TOWN), :]), 16)

        exB.close()
        cc_a2a = cCC.n

        s3 = snap()
        with contextlib.ExitStack() as e3:
            xnt = e3.enter_context(nc.sbuf_tensor("xnt3", [128, DT, 1024], BF16))
            SXN = PStream(nc, es, sems, "SXN")
            ggs = e3.enter_context(nc.sbuf_tensor("ggs", [128, 2, 2, DT, 256], BF16))
            hst = e3.enter_context(nc.sbuf_tensor("hst", [128, 2, 1024], BF16))
            gssb = e3.enter_context(nc.sbuf_tensor("gssb", [128, 3, 512], F32))
            hsq = e3.enter_context(nc.sbuf_tensor("hsq", [128, 2, 1024], BF16))
            rrow = e3.enter_context(nc.sbuf_tensor("rrow", [1, TOWN], F32))
            rrow2 = e3.enter_context(nc.sbuf_tensor("rrow2", [1, TOWN], F32))
            SGG = PStream(nc, es, sems, "SGG")
            SHS = PStream(nc, es, sems, "SHS")

            def pe_hp(hp):
                return s3["cPE"] + (hp + 1) * 5

            @block.sync
            def _(sy):
                barrier(sy, s3, skip=("cCC", "dG"))
                sy.wait_ge(cCC.h, cc_ffn)
                SXN.issue(sy, xnt[:],
                          xnt_in.rearrange("(a p) t -> p a t", p=128), 0)

                def st_h(hp):
                    sy.wait_ge(cVE.h, s3["cVE"] + (hp + 1) * 2)
                    SHS.issue(sy, h_dram[hp * 128:(hp + 1) * 128, :],
                              hst[:, hp % 2, :], hp)

                for hq in range(32):
                    if hq >= 2:
                        sy.wait_ge(cPE.h, pe_hp(2 * hq - 3))
                    r = hq // 8
                    hcol = (hq * 256) % 2048
                    for gi, blk in enumerate([r, 4 + r]):
                        src = ggu_g[blk * 2048:(blk + 1) * 2048, hcol:hcol + 256]
                        SGG.issue(sy, ggs[:, hq % 2, gi, :, :],
                                  src.rearrange("(a p) c -> p a c", p=128), hq)
                    if hq >= 1:
                        st_h(2 * hq - 2)
                        st_h(2 * hq - 1)
                for hp in (62, 63):
                    st_h(hp)

            @block.tensor
            def _(te):
                barrier(te, s3, skip=("cCC", "dG"))
                SXN.wait(te, SXN.tok(0))
                for hp in range(64):
                    hq, hloc = hp // 2, hp % 2
                    SGG.wait(te, SGG.tok(hq, per_grp=2))
                    for th in range(2):
                        u = hp * 2 + th
                        if u >= 3:
                            v_ = u - 3
                            te.wait_ge(cAC.h, s3["cAC"] + (v_ // 2) * 3 + v_ % 2 + 1)
                            te.wait_ge(cVE.h, s3["cVE"] + u - 2)
                        gb = [bA(0), bA(1), bB(2)][u % 3]
                        ub = [bA(2), bA(3), bB(3)][u % 3]
                        for gi, pb in [(0, gb), (1, ub)]:
                            last = None
                            for dt_i in range(DT):
                                last = te.matmul(
                                    pb, ggs[:, hq % 2, gi, dt_i,
                                            hloc * 128:(hloc + 1) * 128],
                                    xnt[:, dt_i, th * 512:(th + 1) * 512],
                                    start=(dt_i == 0), stop=(dt_i == DT - 1))
                            cPE.inc(last)
                    te.wait_ge(cAC.h, s3["cAC"] + hp * 3 + 3)
                    last = None
                    for th in range(2):
                        last = te.matmul(bB(th)[0:1, :], ones_col[:],
                                         hsq[:, hp % 2, th * 512:(th + 1) * 512],
                                         start=(hp == 0), stop=(hp == 63),
                                         skip_group_check=True)
                    cPE.inc(last)

            @block.scalar
            def _(sc):
                barrier(sc, s3, skip=("cCC", "dG"))
                for hp in range(64):
                    for th in range(2):
                        u = hp * 2 + th
                        sc.wait_ge(cPE.h, s3["cPE"] + hp * 5 + th * 2 + 1)
                        if u >= 3:
                            sc.wait_ge(cVE.h, s3["cVE"] + u - 2)
                        cAC.inc(sc.activation(out=gssb[:, u % 3, :],
                                              in_=[bA(0), bA(1), bB(2)][u % 3],
                                              func=AF.Silu))
                    sc.wait_ge(cVE.h, s3["cVE"] + (hp + 1) * 2)
                    cAC.inc(sc.activation(out=hsq[:, hp % 2, :], in_=hst[:, hp % 2, :],
                                          func=AF.Square))

            @block.vector
            def _(v):
                barrier(v, s3, skip=("cCC", "dG"))
                for hp in range(64):
                    for th in range(2):
                        u = hp * 2 + th
                        v.wait_ge(cAC.h, s3["cAC"] + hp * 3 + th + 1)
                        v.wait_ge(cPE.h, s3["cPE"] + hp * 5 + th * 2 + 2)
                        if hp >= 2 and th == 0:
                            SHS.wait(v, SHS.tok(hp - 2))
                        cVE.inc(v.tensor_tensor(hst[:, hp % 2, th * 512:(th + 1) * 512],
                                                gssb[:, u % 3, :],
                                                [bA(2), bA(3), bB(3)][u % 3],
                                                OP.mult))
                v.wait_ge(cPE.h, pe_hp(63))
                for th in range(2):
                    cVE.inc(v.tensor_scalar(rrow[0:1, th * 512:(th + 1) * 512],
                                            bB(th)[0:1, :], 1.0 / HFFN, RMS_EPS,
                                            OP.mult, OP.add))

            @block.scalar
            def _(sc):
                sc.wait_ge(cVE.h, s3["cVE"] + 130)
                cAC.inc(sc.activation(out=rrow2[:], in_=rrow[:], func=AF.Sqrt))

            @block.vector
            def _(v):
                v.wait_ge(cAC.h, s3["cAC"] + 193)
                cVE.inc(v.reciprocal(rrow[:], rrow2[:]))
                v.wait_ge(cVE.h, cVE.n)

            @block.tensor
            def _(te):
                te.wait_ge(cVE.h, s3["cVE"] + 131)
                for th in range(2):
                    last = te.matmul(bB(2 + th), ones_rowf[0:1, :],
                                     rrow[0:1, th * 512:(th + 1) * 512],
                                     start=True, stop=True)
                cPE.inc(last)

            @block.vector
            def _(v):
                v.wait_ge(cPE.h, pe_hp(63) + 1)
                for th in range(2):
                    cVE.inc(v.tensor_copy(rstd_bc[:, th * 512:(th + 1) * 512],
                                          bB(2 + th)))
        s5 = snap()
        with contextlib.ExitStack() as e5:
            ot_in = e5.enter_context(nc.sbuf_tensor("ot_in", [128, 2, 2048], BF16))
            OTs = e5.enter_context(nc.sbuf_tensor("OTs", [128, DT, 1024], BF16))
            wo_b = e5.enter_context(nc.sbuf_tensor("wo_b", [128, 2, DT, 512], BF16))
            gp_b = e5.enter_context(nc.sbuf_tensor("gp_b", [128, 4, 4, 512], BF16))
            h_b = e5.enter_context(nc.sbuf_tensor("h_b", [128, 4, 4, 1024], BF16))
            hn_b = e5.enter_context(nc.sbuf_tensor("hn_b", [128, 2, 4, 1024], BF16))
            fst = e5.enter_context(nc.sbuf_tensor("fst", [128, 8, 512], F32))
            x_b = e5.enter_context(nc.sbuf_tensor("x_b", [128, 2, 512], F32))
            o_st = e5.enter_context(nc.sbuf_tensor("o_st", [128, 2, 512], F32))
            bout_bf = e5.enter_context(nc.sbuf_tensor("bout_bf", [1, D], BF16))
            SOT = PStream(nc, es, sems, "SOT")
            SGP = PStream(nc, es, sems, "SGP")
            SXB = PStream(nc, es, sems, "SXB")
            SOS = PStream(nc, es, sems, "SOS")
            SWO = PStream(nc, es, sems, "SWO")
            SGQ = PStream(nc, es, sems, "SGQ")

            @block.sync
            def _(sy):
                barrier(sy, s5)
                sy.wait_ge(cCC.h, cc_a2a)
                for tt in range(8):
                    if tt >= 2:
                        sy.wait_ge(cPE.h, s5["cPE"] + (tt - 1) * 4)
                    src = a2a_out.rearrange("(i q) c -> i q c", q=1024)[
                        :, tt * 128:(tt + 1) * 128, :].transpose([1, 0, 2])
                    SOT.issue(sy, ot_in[:, tt % 2, :].rearrange("p (i c) -> p i c",
                                                                c=256), src, tt)

            @block.tensor
            def _(te):
                barrier(te, s5)
                for tt in range(8):
                    SOT.wait(te, SOT.tok(tt))
                    for grp in range(4):
                        gidx = tt * 4 + grp
                        if gidx >= 2:
                            te.wait_ge(cAC.h, s5["cAC"] + gidx - 1)
                        last = None
                        for j in range(4):
                            dt_i = grp * 4 + j
                            last = te.matmul(bB(grp % 2)[:, j * 128:(j + 1) * 128],
                                             ot_in[:, tt % 2,
                                                   dt_i * 128:(dt_i + 1) * 128],
                                             ident_bf[:], start=True, stop=True)
                        cPE.inc(last)

            @block.scalar
            def _(sc):
                barrier(sc, s5)
                for gidx in range(32):
                    tt, grp = gidx // 4, gidx % 4
                    sc.wait_ge(cPE.h, s5["cPE"] + gidx + 1)
                    cAC.inc(sc.activation(
                        out=OTs[:, grp * 4:(grp + 1) * 4, tt * 128:(tt + 1) * 128],
                        in_=bB(grp % 2), func=AF.Copy))

            pe_ot = s5["cPE"] + 32
            ac_ot = s5["cAC"] + 32
            s5b = snap()

            pe_z = [[0] * 8 for _ in range(4)]
            pe_y = [[0] * 64 for _ in range(4)]
            pe_T = [0] * 4
            ve_fev = [0] * 4
            ve_xadd = [[0] * 8 for _ in range(4)]
            ve_hn = [[0] * 16 for _ in range(4)]
            pe = pe_ot
            ve = s5b["cVE"]
            for dg in range(4):
                for hq4 in range(16):
                    ve += 1; ve_hn[dg][hq4] = ve
                for b in range(8):
                    pe += 1; pe_z[dg][b] = pe
                for hc in range(64):
                    pe += 1; pe_y[dg][hc] = pe
                for b in range(8):
                    ve += 1
                ve_fev[dg] = ve
                pe += 1; pe_T[dg] = pe
                for tt in range(8):
                    ve += 1; ve_xadd[dg][tt] = ve

            @block.gpsimd
            def _(g):
                barrier(g, s5b)
                dG.inc(g.dma_start(out=bout_bf[:], in_=bout[:, :]), 16)
                for dg in range(4):
                    if dg >= 2:
                        g.wait_ge(cPE.h, pe_z[dg - 2][7])
                    src = wout[:, dg * 512:(dg + 1) * 512]
                    SWO.issue(g, wo_b[:, dg % 2, :, :],
                              src.rearrange("(a p) c -> p a c", p=128), dg)

            @block.sync
            def _(sy):
                for dg in range(4):
                    for hq4 in range(16):
                        if dg > 0 or hq4 >= 4:
                            pg, ph = (dg, hq4 - 4) if hq4 >= 4 else (dg - 1, 12 + hq4)
                            sy.wait_ge(cPE.h, pe_y[pg][ph * 4 + 3])
                            sy.wait_ge(cVE.h, ve_hn[pg][ph])
                        gidx = dg * 16 + hq4
                        SGP.issue(sy, h_b[:, hq4 % 4, :, :],
                                  h_dram[hq4 * 512:(hq4 + 1) * 512, :].rearrange(
                                      "(a p) c -> p a c", p=128), gidx)
                    def ld_x(tt):
                        if dg > 0 or tt >= 2:
                            pg, pt = (dg, tt - 2) if tt >= 2 else (dg - 1, 6 + tt)
                            sy.wait_ge(cVE.h, ve_xadd[pg][pt])
                        SXB.issue(sy, x_b[:, tt % 2, :],
                                  x_own[tt * 128:(tt + 1) * 128,
                                        dg * 512:(dg + 1) * 512], dg * 8 + tt)

                    def st_o(tt):
                        sy.wait_ge(cVE.h, ve_xadd[dg][tt])
                        SOS.issue(sy,
                                  out[tt * 128:(tt + 1) * 128,
                                      dg * 512:(dg + 1) * 512],
                                  o_st[:, tt % 2, :], dg * 8 + tt)

                    ld_x(0)
                    ld_x(1)
                    for tt in range(2, 8):
                        st_o(tt - 2)
                        ld_x(tt)
                    st_o(6)
                    st_o(7)
                for s in SOS.sems:
                    sy.wait_ge(s.h, s.n)
                sy.wait_ge(dL.h, dL.n)

            @block.scalar
            def _(sc):
                for dg in range(4):
                    for hq4 in range(16):
                        if dg > 0 or hq4 >= 4:
                            pg, ph = (dg, hq4 - 4) if hq4 >= 4 else (dg - 1, 12 + hq4)
                            sc.wait_ge(cPE.h, pe_y[pg][ph * 4 + 3])
                            sc.wait_ge(cVE.h, ve_hn[pg][ph])
                        gidx = dg * 16 + hq4
                        SGQ.issue(sc, gp_b[:, hq4 % 4, :, :],
                                  gp_g[hq4 * 512:(hq4 + 1) * 512,
                                       dg * 512:(dg + 1) * 512].rearrange(
                                           "(a p) c -> p a c", p=128), gidx)

            @block.tensor
            def _(te):
                te.wait_ge(cAC.h, ac_ot)
                te.wait_ge(dG.h, s5b["dG"] + 16)
                for dg in range(4):
                    SWO.wait(te, SWO.tok(dg))
                    for b in range(8):
                        mt, th = b // 2, b % 2
                        pb = bA(b % 4) if b < 4 else bB(b % 4)
                        if dg > 0:
                            te.wait_ge(cVE.h, ve_xadd[dg - 1][b])
                        for dt_i in range(DT):
                            te.matmul(pb, wo_b[:, dg % 2, dt_i, mt * 128:(mt + 1) * 128],
                                      OTs[:, dt_i, th * 512:(th + 1) * 512],
                                      start=(dt_i == 0), stop=False,
                                      skip_group_check=True)
                        cPE.inc(te.matmul(
                            pb,
                            bout_bf[0:1, dg * 512 + mt * 128:dg * 512 + (mt + 1) * 128],
                            ones_row[0:1, :], start=False, stop=False,
                            skip_group_check=True))
                    for hc in range(64):
                        hq4, hs = hc // 4, hc % 4
                        if hs == 0:
                            te.wait_ge(cVE.h, ve_hn[dg][hq4])
                            SGQ.wait(te, SGQ.tok(dg * 16 + hq4))
                        last = None
                        for b in range(8):
                            mt, th = b // 2, b % 2
                            pb = bA(b % 4) if b < 4 else bB(b % 4)
                            last = te.matmul(
                                pb, gp_b[:, hq4 % 4, hs, mt * 128:(mt + 1) * 128],
                                hn_b[:, hq4 % 2, hs, th * 512:(th + 1) * 512],
                                start=False, stop=(hc == 63),
                                skip_group_check=True)
                        cPE.inc(last)
                    te.wait_ge(cVE.h, ve_fev[dg])
                    last = None
                    for b in range(8):
                        mt, th = b // 2, b % 2
                        for tsub in range(4):
                            j = th * 4 + tsub
                            tb = bA(j % 4) if j < 4 else bB(j % 4)
                            last = te.matmul(tb[:, mt * 128:(mt + 1) * 128],
                                             fst[:, b, tsub * 128:(tsub + 1) * 128],
                                             ident_f[:], start=True, stop=True,
                                             skip_group_check=True)
                    cPE.inc(last)

            @block.vector
            def _(v):
                for dg in range(4):
                    for hq4 in range(16):
                        SGP.wait(v, SGP.tok(dg * 16 + hq4))
                        if dg > 0 or hq4 >= 2:
                            pg, ph = (dg, hq4 - 2) if hq4 >= 2 else (dg - 1, 14 + hq4)
                            v.wait_ge(cPE.h, pe_y[pg][ph * 4 + 3])
                        cVE.inc(v.tensor_tensor(
                            hn_b[:, hq4 % 2, :, :], h_b[:, hq4 % 4, :, :],
                            rstd_bc[:].unsqueeze(1).broadcast_to([128, 4, 1024]),
                            OP.mult))
                    v.wait_ge(cPE.h, pe_y[dg][63])
                    for b in range(8):
                        pb = bA(b % 4) if b < 4 else bB(b % 4)
                        cVE.inc(v.tensor_copy(fst[:, b, :], pb))
                    v.wait_ge(cPE.h, pe_T[dg])
                    for tt in range(8):
                        tb = bA(tt % 4) if tt < 4 else bB(tt % 4)
                        SXB.wait(v, SXB.tok(dg * 8 + tt))
                        if dg > 0 or tt >= 2:
                            pg, pt = (dg, tt - 2) if tt >= 2 else (dg - 1, 6 + tt)
                            SOS.wait(v, SOS.tok(pg * 8 + pt))
                        cVE.inc(v.tensor_tensor(o_st[:, tt % 2, :], x_b[:, tt % 2, :],
                                                tb, OP.add))

    return nc


def _finish_dbg(nc, block, sems, snp, dbg_aps, dbg, srcs=None, sb_srcs=None):
    dL = sems["dL"]

    @block.sync
    def _(sy):
        for nm, v in snp.items():
            if v > 0:
                sy.wait_ge(sems[nm].h, v)
        for name in dbg:
            if srcs and name in srcs:
                dL.inc(sy.dma_start(out=dbg_aps[name][:, :], in_=srcs[name][:, :]), 16)
            elif sb_srcs and name in sb_srcs:
                ap = sb_srcs[name]
                d = dbg_aps[name]
                if len(ap.shape) == 3:
                    d3 = d.rearrange("(a p) c -> p a c", p=128)
                    dL.inc(sy.dma_start(out=d3, in_=ap), 16)
                else:
                    dL.inc(sy.dma_start(out=d[:, :], in_=ap), 16)
        sy.wait_ge(dL.h, dL.n)


_NC_CACHE = {}


def _get_nc(phases=99, dbg=()):
    key = (phases, tuple(dbg))
    if key not in _NC_CACHE:
        _NC_CACHE[key] = build_nc(phases, dbg)
    return _NC_CACHE[key]


def make_in_maps(L0_attn_x, L0_attn_W_qkv, L0_attn_b_qkv, L0_attn_W_out,
                 L0_attn_b_out, L0_ffn_w_gate_up, L0_ffn_w_proj):
    x = np.ascontiguousarray(np.asarray(L0_attn_x).reshape(SEQ, D))
    Wqkv = np.asarray(L0_attn_W_qkv)
    bq = np.asarray(L0_attn_b_qkv)
    Wout = np.ascontiguousarray(np.asarray(L0_attn_W_out))
    bo = np.ascontiguousarray(np.asarray(L0_attn_b_out).reshape(1, D))
    wgu_full = np.asarray(L0_ffn_w_gate_up)
    wp_full = np.asarray(L0_ffn_w_proj)
    in_maps = []
    for c in range(NCORES):
        cols = [Wqkv[:, p * D + c * 256:p * D + (c + 1) * 256] for p in range(3)]
        wq = np.ascontiguousarray(np.concatenate(cols, axis=1))
        bqc = np.ascontiguousarray(
            np.concatenate([bq[p * D + c * 256:p * D + (c + 1) * 256]
                            for p in range(3)]).reshape(1, 768))
        in_maps.append({
            "x_own": np.ascontiguousarray(x[c * TOWN:(c + 1) * TOWN]),
            "oofs": np.array([[i * SEQ + c * TOWN for i in range(NCORES)]], np.int32),
            "wqkv": wq, "bqkv": bqc, "wout": Wout, "bout": bo,
            "wgu": np.ascontiguousarray(wgu_full[c * 2048:(c + 1) * 2048]),
            "wp": np.ascontiguousarray(wp_full[:, c * 1024:(c + 1) * 1024]),
        })
    return in_maps


def kernel(**inputs):
    nc = _get_nc()
    in_maps = make_in_maps(**inputs)
    res = run_bass_kernel_spmd(nc, in_maps, core_ids=list(range(NCORES)))
    outp = np.concatenate([res.results[c]["out"] for c in range(NCORES)], axis=0)
    return outp.reshape(1, SEQ, D)


# version 23
# speedup vs baseline: 1.0448x; 1.0081x over previous
import contextlib
import numpy as np
import concourse.bass as bass
import concourse.mybir as mybir
from concourse.bass_utils import run_bass_kernel_spmd

F32 = mybir.dt.float32
BF16 = mybir.dt.bfloat16
AF = mybir.ActivationFunctionType
OP = mybir.AluOpType
AX = mybir.AxisListType

NCORES = 8
D = 2048
SEQ = 8192
TOWN = 1024
NCH = 16
DT = 16
HFFN = 8192
MAGIC = 12582912.0
QK_SCALE = 0.08838834764831845
LN_EPS = 1e-5
RMS_EPS = float(np.finfo(np.float32).eps)
NEG_BIG = -3.0e38
RG = [list(range(NCORES))]


class Sem:
    def __init__(self, h, name):
        self.h = h
        self.name = name
        self.n = 0

    def inc(self, instr, k=1):
        instr.then_inc(self.h, k)
        self.n += k
        return self.n


class PStream:

    def __init__(self, nc, scope, sems, name):
        self.sems = []
        for i in range(2):
            nm = f"{name}{i}"
            s = Sem(scope.enter_context(nc.semaphore(nm)), nm)
            sems[nm] = s
            self.sems.append(s)

    def issue(self, q, out, in_, grp):
        s = self.sems[grp % 2]
        s.inc(q.dma_start(out=out, in_=in_), 16)
        return (grp % 2, s.n)

    def tok(self, grp, per_grp=1):
        return (grp % 2, (grp // 2 + 1) * 16 * per_grp)

    def wait(self, eng, tok):
        eng.wait_ge(self.sems[tok[0]].h, tok[1])


def build_nc(phases=99, dbg=(), p4_nch=NCH, p4_tail=True):
    nc = bass.Bass("TRN2", target_bir_lowering=False, debug=False)

    x_own = nc.dram_tensor("x_own", [TOWN, D], F32, kind="ExternalInput").ap()
    wqkv = nc.dram_tensor("wqkv", [D, 768], F32, kind="ExternalInput").ap()
    bqkv = nc.dram_tensor("bqkv", [1, 768], F32, kind="ExternalInput").ap()
    wout = nc.dram_tensor("wout", [D, D], F32, kind="ExternalInput").ap()
    bout = nc.dram_tensor("bout", [1, D], F32, kind="ExternalInput").ap()
    wgu = nc.dram_tensor("wgu", [2048, 2048], F32, kind="ExternalInput").ap()
    wp = nc.dram_tensor("wp", [2048, 1024], F32, kind="ExternalInput").ap()
    out = nc.dram_tensor("out", [TOWN, D], F32, kind="ExternalOutput").ap()

    ggu_in = nc.dram_tensor("ggu_in", [2048, 2048], BF16).ap()
    ggu_g = nc.dram_tensor("ggu_g", [16384, 2048], BF16, addr_space="Shared").ap()
    gp_in = nc.dram_tensor("gp_in", [1024, 2048], BF16).ap()
    gp_g = nc.dram_tensor("gp_g", [8192, 2048], BF16, addr_space="Shared").ap()
    xnt_in = nc.dram_tensor("xnt_in", [2048, 1024], BF16).ap()
    xnt_g = nc.dram_tensor("xnt_g", [16384, 1024], BF16, addr_space="Shared").ap()
    h_dram = nc.dram_tensor("h_dram", [HFFN, TOWN], BF16).ap()
    hn_dram = nc.dram_tensor("hn_dram", [HFFN, TOWN], BF16).ap()
    a2a_in = nc.dram_tensor("a2a_in", [SEQ, 256], BF16).ap()
    ag_o = nc.dram_tensor("ag_o", [NCORES * SEQ, 256], BF16, addr_space="Shared").ap()
    a2a_out = nc.dram_tensor("a2a_out", [SEQ, 256], BF16).ap()
    oofs = nc.dram_tensor("oofs", [1, NCORES], mybir.dt.int32, kind="ExternalInput").ap()

    dbg_specs = {
        "ggu_g": ([16384, 2048], BF16), "gp_g": ([8192, 2048], BF16),
        "xnt_g": ([16384, 1024], BF16),
        "qt_d": ([256, SEQ], BF16), "kt_d": ([256, SEQ], BF16),
        "v_d": ([SEQ, 256], BF16),
        "h_d": ([HFFN, TOWN], BF16), "rstd_d": ([128, TOWN], F32),
        "a2a_d": ([SEQ, 256], BF16), "hn_d": ([HFFN, TOWN], BF16),
    }
    dbg_aps = {name: nc.dram_tensor("d_" + name, *dbg_specs[name],
                                    kind="ExternalOutput").ap()
               for name in dbg}

    es = contextlib.ExitStack()
    with es:
        ident_bf = es.enter_context(nc.sbuf_tensor("ident_bf", [128, 128], BF16))
        ident_f = es.enter_context(nc.sbuf_tensor("ident_f", [128, 128], F32))
        ones_row = es.enter_context(nc.sbuf_tensor("ones_row", [1, 512], BF16))
        ones_col = es.enter_context(nc.sbuf_tensor("ones_col", [128, 1], BF16))
        ones_rowf = es.enter_context(nc.sbuf_tensor("ones_rowf", [1, 128], F32))
        rstd_bc = es.enter_context(nc.sbuf_tensor("rstd_bc", [128, TOWN], F32))
        oofs_sb = es.enter_context(nc.sbuf_tensor("oofs_sb", [1, NCORES], mybir.dt.int32))

        pA = es.enter_context(nc.psum_tensor("pA", [128, 2048], F32))
        pB = es.enter_context(nc.psum_tensor("pB", [128, 2048], F32))

        def bA(i):
            return pA[:, i * 512:(i + 1) * 512]

        def bB(i):
            return pB[:, i * 512:(i + 1) * 512]

        sems = {}
        for nm in ["dL", "dG", "cPE", "cAC", "cVE", "cGP", "cCC"]:
            sems[nm] = Sem(es.enter_context(nc.semaphore(nm)), nm)
        dL, dG = sems["dL"], sems["dG"]
        cPE, cAC, cVE, cGP, cCC = (sems[k] for k in ["cPE", "cAC", "cVE", "cGP", "cCC"])

        def snap():
            return {nm: s.n for nm, s in sems.items()}

        block = es.enter_context(nc.Block())

        def barrier(eng, snp, skip=()):
            for nm, v in snp.items():
                if v > 0 and nm not in skip:
                    eng.wait_ge(sems[nm].h, v)

        @block.gpsimd
        def _(g):
            for t_ in (ident_bf, ident_f):
                cGP.inc(g.memset(t_[:], 0.0))
                g.wait_ge(cGP.h, cGP.n)
                cGP.inc(g.affine_select(out=t_[:], in_=t_[:], compare_op=OP.not_equal,
                                        fill=1.0, base=0, pattern=[[-1, 128]],
                                        channel_multiplier=1))
            cGP.inc(g.memset(ones_row[:], 1.0))
            cGP.inc(g.memset(ones_col[:], 1.0))
            cGP.inc(g.memset(ones_rowf[:], 1.0))

        qw = es.enter_context(nc.sbuf_tensor("qw", [128, 2, 2048], BF16))
        qh = es.enter_context(nc.sbuf_tensor("qh", [128, 2, 2048], BF16))
        qg = es.enter_context(nc.sbuf_tensor("qg", [128, 2, 32], F32))
        qsb = es.enter_context(nc.sbuf_tensor("qsb", [128, 2, 32], BF16))
        qgr = es.enter_context(nc.sbuf_tensor("qgr", [128, 2, 32], F32))
        qstage = es.enter_context(nc.sbuf_tensor("qstage", [128, 2, 512], BF16))
        qr = es.enter_context(nc.sbuf_tensor("qr", [128, 2048], F32))

        s1 = snap()
        with contextlib.ExitStack() as e1:
            xnt = e1.enter_context(nc.sbuf_tensor("xnt", [128, DT, 1024], BF16))
            xt = e1.enter_context(nc.sbuf_tensor("xt", [128, 2, 2048], F32))
            sqs = e1.enter_context(nc.sbuf_tensor("sqs", [128, 2048], F32))
            xn_tm = e1.enter_context(nc.sbuf_tensor("xn_tm", [128, 2, 2048], BF16))
            stt = e1.enter_context(nc.sbuf_tensor("stt", [128, 2, 8], F32))
            xn_f = e1.enter_context(nc.sbuf_tensor("xn_f", [128, 2048], F32))
            NA = 6
            NV1 = 9
            SX1 = PStream(nc, es, sems, "SX1")

            @block.sync
            def _(sy):
                barrier(sy, s1)
                for i in range(8):
                    if i >= 2:
                        sy.wait_ge(cVE.h, s1["cVE"] + (i - 1) * NV1)
                        sy.wait_ge(cAC.h, s1["cAC"] + (i - 2) * NA + 1)
                    SX1.issue(sy, xt[:, i % 2, :], x_own[i * 128:(i + 1) * 128, :], i)

            @block.scalar
            def _(sc):
                barrier(sc, s1)
                for i in range(8):
                    par = i % 2
                    SX1.wait(sc, SX1.tok(i))
                    cAC.inc(sc.activation(out=sqs[:], in_=xt[:, par, :], func=AF.Square,
                                          accum_out=stt[:, par, 6:7]))
                    sc.wait_ge(cVE.h, s1["cVE"] + i * NV1 + 5)
                    cAC.inc(sc.activation(out=stt[:, par, 2:3], in_=stt[:, par, 1:2],
                                          func=AF.Sqrt))
                    for grp in range(4):
                        sc.wait_ge(cPE.h, s1["cPE"] + i * 4 + grp + 1)
                        cAC.inc(sc.activation(
                            out=xnt[:, grp * 4:(grp + 1) * 4, i * 128:(i + 1) * 128],
                            in_=bB(grp % 2), func=AF.Copy))

            @block.vector
            def _(v):
                barrier(v, s1)
                for i in range(8):
                    par = i % 2
                    SX1.wait(v, SX1.tok(i))
                    cVE.inc(v.reduce_sum(stt[:, par, 0:1], xt[:, par, :], AX.X))
                    v.wait_ge(cAC.h, s1["cAC"] + i * NA + 1)
                    v.wait_ge(cVE.h, cVE.n)
                    cVE.inc(v.tensor_scalar(stt[:, par, 4:5], stt[:, par, 0:1],
                                            -1.0 / D, None, OP.mult))
                    v.wait_ge(cVE.h, cVE.n)
                    cVE.inc(v.tensor_tensor(stt[:, par, 7:8], stt[:, par, 4:5],
                                            stt[:, par, 4:5], OP.mult))
                    v.wait_ge(cVE.h, cVE.n)
                    cVE.inc(v.tensor_scalar(stt[:, par, 1:2], stt[:, par, 6:7],
                                            1.0 / D, LN_EPS, OP.mult, OP.add))
                    v.wait_ge(cVE.h, cVE.n)
                    cVE.inc(v.tensor_tensor(stt[:, par, 1:2], stt[:, par, 1:2],
                                            stt[:, par, 7:8], OP.subtract))
                    v.wait_ge(cAC.h, s1["cAC"] + i * NA + 2)
                    cVE.inc(v.reciprocal(stt[:, par, 3:4], stt[:, par, 2:3]))
                    v.wait_ge(cVE.h, cVE.n)
                    cVE.inc(v.tensor_tensor(stt[:, par, 5:6], stt[:, par, 4:5],
                                            stt[:, par, 3:4], OP.mult))
                    v.wait_ge(cVE.h, cVE.n)
                    cVE.inc(v.tensor_scalar(xn_f[:], xt[:, par, :],
                                            stt[:, par, 3:4], stt[:, par, 5:6],
                                            OP.mult, OP.add))
                    v.wait_ge(cVE.h, cVE.n)
                    if i >= 2:
                        v.wait_ge(cPE.h, s1["cPE"] + (i - 1) * 4)
                    cVE.inc(v.tensor_copy(xn_tm[:, par, :], xn_f[:]))

            @block.tensor
            def _(te):
                barrier(te, s1)
                for i in range(8):
                    te.wait_ge(cVE.h, s1["cVE"] + (i + 1) * NV1)
                    for grp in range(4):
                        gidx = i * 4 + grp
                        if gidx >= 2:
                            pi, pg = (gidx - 2) // 4, (gidx - 2) % 4
                            te.wait_ge(cAC.h, s1["cAC"] + pi * NA + 2 + pg + 1)
                        last = None
                        for j in range(4):
                            dt_i = grp * 4 + j
                            last = te.matmul(bB(grp % 2)[:, j * 128:(j + 1) * 128],
                                             xn_tm[:, i % 2, dt_i * 128:(dt_i + 1) * 128],
                                             ident_bf[:], start=True, stop=True)
                        cPE.inc(last)

            ac_p1 = s1["cAC"] + 8 * NA

            @block.sync
            def _(sy):
                sy.wait_ge(cAC.h, ac_p1)
                dL.inc(sy.dma_start(out=xnt_in.rearrange("(a p) t -> p a t", p=128),
                                    in_=xnt[:]), 16)

            st_p1 = dL.n

            @block.gpsimd
            def _(g):
                g.wait_ge(dL.h, st_p1)
                cCC.inc(g.collective_compute("AllGather", OP.bypass, replica_groups=RG,
                                             ins=[xnt_in.opt()], outs=[xnt_g.opt()]), 1)
        cc_xnt = cCC.n

        s0 = snap()
        if True:

            tiles = [("gu", i) for i in range(16)] + [("wp", i) for i in range(16)]
            NV = 8
            NVQ = 6
            SQW = PStream(nc, es, sems, "SQW")
            SST0 = PStream(nc, es, sems, "SST0")

            def tF(t):
                return 2048 if t[0] == "gu" else 1024

            pe_groups_after = []
            tot = 0
            for t in tiles:
                tot += tF(t) // 512
                pe_groups_after.append(tot)
            NBG = tot

            def pool_quant(g, i):
                t = tiles[i]
                par = i % 2
                F = tF(t)
                G = F // 64
                h3 = qh[:, par, :F].rearrange("p (g k) -> p g k", k=64)
                sb = qsb[:, par, :G].unsqueeze(2).broadcast_to([128, G, 64])
                g.wait_ge(cVE.h, s0["cVE"] + (i + 1) * NVQ)
                cGP.inc(g.tensor_scalar(qr[:, :F], qr[:, :F],
                                        1.0, -1.0, OP.min, OP.max))
                g.wait_ge(cGP.h, cGP.n)
                if i >= 2:
                    g.wait_ge(cPE.h, s0["cPE"] + pe_groups_after[i - 2])
                cGP.inc(g.tensor_tensor(
                    h3, qr[:, :F].rearrange("p (g k) -> p g k", k=64),
                    sb, OP.mult))

            @block.gpsimd
            def _(g):
                for i, t in enumerate(tiles):
                    if i >= 2:
                        g.wait_ge(cVE.h, s0["cVE"] + (i - 1) * NVQ)
                    src = (wgu if t[0] == "gu" else wp)[t[1] * 128:(t[1] + 1) * 128, :]
                    SQW.issue(g, qw[:, i % 2, :tF(t)], src, i)
                    if i >= 1:
                        pool_quant(g, i - 1)
                pool_quant(g, len(tiles) - 1)

            @block.vector
            def _(v):
                for i, t in enumerate(tiles):
                    par = i % 2
                    F = tF(t)
                    G = F // 64
                    SQW.wait(v, SQW.tok(i))
                    if i >= 2:
                        v.wait_ge(cGP.h, s0["cGP"] + (i - 1) * 2)
                        v.wait_ge(cPE.h, s0["cPE"] + pe_groups_after[i - 2])
                    w3 = qw[:, par, :F].rearrange("p (g k) -> p g k", k=64)
                    h3 = qh[:, par, :F].rearrange("p (g k) -> p g k", k=64)
                    cVE.inc(v.tensor_reduce(qg[:, par, :G], w3, AX.X, OP.add,
                                            apply_absolute_value=True))
                    v.wait_ge(cVE.h, cVE.n)
                    cVE.inc(v.tensor_scalar(qg[:, par, :G], qg[:, par, :G],
                                            1.0 / 64.0, 1e-8, OP.mult, OP.max))
                    v.wait_ge(cVE.h, cVE.n)
                    cVE.inc(v.tensor_copy(qsb[:, par, :G], qg[:, par, :G]))
                    v.wait_ge(cVE.h, cVE.n)
                    cVE.inc(v.reciprocal(qgr[:, par, :G], qsb[:, par, :G]))
                    v.wait_ge(cVE.h, cVE.n)
                    rb = qgr[:, par, :G].unsqueeze(2).broadcast_to([128, G, 64])
                    cVE.inc(v.tensor_tensor(h3, w3, rb, OP.mult))
                    v.wait_ge(cVE.h, cVE.n)
                    if i >= 1:
                        v.wait_ge(cGP.h, s0["cGP"] + i * 2)
                    cVE.inc(v.tensor_scalar(qr[:, :F], qh[:, par, :F],
                                            MAGIC, -MAGIC, OP.add, OP.add))



            @block.tensor
            def _(te):
                gidx = 0
                for i, t in enumerate(tiles):
                    te.wait_ge(cGP.h, s0["cGP"] + (i + 1) * 2)
                    for grp in range(tF(t) // 512):
                        if gidx >= 2:
                            te.wait_ge(cAC.h, s0["cAC"] + gidx - 1)
                        pb = bB(2 + gidx % 2)
                        last = None
                        for j in range(4):
                            dt_i = grp * 4 + j
                            last = te.matmul(pb[:, j * 128:(j + 1) * 128],
                                             qh[:, i % 2, dt_i * 128:(dt_i + 1) * 128],
                                             ident_bf[:], start=True, stop=True)
                        cPE.inc(last)
                        gidx += 1

            @block.scalar
            def _(sc):
                for gidx in range(NBG):
                    sc.wait_ge(cPE.h, s0["cPE"] + gidx + 1)
                    if gidx >= 2:
                        SST0.wait(sc, SST0.tok(gidx - 2))
                    cAC.inc(sc.activation(out=qstage[:, gidx % 2, :],
                                          in_=bB(2 + gidx % 2), func=AF.Copy))

            @block.sync
            def _(sy):
                gidx = 0
                for i, t in enumerate(tiles):
                    for grp in range(tF(t) // 512):
                        sy.wait_ge(cAC.h, s0["cAC"] + gidx + 1)
                        dst = (ggu_in if t[0] == "gu" else gp_in)[
                            grp * 512:(grp + 1) * 512, t[1] * 128:(t[1] + 1) * 128]
                        SST0.issue(
                            sy, dst.rearrange("(a p) c -> p a c", p=128),
                            qstage[:, gidx % 2, :].rearrange("p (a c) -> p a c", c=128),
                            gidx)
                        gidx += 1

            st_p0 = [s.n for s in SST0.sems]

            @block.gpsimd
            def _(g):
                for s, v in zip(SST0.sems, st_p0):
                    g.wait_ge(s.h, v)
                cCC.inc(g.collective_compute("AllGather", OP.bypass, replica_groups=RG,
                                             ins=[ggu_in.opt()], outs=[ggu_g.opt()]), 1)
                cCC.inc(g.collective_compute("AllGather", OP.bypass, replica_groups=RG,
                                             ins=[gp_in.opt()], outs=[gp_g.opt()]), 1)
        cc_ffn = cCC.n

        s2 = snap()
        exB = es.enter_context(contextlib.ExitStack())
        QT = exB.enter_context(nc.sbuf_tensor("QT", [128, 2, SEQ], BF16))
        KT = exB.enter_context(nc.sbuf_tensor("KT", [128, 2, SEQ], BF16))
        Vsb = exB.enter_context(nc.sbuf_tensor("Vsb", [128, 64, 256], BF16))
        with contextlib.ExitStack() as e2:
            xg = e2.enter_context(nc.sbuf_tensor("xg", [128, 3, DT, 512], BF16))
            wqkv_bf = e2.enter_context(nc.sbuf_tensor("wqkv_bf", [128, DT, 768], BF16))
            bqkv_bf = e2.enter_context(nc.sbuf_tensor("bqkv_bf", [1, 768], BF16))
            SXG = PStream(nc, es, sems, "SXG")
            SXH = PStream(nc, es, sems, "SXH")

            @block.gpsimd
            def _(g):
                dG.inc(g.dma_start(out=bqkv_bf[:], in_=bqkv[:, :]), 16)
                dG.inc(g.dma_start(out=wqkv_bf[:],
                                   in_=wqkv.rearrange("(a p) c -> p a c", p=128)), 16)

            dg_wqkv = dG.n

            @block.sync
            def _(sy):
                sy.wait_ge(cCC.h, cc_xnt)
                for tp in range(16):
                    if tp >= 3:
                        sy.wait_ge(cPE.h, s2["cPE"] + (tp - 2) * 8)
                    r = tp // 2
                    coff = (tp % 2) * 512
                    src = xnt_g[r * 2048:(r + 1) * 2048, coff:coff + 256]
                    SXG.issue(sy, xg[:, tp % 3, :, 0:256],
                              src.rearrange("(a p) t -> p a t", p=128), tp)

            @block.gpsimd
            def _(g):
                g.wait_ge(cCC.h, cc_xnt)
                for tp in range(16):
                    if tp >= 3:
                        g.wait_ge(cPE.h, s2["cPE"] + (tp - 2) * 8)
                    r = tp // 2
                    coff = (tp % 2) * 512 + 256
                    src = xnt_g[r * 2048:(r + 1) * 2048, coff:coff + 256]
                    SXH.issue(g, xg[:, tp % 3, :, 256:512],
                              src.rearrange("(a p) t -> p a t", p=128), tp)

            @block.tensor
            def _(te):
                te.wait_ge(dG.h, dg_wqkv)
                for tp in range(16):
                    par = tp % 3
                    SXG.wait(te, SXG.tok(tp))
                    SXH.wait(te, SXH.tok(tp))
                    for mt in range(4):
                        if tp > 0 or mt >= 2:
                            pj, pmt = (tp, mt - 2) if mt >= 2 else (tp - 1, mt + 2)
                            te.wait_ge(cAC.h, s2["cAC"] + pj * 8 + pmt + 1)
                        pb = bA(mt % 2)
                        for dt_i in range(DT):
                            te.matmul(pb, wqkv_bf[:, dt_i, mt * 128:(mt + 1) * 128],
                                      xg[:, par, dt_i, :], start=(dt_i == 0), stop=False)
                        cPE.inc(te.matmul(pb, bqkv_bf[0:1, mt * 128:(mt + 1) * 128],
                                          ones_row[0:1, :], start=False, stop=True))
                    for vt in range(4):
                        if tp > 0 or vt >= 2:
                            pj, pvt = (tp, vt - 2) if vt >= 2 else (tp - 1, vt + 2)
                            te.wait_ge(cAC.h, s2["cAC"] + pj * 8 + 4 + pvt + 1)
                        pb = bA(2 + vt % 2)[:, 0:256]
                        for dt_i in range(DT):
                            te.matmul(pb, xg[:, par, dt_i, vt * 128:(vt + 1) * 128],
                                      wqkv_bf[:, dt_i, 512:768],
                                      start=(dt_i == 0), stop=False)
                        cPE.inc(te.matmul(pb, ones_row[0:1, 0:128],
                                          bqkv_bf[0:1, 512:768], start=False, stop=True))

            @block.scalar
            def _(sc):
                for tp in range(16):
                    for mt in range(4):
                        sc.wait_ge(cPE.h, s2["cPE"] + tp * 8 + mt + 1)
                        dstt = QT if mt < 2 else KT
                        scl = QK_SCALE if mt < 2 else 1.0
                        cAC.inc(sc.activation(
                            out=dstt[:, mt % 2, tp * 512:(tp + 1) * 512],
                            in_=bA(mt % 2), func=AF.Copy, scale=scl))
                    for vt in range(4):
                        sc.wait_ge(cPE.h, s2["cPE"] + tp * 8 + 4 + vt + 1)
                        cAC.inc(sc.activation(out=Vsb[:, tp * 4 + vt, :],
                                              in_=bA(2 + vt % 2)[:, 0:256], func=AF.Copy))
        s4 = snap()
        with contextlib.ExitStack() as e4:
            masksT = e4.enter_context(nc.sbuf_tensor("masksT", [128, 4, 512], F32))
            PT_sb = e4.enter_context(nc.sbuf_tensor("PT_sb", [128, 4, 1024], BF16))
            OT_raw = e4.enter_context(nc.sbuf_tensor("OT_raw", [128, 2, 512], BF16))
            a2a_st = e4.enter_context(nc.sbuf_tensor("a2a_st", [128, 2, 4, 128], BF16))
            m0 = e4.enter_context(nc.sbuf_tensor("m0", [128, 2, 4], F32))
            m0s_sb = e4.enter_context(nc.sbuf_tensor("m0s_sb", [1, 2], F32))
            nm0s = e4.enter_context(nc.sbuf_tensor("nm0s", [128, 32], F32))
            lrow_sb = e4.enter_context(nc.sbuf_tensor("lrow_sb", [1, 512], F32))
            onef = e4.enter_context(nc.sbuf_tensor("onef", [1, 1], F32))
            linv = e4.enter_context(nc.sbuf_tensor("linv", [128, 4], F32))
            SAS = PStream(nc, es, sems, "SAS")

            NHC = 2 * p4_nch
            hcs = [(head, chunk) for head in range(2) for chunk in range(p4_nch)]

            peL, acL, veL = {}, {}, {}
            pe_n = [s4["cPE"]]
            ac_n = [s4["cAC"]]
            ve_n = [s4["cVE"]]

            def _mk(counter, lab_dict):
                def _add(lab, k=1):
                    counter[0] += k
                    lab_dict[lab] = counter[0]
                return _add

            PEp, ACp, VEp = _mk(pe_n, peL), _mk(ac_n, acL), _mk(ve_n, veL)
            for hc in range(NHC + 2):
                if 1 <= hc <= NHC:
                    PEp(("pT1", hc - 1))
                if 2 <= hc <= NHC + 1:
                    PEp(("pTb", hc - 2))
                if hc < NHC:
                    PEp(("pSq", hc))
                if hc < NHC:
                    VEp(("pr1", hc))
                if 1 <= hc <= NHC:
                    VEp(("pr2", hc - 1))
                if 2 <= hc <= NHC + 1:
                    VEp(("pcp", hc - 2))
            for hc, (head, chunk) in enumerate(hcs):
                nu = 2 * (chunk + 1)
                s_order = [1, 0] + list(range(2, nu))
                for i, u in enumerate(s_order):
                    PEp(("ST", hc, u))
                    if i >= 2:
                        PEp(("PVh", hc, s_order[i - 2]))
                for t in (s_order[nu - 2], s_order[nu - 1]):
                    PEp(("PVh", hc, t))
                PEp(("otm", hc))
                PEp(("lrowT", hc))
                for u in range(nu):
                    ACp(("exph", hc, u))
                ACp(("otev", hc))
                for u in range(nu):
                    if u // 2 == chunk:
                        VEp(("maskT", hc, u), 2)
                VEp(("lrowev", hc))
                VEp(("recip", hc))
                VEp(("scale", hc), 4)

            def lastPV(hc):
                chunk = hcs[hc][1]
                return ("PVh", hc, 0) if chunk == 0 else ("PVh", hc, 2 * chunk + 1)

            @block.gpsimd
            def _(g):
                barrier(g, s4)
                for kc in range(4):
                    cGP.inc(g.memset(masksT[:, kc, :], 0.0))
                    g.wait_ge(cGP.h, cGP.n)
                    cGP.inc(g.affine_select(out=masksT[:, kc, :], in_=masksT[:, kc, :],
                                            compare_op=OP.is_ge, fill=NEG_BIG,
                                            base=-kc * 128, pattern=[[1, 512]],
                                            channel_multiplier=-1))
                cGP.inc(g.memset(onef[:], 1.0))

            gp_masks = cGP.n

            @block.tensor
            def _(te):
                barrier(te, s4)
                for it in range(NHC + 2):
                    if 1 <= it <= NHC:
                        k = it - 1
                        tgt = pA if k % 2 == 0 else pB
                        te.wait_ge(cVE.h, veL[("pr1", k)])
                        cPE.inc(te.matmul(tgt[0:1, 0:128], m0[:, k % 2, 0:1],
                                          ident_f[:], start=True, stop=True))
                    if 2 <= it <= NHC + 1:
                        k = it - 2
                        tgt = pB if k % 2 == 0 else pA
                        te.wait_ge(cVE.h, veL[("pr2", k)])
                        if k + 1 < NHC:
                            te.wait_ge(cVE.h, veL[("pr1", k + 1)])
                        cPE.inc(te.matmul(tgt[:, 192:193], ones_rowf[0:1, :],
                                          m0s_sb[0:1, k % 2:k % 2 + 1],
                                          start=True, stop=True))
                    if it < NHC:
                        k = it
                        head, chunk = hcs[k]
                        qb = chunk * 512
                        tgt = pA if k % 2 == 0 else pB
                        if k >= 3:
                            te.wait_ge(cVE.h, veL[("pcp", k - 3)])
                        last = None
                        for qt in range(4):
                            last = te.matmul(
                                tgt[:, qt * 512:(qt + 1) * 512],
                                QT[:, head, qb + qt * 128:qb + (qt + 1) * 128],
                                KT[:, head, 0:512], start=True, stop=True)
                        cPE.inc(last)
                te.wait_ge(cGP.h, gp_masks)
                for hc, (head, chunk) in enumerate(hcs):
                    qb = chunk * 512
                    nu = 2 * (chunk + 1)
                    s_order = [1, 0] + list(range(2, nu))
                    ob = bB(2 + hc % 2)

                    def emit_ST(u):
                        kb = u // 2
                        h = u % 2
                        p = u % 2
                        if u == 0:
                            if hc > 0:
                                te.wait_ge(cVE.h, veL[("scale", hc - 1)])
                        elif u == 1:
                            if hc > 0:
                                te.wait_ge(cAC.h, acL[("otev", hc - 1)])
                            else:
                                te.wait_ge(cVE.h, veL[("pcp", NHC - 1)])
                        else:
                            te.wait_ge(cAC.h, acL[("exph", hc, u - 2)])
                        last = None
                        for j in range(2):
                            kc = 2 * h + j
                            koff = kb * 512 + kc * 128
                            last = te.matmul(
                                pA[:, p * 1024 + j * 512:p * 1024 + (j + 1) * 512],
                                KT[:, head, koff:koff + 128],
                                QT[:, head, qb:qb + 512],
                                start=True, stop=True)
                        cPE.inc(last)

                    def emit_PVh(u, first, last_f):
                        kb = u // 2
                        h = u % 2
                        p = u % 4
                        te.wait_ge(cAC.h, acL[("exph", hc, u)])
                        if first and hc >= 2:
                            te.wait_ge(cAC.h, acL[("otev", hc - 2)])
                        last = None
                        for j in range(2):
                            kc = 2 * h + j
                            te.matmul(ob, Vsb[:, kb * 4 + kc,
                                              head * 128:(head + 1) * 128],
                                      PT_sb[:, p, j * 512:(j + 1) * 512],
                                      start=(first and j == 0),
                                      stop=(last_f and j == 1),
                                      skip_group_check=True)
                        for j in range(2):
                            last = te.matmul(
                                bB(0)[0:1, 0:512], ones_col[:],
                                PT_sb[:, p, j * 512:(j + 1) * 512],
                                start=(first and j == 0),
                                stop=(last_f and j == 1),
                                skip_group_check=True)
                        cPE.inc(last)

                    npv = 0
                    for i, u in enumerate(s_order):
                        emit_ST(u)
                        if i >= 2:
                            emit_PVh(s_order[i - 2], npv == 0, npv == nu - 1)
                            npv += 1
                    for t in (s_order[nu - 2], s_order[nu - 1]):
                        emit_PVh(t, npv == 0, npv == nu - 1)
                        npv += 1
                    te.wait_ge(cAC.h, acL[("otev", hc)])
                    last = None
                    for qt in range(4):
                        last = te.matmul(pA[:, qt * 128:(qt + 1) * 128],
                                         OT_raw[:, hc % 2, qt * 128:(qt + 1) * 128],
                                         ident_bf[:], start=True, stop=True)
                    cPE.inc(last)
                    te.wait_ge(cVE.h, veL[("lrowev", hc)])
                    last = None
                    for qt in range(4):
                        last = te.matmul(bB(1)[:, qt:qt + 1],
                                         lrow_sb[0:1, qt * 128:(qt + 1) * 128],
                                         onef[0:1, 0:1], start=True, stop=True)
                    cPE.inc(last)

            @block.scalar
            def _(sc):
                barrier(sc, s4)
                sc.wait_ge(cVE.h, veL[("pcp", NHC - 1)])
                for hc, (head, chunk) in enumerate(hcs):
                    nu = 2 * (chunk + 1)
                    for u in range(nu):
                        p = u % 4
                        sc.wait_ge(cPE.h, peL[("ST", hc, u)])
                        if u // 2 == chunk:
                            sc.wait_ge(cVE.h, veL[("maskT", hc, u)])
                        if u < 4:
                            if hc > 0:
                                sc.wait_ge(cPE.h, peL[lastPV(hc - 1)])
                        else:
                            sc.wait_ge(cPE.h, peL[("PVh", hc, u - 4)])
                        cAC.inc(sc.activation(out=PT_sb[:, p, 0:1024],
                                              in_=pA[:, (u % 2) * 1024:
                                                      (u % 2 + 1) * 1024],
                                              func=AF.Exp, bias=nm0s[:, hc:hc + 1],
                                              scale=1.0))
                    sc.wait_ge(cPE.h, peL[lastPV(hc)])
                    cAC.inc(sc.activation(out=OT_raw[:, hc % 2, :],
                                          in_=bB(2 + hc % 2), func=AF.Copy))

            @block.vector
            def _(v):
                barrier(v, s4)
                for it in range(NHC + 2):
                    if it < NHC:
                        k = it
                        tgt = pA if k % 2 == 0 else pB
                        v.wait_ge(cPE.h, peL[("pSq", k)])
                        cVE.inc(v.reduce_max(m0[:, k % 2, 0:1], tgt[:, :],
                                             AX.X, negate=True))
                    if 1 <= it <= NHC:
                        k = it - 1
                        tgt = pA if k % 2 == 0 else pB
                        v.wait_ge(cPE.h, peL[("pT1", k)])
                        v.wait_ge(cVE.h, cVE.n)
                        cVE.inc(v.tensor_reduce(m0s_sb[0:1, k % 2:k % 2 + 1],
                                                tgt[0:1, 0:128], AX.X, OP.min))
                    if 2 <= it <= NHC + 1:
                        k = it - 2
                        tgt = pB if k % 2 == 0 else pA
                        v.wait_ge(cPE.h, peL[("pTb", k)])
                        v.wait_ge(cVE.h, cVE.n)
                        cVE.inc(v.tensor_copy(nm0s[:, k:k + 1], tgt[:, 192:193]))
                v.wait_ge(cGP.h, gp_masks)
                for hc, (head, chunk) in enumerate(hcs):
                    nu = 2 * (chunk + 1)
                    for u in range(nu):
                        if u // 2 == chunk:
                            p = u % 2
                            v.wait_ge(cPE.h, peL[("ST", hc, u)])
                            for j in range(2):
                                kc = 2 * (u % 2) + j
                                cVE.inc(v.tensor_tensor(
                                    pA[:, p * 1024 + j * 512:p * 1024 + (j + 1) * 512],
                                    pA[:, p * 1024 + j * 512:p * 1024 + (j + 1) * 512],
                                    masksT[:, kc, :], OP.add))
                    v.wait_ge(cPE.h, peL[lastPV(hc)])
                    v.wait_ge(cVE.h, cVE.n)
                    cVE.inc(v.tensor_copy(lrow_sb[:], bB(0)[0:1, 0:512]))
                    v.wait_ge(cPE.h, peL[("otm", hc)])
                    v.wait_ge(cPE.h, peL[("lrowT", hc)])
                    if hc >= 2:
                        SAS.wait(v, SAS.tok(hc - 2))
                    v.wait_ge(cVE.h, cVE.n)
                    cVE.inc(v.reciprocal(linv[:], bB(1)[:, 0:4]))
                    v.wait_ge(cVE.h, cVE.n)
                    for qt in range(4):
                        cVE.inc(v.tensor_scalar(a2a_st[:, hc % 2, qt, :],
                                                pA[:, qt * 128:(qt + 1) * 128],
                                                linv[:, qt:qt + 1], None, OP.mult))

            @block.sync
            def _(sy):
                barrier(sy, s4)
                for hc, (head, chunk) in enumerate(hcs):
                    sy.wait_ge(cVE.h, veL[("scale", hc)])
                    dst = a2a_in[chunk * 512:(chunk + 1) * 512,
                                 head * 128:(head + 1) * 128]
                    SAS.issue(sy, dst.rearrange("(a p) c -> p a c", p=128),
                              a2a_st[:, hc % 2, :, :], hc)

            st_a2a = [s.n for s in SAS.sems]

            @block.gpsimd
            def _(g):
                for s, v in zip(SAS.sems, st_a2a):
                    g.wait_ge(s.h, v)
                cCC.inc(g.collective_compute("AllGather", OP.bypass, replica_groups=RG,
                                             ins=[a2a_in.opt()], outs=[ag_o.opt()]), 1)
                dG.inc(g.dma_start(out=oofs_sb[:], in_=oofs[:, :]), 16)
                g.wait_ge(cCC.h, cCC.n)
                g.wait_ge(dG.h, dG.n)
                with g.register("vreg") as vreg:
                    for i in range(NCORES):
                        g.reg_load(vreg, oofs_sb[0:1, i:i + 1])
                        v = g.snap(vreg, min_val=0, max_val=NCORES * SEQ - TOWN)
                        dG.inc(g.dma_start(out=a2a_out[i * TOWN:(i + 1) * TOWN, :],
                                           in_=ag_o[bass.ds(v, TOWN), :]), 16)

        exB.close()
        cc_a2a = cCC.n

        s3 = snap()
        with contextlib.ExitStack() as e3:
            xnt = e3.enter_context(nc.sbuf_tensor("xnt3", [128, DT, 1024], BF16))
            SXN = PStream(nc, es, sems, "SXN")
            ggs = e3.enter_context(nc.sbuf_tensor("ggs", [128, 2, 2, DT, 256], BF16))
            hst = e3.enter_context(nc.sbuf_tensor("hst", [128, 2, 1024], BF16))
            gssb = e3.enter_context(nc.sbuf_tensor("gssb", [128, 3, 512], F32))
            hsq = e3.enter_context(nc.sbuf_tensor("hsq", [128, 2, 1024], BF16))
            rrow = e3.enter_context(nc.sbuf_tensor("rrow", [1, TOWN], F32))
            rrow2 = e3.enter_context(nc.sbuf_tensor("rrow2", [1, TOWN], F32))
            SGG = PStream(nc, es, sems, "SGG")
            SHS = PStream(nc, es, sems, "SHS")

            def pe_hp(hp):
                return s3["cPE"] + (hp + 1) * 5

            @block.sync
            def _(sy):
                barrier(sy, s3, skip=("cCC", "dG"))
                sy.wait_ge(cCC.h, cc_ffn)
                SXN.issue(sy, xnt[:],
                          xnt_in.rearrange("(a p) t -> p a t", p=128), 0)

                def st_h(hp):
                    sy.wait_ge(cVE.h, s3["cVE"] + (hp + 1) * 2)
                    SHS.issue(sy, h_dram[hp * 128:(hp + 1) * 128, :],
                              hst[:, hp % 2, :], hp)

                for hq in range(32):
                    if hq >= 2:
                        sy.wait_ge(cPE.h, pe_hp(2 * hq - 3))
                    r = hq // 8
                    hcol = (hq * 256) % 2048
                    for gi, blk in enumerate([r, 4 + r]):
                        src = ggu_g[blk * 2048:(blk + 1) * 2048, hcol:hcol + 256]
                        SGG.issue(sy, ggs[:, hq % 2, gi, :, :],
                                  src.rearrange("(a p) c -> p a c", p=128), hq)
                    if hq >= 1:
                        st_h(2 * hq - 2)
                        st_h(2 * hq - 1)
                for hp in (62, 63):
                    st_h(hp)

            @block.tensor
            def _(te):
                barrier(te, s3, skip=("cCC", "dG"))
                SXN.wait(te, SXN.tok(0))
                for hp in range(64):
                    hq, hloc = hp // 2, hp % 2
                    SGG.wait(te, SGG.tok(hq, per_grp=2))
                    for th in range(2):
                        u = hp * 2 + th
                        if u >= 3:
                            v_ = u - 3
                            te.wait_ge(cAC.h, s3["cAC"] + (v_ // 2) * 3 + v_ % 2 + 1)
                            te.wait_ge(cVE.h, s3["cVE"] + u - 2)
                        gb = [bA(0), bA(1), bB(2)][u % 3]
                        ub = [bA(2), bA(3), bB(3)][u % 3]
                        for gi, pb in [(0, gb), (1, ub)]:
                            last = None
                            for dt_i in range(DT):
                                last = te.matmul(
                                    pb, ggs[:, hq % 2, gi, dt_i,
                                            hloc * 128:(hloc + 1) * 128],
                                    xnt[:, dt_i, th * 512:(th + 1) * 512],
                                    start=(dt_i == 0), stop=(dt_i == DT - 1))
                            cPE.inc(last)
                    te.wait_ge(cAC.h, s3["cAC"] + hp * 3 + 3)
                    last = None
                    for th in range(2):
                        last = te.matmul(bB(th)[0:1, :], ones_col[:],
                                         hsq[:, hp % 2, th * 512:(th + 1) * 512],
                                         start=(hp == 0), stop=(hp == 63),
                                         skip_group_check=True)
                    cPE.inc(last)

            @block.scalar
            def _(sc):
                barrier(sc, s3, skip=("cCC", "dG"))
                for hp in range(64):
                    for th in range(2):
                        u = hp * 2 + th
                        sc.wait_ge(cPE.h, s3["cPE"] + hp * 5 + th * 2 + 1)
                        if u >= 3:
                            sc.wait_ge(cVE.h, s3["cVE"] + u - 2)
                        cAC.inc(sc.activation(out=gssb[:, u % 3, :],
                                              in_=[bA(0), bA(1), bB(2)][u % 3],
                                              func=AF.Silu))
                    sc.wait_ge(cVE.h, s3["cVE"] + (hp + 1) * 2)
                    cAC.inc(sc.activation(out=hsq[:, hp % 2, :], in_=hst[:, hp % 2, :],
                                          func=AF.Square))

            @block.vector
            def _(v):
                barrier(v, s3, skip=("cCC", "dG"))
                for hp in range(64):
                    for th in range(2):
                        u = hp * 2 + th
                        v.wait_ge(cAC.h, s3["cAC"] + hp * 3 + th + 1)
                        v.wait_ge(cPE.h, s3["cPE"] + hp * 5 + th * 2 + 2)
                        if hp >= 2 and th == 0:
                            SHS.wait(v, SHS.tok(hp - 2))
                        cVE.inc(v.tensor_tensor(hst[:, hp % 2, th * 512:(th + 1) * 512],
                                                gssb[:, u % 3, :],
                                                [bA(2), bA(3), bB(3)][u % 3],
                                                OP.mult))
                v.wait_ge(cPE.h, pe_hp(63))
                for th in range(2):
                    cVE.inc(v.tensor_scalar(rrow[0:1, th * 512:(th + 1) * 512],
                                            bB(th)[0:1, :], 1.0 / HFFN, RMS_EPS,
                                            OP.mult, OP.add))

            @block.scalar
            def _(sc):
                sc.wait_ge(cVE.h, s3["cVE"] + 130)
                cAC.inc(sc.activation(out=rrow2[:], in_=rrow[:], func=AF.Sqrt))

            @block.vector
            def _(v):
                v.wait_ge(cAC.h, s3["cAC"] + 193)
                cVE.inc(v.reciprocal(rrow[:], rrow2[:]))
                v.wait_ge(cVE.h, cVE.n)

            @block.tensor
            def _(te):
                te.wait_ge(cVE.h, s3["cVE"] + 131)
                for th in range(2):
                    last = te.matmul(bB(2 + th), ones_rowf[0:1, :],
                                     rrow[0:1, th * 512:(th + 1) * 512],
                                     start=True, stop=True)
                cPE.inc(last)

            @block.vector
            def _(v):
                v.wait_ge(cPE.h, pe_hp(63) + 1)
                for th in range(2):
                    cVE.inc(v.tensor_copy(rstd_bc[:, th * 512:(th + 1) * 512],
                                          bB(2 + th)))
        s5 = snap()
        with contextlib.ExitStack() as e5:
            ot_in = e5.enter_context(nc.sbuf_tensor("ot_in", [128, 2, 2048], BF16))
            OTs = e5.enter_context(nc.sbuf_tensor("OTs", [128, DT, 1024], BF16))
            wo_b = e5.enter_context(nc.sbuf_tensor("wo_b", [128, 2, DT, 512], BF16))
            gp_b = e5.enter_context(nc.sbuf_tensor("gp_b", [128, 4, 4, 512], BF16))
            h_b = e5.enter_context(nc.sbuf_tensor("h_b", [128, 4, 4, 1024], BF16))
            hn_b = e5.enter_context(nc.sbuf_tensor("hn_b", [128, 2, 4, 1024], BF16))
            fst = e5.enter_context(nc.sbuf_tensor("fst", [128, 8, 512], F32))
            x_b = e5.enter_context(nc.sbuf_tensor("x_b", [128, 2, 512], F32))
            o_st = e5.enter_context(nc.sbuf_tensor("o_st", [128, 2, 512], F32))
            bout_bf = e5.enter_context(nc.sbuf_tensor("bout_bf", [1, D], BF16))
            SOT = PStream(nc, es, sems, "SOT")
            SGP = PStream(nc, es, sems, "SGP")
            SXB = PStream(nc, es, sems, "SXB")
            SOS = PStream(nc, es, sems, "SOS")
            SWO = PStream(nc, es, sems, "SWO")
            SGQ = PStream(nc, es, sems, "SGQ")

            @block.sync
            def _(sy):
                barrier(sy, s5)
                sy.wait_ge(cCC.h, cc_a2a)
                for tt in range(8):
                    if tt >= 2:
                        sy.wait_ge(cPE.h, s5["cPE"] + (tt - 1) * 4)
                    src = a2a_out.rearrange("(i q) c -> i q c", q=1024)[
                        :, tt * 128:(tt + 1) * 128, :].transpose([1, 0, 2])
                    SOT.issue(sy, ot_in[:, tt % 2, :].rearrange("p (i c) -> p i c",
                                                                c=256), src, tt)

            @block.tensor
            def _(te):
                barrier(te, s5)
                for tt in range(8):
                    SOT.wait(te, SOT.tok(tt))
                    for grp in range(4):
                        gidx = tt * 4 + grp
                        if gidx >= 2:
                            te.wait_ge(cAC.h, s5["cAC"] + gidx - 1)
                        last = None
                        for j in range(4):
                            dt_i = grp * 4 + j
                            last = te.matmul(bB(grp % 2)[:, j * 128:(j + 1) * 128],
                                             ot_in[:, tt % 2,
                                                   dt_i * 128:(dt_i + 1) * 128],
                                             ident_bf[:], start=True, stop=True)
                        cPE.inc(last)

            @block.scalar
            def _(sc):
                barrier(sc, s5)
                for gidx in range(32):
                    tt, grp = gidx // 4, gidx % 4
                    sc.wait_ge(cPE.h, s5["cPE"] + gidx + 1)
                    cAC.inc(sc.activation(
                        out=OTs[:, grp * 4:(grp + 1) * 4, tt * 128:(tt + 1) * 128],
                        in_=bB(grp % 2), func=AF.Copy))

            pe_ot = s5["cPE"] + 32
            ac_ot = s5["cAC"] + 32
            s5b = snap()

            pe_z = [[0] * 8 for _ in range(4)]
            pe_y = [[0] * 64 for _ in range(4)]
            pe_T = [0] * 4
            ve_fev = [0] * 4
            ve_xadd = [[0] * 8 for _ in range(4)]
            ve_hn = [[0] * 16 for _ in range(4)]
            pe = pe_ot
            ve = s5b["cVE"]
            for dg in range(4):
                for hq4 in range(16):
                    ve += 1; ve_hn[dg][hq4] = ve
                for b in range(8):
                    pe += 1; pe_z[dg][b] = pe
                for hc in range(64):
                    pe += 1; pe_y[dg][hc] = pe
                for b in range(8):
                    ve += 1
                ve_fev[dg] = ve
                pe += 1; pe_T[dg] = pe
                for tt in range(8):
                    ve += 1; ve_xadd[dg][tt] = ve

            @block.gpsimd
            def _(g):
                barrier(g, s5b)
                dG.inc(g.dma_start(out=bout_bf[:], in_=bout[:, :]), 16)
                for dg in range(4):
                    if dg >= 2:
                        g.wait_ge(cPE.h, pe_z[dg - 2][7])
                    src = wout[:, dg * 512:(dg + 1) * 512]
                    SWO.issue(g, wo_b[:, dg % 2, :, :],
                              src.rearrange("(a p) c -> p a c", p=128), dg)

            @block.sync
            def _(sy):
                for dg in range(4):
                    for hq4 in range(16):
                        if dg > 0 or hq4 >= 4:
                            pg, ph = (dg, hq4 - 4) if hq4 >= 4 else (dg - 1, 12 + hq4)
                            sy.wait_ge(cPE.h, pe_y[pg][ph * 4 + 3])
                            sy.wait_ge(cVE.h, ve_hn[pg][ph])
                        gidx = dg * 16 + hq4
                        SGP.issue(sy, h_b[:, hq4 % 4, :, :],
                                  h_dram[hq4 * 512:(hq4 + 1) * 512, :].rearrange(
                                      "(a p) c -> p a c", p=128), gidx)
                    def ld_x(tt):
                        if dg > 0 or tt >= 2:
                            pg, pt = (dg, tt - 2) if tt >= 2 else (dg - 1, 6 + tt)
                            sy.wait_ge(cVE.h, ve_xadd[pg][pt])
                        SXB.issue(sy, x_b[:, tt % 2, :],
                                  x_own[tt * 128:(tt + 1) * 128,
                                        dg * 512:(dg + 1) * 512], dg * 8 + tt)

                    def st_o(tt):
                        sy.wait_ge(cVE.h, ve_xadd[dg][tt])
                        SOS.issue(sy,
                                  out[tt * 128:(tt + 1) * 128,
                                      dg * 512:(dg + 1) * 512],
                                  o_st[:, tt % 2, :], dg * 8 + tt)

                    ld_x(0)
                    ld_x(1)
                    for tt in range(2, 8):
                        st_o(tt - 2)
                        ld_x(tt)
                    st_o(6)
                    st_o(7)
                for s in SOS.sems:
                    sy.wait_ge(s.h, s.n)
                sy.wait_ge(dL.h, dL.n)

            @block.scalar
            def _(sc):
                for dg in range(4):
                    for hq4 in range(16):
                        if dg > 0 or hq4 >= 4:
                            pg, ph = (dg, hq4 - 4) if hq4 >= 4 else (dg - 1, 12 + hq4)
                            sc.wait_ge(cPE.h, pe_y[pg][ph * 4 + 3])
                            sc.wait_ge(cVE.h, ve_hn[pg][ph])
                        gidx = dg * 16 + hq4
                        SGQ.issue(sc, gp_b[:, hq4 % 4, :, :],
                                  gp_g[hq4 * 512:(hq4 + 1) * 512,
                                       dg * 512:(dg + 1) * 512].rearrange(
                                           "(a p) c -> p a c", p=128), gidx)

            @block.tensor
            def _(te):
                te.wait_ge(cAC.h, ac_ot)
                te.wait_ge(dG.h, s5b["dG"] + 16)
                for dg in range(4):
                    SWO.wait(te, SWO.tok(dg))
                    for b in range(8):
                        mt, th = b // 2, b % 2
                        pb = bA(b % 4) if b < 4 else bB(b % 4)
                        if dg > 0:
                            te.wait_ge(cVE.h, ve_xadd[dg - 1][b])
                        for dt_i in range(DT):
                            te.matmul(pb, wo_b[:, dg % 2, dt_i, mt * 128:(mt + 1) * 128],
                                      OTs[:, dt_i, th * 512:(th + 1) * 512],
                                      start=(dt_i == 0), stop=False,
                                      skip_group_check=True)
                        cPE.inc(te.matmul(
                            pb,
                            bout_bf[0:1, dg * 512 + mt * 128:dg * 512 + (mt + 1) * 128],
                            ones_row[0:1, :], start=False, stop=False,
                            skip_group_check=True))
                    for hc in range(64):
                        hq4, hs = hc // 4, hc % 4
                        if hs == 0:
                            te.wait_ge(cVE.h, ve_hn[dg][hq4])
                            SGQ.wait(te, SGQ.tok(dg * 16 + hq4))
                        last = None
                        for b in range(8):
                            mt, th = b // 2, b % 2
                            pb = bA(b % 4) if b < 4 else bB(b % 4)
                            last = te.matmul(
                                pb, gp_b[:, hq4 % 4, hs, mt * 128:(mt + 1) * 128],
                                hn_b[:, hq4 % 2, hs, th * 512:(th + 1) * 512],
                                start=False, stop=(hc == 63),
                                skip_group_check=True)
                        cPE.inc(last)
                    te.wait_ge(cVE.h, ve_fev[dg])
                    last = None
                    for b in range(8):
                        mt, th = b // 2, b % 2
                        for tsub in range(4):
                            j = th * 4 + tsub
                            tb = bA(j % 4) if j < 4 else bB(j % 4)
                            last = te.matmul(tb[:, mt * 128:(mt + 1) * 128],
                                             fst[:, b, tsub * 128:(tsub + 1) * 128],
                                             ident_f[:], start=True, stop=True,
                                             skip_group_check=True)
                    cPE.inc(last)

            @block.vector
            def _(v):
                for dg in range(4):
                    for hq4 in range(16):
                        SGP.wait(v, SGP.tok(dg * 16 + hq4))
                        if dg > 0 or hq4 >= 2:
                            pg, ph = (dg, hq4 - 2) if hq4 >= 2 else (dg - 1, 14 + hq4)
                            v.wait_ge(cPE.h, pe_y[pg][ph * 4 + 3])
                        cVE.inc(v.tensor_tensor(
                            hn_b[:, hq4 % 2, :, :], h_b[:, hq4 % 4, :, :],
                            rstd_bc[:].unsqueeze(1).broadcast_to([128, 4, 1024]),
                            OP.mult))
                    v.wait_ge(cPE.h, pe_y[dg][63])
                    for b in range(8):
                        pb = bA(b % 4) if b < 4 else bB(b % 4)
                        cVE.inc(v.tensor_copy(fst[:, b, :], pb))
                    v.wait_ge(cPE.h, pe_T[dg])
                    for tt in range(8):
                        tb = bA(tt % 4) if tt < 4 else bB(tt % 4)
                        SXB.wait(v, SXB.tok(dg * 8 + tt))
                        if dg > 0 or tt >= 2:
                            pg, pt = (dg, tt - 2) if tt >= 2 else (dg - 1, 6 + tt)
                            SOS.wait(v, SOS.tok(pg * 8 + pt))
                        cVE.inc(v.tensor_tensor(o_st[:, tt % 2, :], x_b[:, tt % 2, :],
                                                tb, OP.add))

    return nc


def _finish_dbg(nc, block, sems, snp, dbg_aps, dbg, srcs=None, sb_srcs=None):
    dL = sems["dL"]

    @block.sync
    def _(sy):
        for nm, v in snp.items():
            if v > 0:
                sy.wait_ge(sems[nm].h, v)
        for name in dbg:
            if srcs and name in srcs:
                dL.inc(sy.dma_start(out=dbg_aps[name][:, :], in_=srcs[name][:, :]), 16)
            elif sb_srcs and name in sb_srcs:
                ap = sb_srcs[name]
                d = dbg_aps[name]
                if len(ap.shape) == 3:
                    d3 = d.rearrange("(a p) c -> p a c", p=128)
                    dL.inc(sy.dma_start(out=d3, in_=ap), 16)
                else:
                    dL.inc(sy.dma_start(out=d[:, :], in_=ap), 16)
        sy.wait_ge(dL.h, dL.n)


_NC_CACHE = {}


def _get_nc(phases=99, dbg=()):
    key = (phases, tuple(dbg))
    if key not in _NC_CACHE:
        _NC_CACHE[key] = build_nc(phases, dbg)
    return _NC_CACHE[key]


def make_in_maps(L0_attn_x, L0_attn_W_qkv, L0_attn_b_qkv, L0_attn_W_out,
                 L0_attn_b_out, L0_ffn_w_gate_up, L0_ffn_w_proj):
    x = np.ascontiguousarray(np.asarray(L0_attn_x).reshape(SEQ, D))
    Wqkv = np.asarray(L0_attn_W_qkv)
    bq = np.asarray(L0_attn_b_qkv)
    Wout = np.ascontiguousarray(np.asarray(L0_attn_W_out))
    bo = np.ascontiguousarray(np.asarray(L0_attn_b_out).reshape(1, D))
    wgu_full = np.asarray(L0_ffn_w_gate_up)
    wp_full = np.asarray(L0_ffn_w_proj)
    in_maps = []
    for c in range(NCORES):
        cols = [Wqkv[:, p * D + c * 256:p * D + (c + 1) * 256] for p in range(3)]
        wq = np.ascontiguousarray(np.concatenate(cols, axis=1))
        bqc = np.ascontiguousarray(
            np.concatenate([bq[p * D + c * 256:p * D + (c + 1) * 256]
                            for p in range(3)]).reshape(1, 768))
        in_maps.append({
            "x_own": np.ascontiguousarray(x[c * TOWN:(c + 1) * TOWN]),
            "oofs": np.array([[i * SEQ + c * TOWN for i in range(NCORES)]], np.int32),
            "wqkv": wq, "bqkv": bqc, "wout": Wout, "bout": bo,
            "wgu": np.ascontiguousarray(wgu_full[c * 2048:(c + 1) * 2048]),
            "wp": np.ascontiguousarray(wp_full[:, c * 1024:(c + 1) * 1024]),
        })
    return in_maps


def kernel(**inputs):
    nc = _get_nc()
    in_maps = make_in_maps(**inputs)
    res = run_bass_kernel_spmd(nc, in_maps, core_ids=list(range(NCORES)))
    outp = np.concatenate([res.results[c]["out"] for c in range(NCORES)], axis=0)
    return outp.reshape(1, SEQ, D)


# version 24 (final)
# speedup vs baseline: 1.0686x; 1.0228x over previous
import contextlib
import numpy as np
import concourse.bass as bass
import concourse.mybir as mybir
from concourse.bass_utils import run_bass_kernel_spmd

F32 = mybir.dt.float32
BF16 = mybir.dt.bfloat16
AF = mybir.ActivationFunctionType
OP = mybir.AluOpType
AX = mybir.AxisListType

NCORES = 8
D = 2048
SEQ = 8192
TOWN = 1024
NCH = 16
DT = 16
HFFN = 8192
MAGIC = 12582912.0
QK_SCALE = 0.08838834764831845
LN_EPS = 1e-5
RMS_EPS = float(np.finfo(np.float32).eps)
NEG_BIG = -3.0e38
RG = [list(range(NCORES))]


class Sem:
    def __init__(self, h, name):
        self.h = h
        self.name = name
        self.n = 0

    def inc(self, instr, k=1):
        instr.then_inc(self.h, k)
        self.n += k
        return self.n


class PStream:

    def __init__(self, nc, scope, sems, name, nsem=2):
        self.nsem = nsem
        self.sems = []
        for i in range(nsem):
            nm = f"{name}{i}"
            s = Sem(scope.enter_context(nc.semaphore(nm)), nm)
            sems[nm] = s
            self.sems.append(s)

    def issue(self, q, out, in_, grp):
        s = self.sems[grp % self.nsem]
        s.inc(q.dma_start(out=out, in_=in_), 16)
        return (grp % self.nsem, s.n)

    def tok(self, grp, per_grp=1):
        return (grp % self.nsem, (grp // self.nsem + 1) * 16 * per_grp)

    def wait(self, eng, tok):
        eng.wait_ge(self.sems[tok[0]].h, tok[1])


def build_nc(phases=99, dbg=(), p4_nch=NCH, p4_tail=True):
    nc = bass.Bass("TRN2", target_bir_lowering=False, debug=False)

    x_own = nc.dram_tensor("x_own", [TOWN, D], F32, kind="ExternalInput").ap()
    wqkv = nc.dram_tensor("wqkv", [D, 768], F32, kind="ExternalInput").ap()
    bqkv = nc.dram_tensor("bqkv", [1, 768], F32, kind="ExternalInput").ap()
    wout = nc.dram_tensor("wout", [D, D], F32, kind="ExternalInput").ap()
    bout = nc.dram_tensor("bout", [1, D], F32, kind="ExternalInput").ap()
    wgu = nc.dram_tensor("wgu", [2048, 2048], F32, kind="ExternalInput").ap()
    wp = nc.dram_tensor("wp", [2048, 1024], F32, kind="ExternalInput").ap()
    out = nc.dram_tensor("out", [TOWN, D], F32, kind="ExternalOutput").ap()

    ggu_in = nc.dram_tensor("ggu_in", [2048, 2048], BF16).ap()
    ggu_g = nc.dram_tensor("ggu_g", [16384, 2048], BF16, addr_space="Shared").ap()
    gp_in = nc.dram_tensor("gp_in", [1024, 2048], BF16).ap()
    gp_g = nc.dram_tensor("gp_g", [8192, 2048], BF16, addr_space="Shared").ap()
    xnt_in = nc.dram_tensor("xnt_in", [2048, 1024], BF16).ap()
    xnt_g = nc.dram_tensor("xnt_g", [16384, 1024], BF16, addr_space="Shared").ap()
    h_dram = nc.dram_tensor("h_dram", [HFFN, TOWN], BF16).ap()
    hn_dram = nc.dram_tensor("hn_dram", [HFFN, TOWN], BF16).ap()
    a2a_in = nc.dram_tensor("a2a_in", [SEQ, 256], BF16).ap()
    ag_o = nc.dram_tensor("ag_o", [NCORES * SEQ, 256], BF16, addr_space="Shared").ap()
    a2a_out = nc.dram_tensor("a2a_out", [SEQ, 256], BF16).ap()
    oofs = nc.dram_tensor("oofs", [1, NCORES], mybir.dt.int32, kind="ExternalInput").ap()

    dbg_specs = {
        "ggu_g": ([16384, 2048], BF16), "gp_g": ([8192, 2048], BF16),
        "xnt_g": ([16384, 1024], BF16),
        "qt_d": ([256, SEQ], BF16), "kt_d": ([256, SEQ], BF16),
        "v_d": ([SEQ, 256], BF16),
        "h_d": ([HFFN, TOWN], BF16), "rstd_d": ([128, TOWN], F32),
        "a2a_d": ([SEQ, 256], BF16), "hn_d": ([HFFN, TOWN], BF16),
    }
    dbg_aps = {name: nc.dram_tensor("d_" + name, *dbg_specs[name],
                                    kind="ExternalOutput").ap()
               for name in dbg}

    es = contextlib.ExitStack()
    with es:
        ident_bf = es.enter_context(nc.sbuf_tensor("ident_bf", [128, 128], BF16))
        ident_f = es.enter_context(nc.sbuf_tensor("ident_f", [128, 128], F32))
        ones_row = es.enter_context(nc.sbuf_tensor("ones_row", [1, 512], BF16))
        ones_col = es.enter_context(nc.sbuf_tensor("ones_col", [128, 1], BF16))
        ones_rowf = es.enter_context(nc.sbuf_tensor("ones_rowf", [1, 128], F32))
        rstd_bc = es.enter_context(nc.sbuf_tensor("rstd_bc", [128, TOWN], F32))
        oofs_sb = es.enter_context(nc.sbuf_tensor("oofs_sb", [1, NCORES], mybir.dt.int32))

        pA = es.enter_context(nc.psum_tensor("pA", [128, 2048], F32))
        pB = es.enter_context(nc.psum_tensor("pB", [128, 2048], F32))

        def bA(i):
            return pA[:, i * 512:(i + 1) * 512]

        def bB(i):
            return pB[:, i * 512:(i + 1) * 512]

        sems = {}
        for nm in ["dL", "dG", "cPE", "cAC", "cVE", "cGP", "cCC"]:
            sems[nm] = Sem(es.enter_context(nc.semaphore(nm)), nm)
        dL, dG = sems["dL"], sems["dG"]
        cPE, cAC, cVE, cGP, cCC = (sems[k] for k in ["cPE", "cAC", "cVE", "cGP", "cCC"])

        def snap():
            return {nm: s.n for nm, s in sems.items()}

        block = es.enter_context(nc.Block())

        def barrier(eng, snp, skip=()):
            for nm, v in snp.items():
                if v > 0 and nm not in skip:
                    eng.wait_ge(sems[nm].h, v)

        @block.gpsimd
        def _(g):
            for t_ in (ident_bf, ident_f):
                cGP.inc(g.memset(t_[:], 0.0))
                g.wait_ge(cGP.h, cGP.n)
                cGP.inc(g.affine_select(out=t_[:], in_=t_[:], compare_op=OP.not_equal,
                                        fill=1.0, base=0, pattern=[[-1, 128]],
                                        channel_multiplier=1))
            cGP.inc(g.memset(ones_row[:], 1.0))
            cGP.inc(g.memset(ones_col[:], 1.0))
            cGP.inc(g.memset(ones_rowf[:], 1.0))

        qw = es.enter_context(nc.sbuf_tensor("qw", [128, 2, 2048], BF16))
        qh = es.enter_context(nc.sbuf_tensor("qh", [128, 2, 2048], BF16))
        qg = es.enter_context(nc.sbuf_tensor("qg", [128, 2, 32], F32))
        qsb = es.enter_context(nc.sbuf_tensor("qsb", [128, 2, 32], BF16))
        qgr = es.enter_context(nc.sbuf_tensor("qgr", [128, 2, 32], F32))
        qstage = es.enter_context(nc.sbuf_tensor("qstage", [128, 2, 512], BF16))
        qr = es.enter_context(nc.sbuf_tensor("qr", [128, 2048], F32))

        s1 = snap()
        with contextlib.ExitStack() as e1:
            xnt = e1.enter_context(nc.sbuf_tensor("xnt", [128, DT, 1024], BF16))
            xt = e1.enter_context(nc.sbuf_tensor("xt", [128, 2, 2048], F32))
            sqs = e1.enter_context(nc.sbuf_tensor("sqs", [128, 2048], F32))
            xn_tm = e1.enter_context(nc.sbuf_tensor("xn_tm", [128, 2, 2048], BF16))
            stt = e1.enter_context(nc.sbuf_tensor("stt", [128, 2, 8], F32))
            xn_f = e1.enter_context(nc.sbuf_tensor("xn_f", [128, 2048], F32))
            NA = 6
            NV1 = 9
            SX1 = PStream(nc, es, sems, "SX1")

            @block.sync
            def _(sy):
                barrier(sy, s1)
                for i in range(8):
                    if i >= 2:
                        sy.wait_ge(cVE.h, s1["cVE"] + (i - 1) * NV1)
                        sy.wait_ge(cAC.h, s1["cAC"] + (i - 2) * NA + 1)
                    SX1.issue(sy, xt[:, i % 2, :], x_own[i * 128:(i + 1) * 128, :], i)

            @block.scalar
            def _(sc):
                barrier(sc, s1)
                for i in range(8):
                    par = i % 2
                    SX1.wait(sc, SX1.tok(i))
                    cAC.inc(sc.activation(out=sqs[:], in_=xt[:, par, :], func=AF.Square,
                                          accum_out=stt[:, par, 6:7]))
                    sc.wait_ge(cVE.h, s1["cVE"] + i * NV1 + 5)
                    cAC.inc(sc.activation(out=stt[:, par, 2:3], in_=stt[:, par, 1:2],
                                          func=AF.Sqrt))
                    for grp in range(4):
                        sc.wait_ge(cPE.h, s1["cPE"] + i * 4 + grp + 1)
                        cAC.inc(sc.activation(
                            out=xnt[:, grp * 4:(grp + 1) * 4, i * 128:(i + 1) * 128],
                            in_=bB(grp % 2), func=AF.Copy))

            @block.vector
            def _(v):
                barrier(v, s1)
                for i in range(8):
                    par = i % 2
                    SX1.wait(v, SX1.tok(i))
                    cVE.inc(v.reduce_sum(stt[:, par, 0:1], xt[:, par, :], AX.X))
                    v.wait_ge(cAC.h, s1["cAC"] + i * NA + 1)
                    v.wait_ge(cVE.h, cVE.n)
                    cVE.inc(v.tensor_scalar(stt[:, par, 4:5], stt[:, par, 0:1],
                                            -1.0 / D, None, OP.mult))
                    v.wait_ge(cVE.h, cVE.n)
                    cVE.inc(v.tensor_tensor(stt[:, par, 7:8], stt[:, par, 4:5],
                                            stt[:, par, 4:5], OP.mult))
                    v.wait_ge(cVE.h, cVE.n)
                    cVE.inc(v.tensor_scalar(stt[:, par, 1:2], stt[:, par, 6:7],
                                            1.0 / D, LN_EPS, OP.mult, OP.add))
                    v.wait_ge(cVE.h, cVE.n)
                    cVE.inc(v.tensor_tensor(stt[:, par, 1:2], stt[:, par, 1:2],
                                            stt[:, par, 7:8], OP.subtract))
                    v.wait_ge(cAC.h, s1["cAC"] + i * NA + 2)
                    cVE.inc(v.reciprocal(stt[:, par, 3:4], stt[:, par, 2:3]))
                    v.wait_ge(cVE.h, cVE.n)
                    cVE.inc(v.tensor_tensor(stt[:, par, 5:6], stt[:, par, 4:5],
                                            stt[:, par, 3:4], OP.mult))
                    v.wait_ge(cVE.h, cVE.n)
                    cVE.inc(v.tensor_scalar(xn_f[:], xt[:, par, :],
                                            stt[:, par, 3:4], stt[:, par, 5:6],
                                            OP.mult, OP.add))
                    v.wait_ge(cVE.h, cVE.n)
                    if i >= 2:
                        v.wait_ge(cPE.h, s1["cPE"] + (i - 1) * 4)
                    cVE.inc(v.tensor_copy(xn_tm[:, par, :], xn_f[:]))

            @block.tensor
            def _(te):
                barrier(te, s1)
                for i in range(8):
                    te.wait_ge(cVE.h, s1["cVE"] + (i + 1) * NV1)
                    for grp in range(4):
                        gidx = i * 4 + grp
                        if gidx >= 2:
                            pi, pg = (gidx - 2) // 4, (gidx - 2) % 4
                            te.wait_ge(cAC.h, s1["cAC"] + pi * NA + 2 + pg + 1)
                        last = None
                        for j in range(4):
                            dt_i = grp * 4 + j
                            last = te.matmul(bB(grp % 2)[:, j * 128:(j + 1) * 128],
                                             xn_tm[:, i % 2, dt_i * 128:(dt_i + 1) * 128],
                                             ident_bf[:], start=True, stop=True)
                        cPE.inc(last)

            ac_p1 = s1["cAC"] + 8 * NA

            @block.sync
            def _(sy):
                sy.wait_ge(cAC.h, ac_p1)
                dL.inc(sy.dma_start(out=xnt_in.rearrange("(a p) t -> p a t", p=128),
                                    in_=xnt[:]), 16)

            st_p1 = dL.n

            @block.gpsimd
            def _(g):
                g.wait_ge(dL.h, st_p1)
                cCC.inc(g.collective_compute("AllGather", OP.bypass, replica_groups=RG,
                                             ins=[xnt_in.opt()], outs=[xnt_g.opt()]), 1)
        cc_xnt = cCC.n

        s0 = snap()
        if True:

            tiles = [("gu", i) for i in range(16)] + [("wp", i) for i in range(16)]
            NV = 8
            NVQ = 6
            SQW = PStream(nc, es, sems, "SQW")
            SST0 = PStream(nc, es, sems, "SST0")

            def tF(t):
                return 2048 if t[0] == "gu" else 1024

            pe_groups_after = []
            tot = 0
            for t in tiles:
                tot += tF(t) // 512
                pe_groups_after.append(tot)
            NBG = tot

            def pool_quant(g, i):
                t = tiles[i]
                par = i % 2
                F = tF(t)
                G = F // 64
                h3 = qh[:, par, :F].rearrange("p (g k) -> p g k", k=64)
                sb = qsb[:, par, :G].unsqueeze(2).broadcast_to([128, G, 64])
                g.wait_ge(cVE.h, s0["cVE"] + (i + 1) * NVQ)
                cGP.inc(g.tensor_scalar(qr[:, :F], qr[:, :F],
                                        1.0, -1.0, OP.min, OP.max))
                g.wait_ge(cGP.h, cGP.n)
                if i >= 2:
                    g.wait_ge(cPE.h, s0["cPE"] + pe_groups_after[i - 2])
                cGP.inc(g.tensor_tensor(
                    h3, qr[:, :F].rearrange("p (g k) -> p g k", k=64),
                    sb, OP.mult))

            @block.gpsimd
            def _(g):
                for i, t in enumerate(tiles):
                    if i >= 2:
                        g.wait_ge(cVE.h, s0["cVE"] + (i - 1) * NVQ)
                    src = (wgu if t[0] == "gu" else wp)[t[1] * 128:(t[1] + 1) * 128, :]
                    SQW.issue(g, qw[:, i % 2, :tF(t)], src, i)
                    if i >= 1:
                        pool_quant(g, i - 1)
                pool_quant(g, len(tiles) - 1)

            @block.vector
            def _(v):
                for i, t in enumerate(tiles):
                    par = i % 2
                    F = tF(t)
                    G = F // 64
                    SQW.wait(v, SQW.tok(i))
                    if i >= 2:
                        v.wait_ge(cGP.h, s0["cGP"] + (i - 1) * 2)
                        v.wait_ge(cPE.h, s0["cPE"] + pe_groups_after[i - 2])
                    w3 = qw[:, par, :F].rearrange("p (g k) -> p g k", k=64)
                    h3 = qh[:, par, :F].rearrange("p (g k) -> p g k", k=64)
                    cVE.inc(v.tensor_reduce(qg[:, par, :G], w3, AX.X, OP.add,
                                            apply_absolute_value=True))
                    v.wait_ge(cVE.h, cVE.n)
                    cVE.inc(v.tensor_scalar(qg[:, par, :G], qg[:, par, :G],
                                            1.0 / 64.0, 1e-8, OP.mult, OP.max))
                    v.wait_ge(cVE.h, cVE.n)
                    cVE.inc(v.tensor_copy(qsb[:, par, :G], qg[:, par, :G]))
                    v.wait_ge(cVE.h, cVE.n)
                    cVE.inc(v.reciprocal(qgr[:, par, :G], qsb[:, par, :G]))
                    v.wait_ge(cVE.h, cVE.n)
                    rb = qgr[:, par, :G].unsqueeze(2).broadcast_to([128, G, 64])
                    cVE.inc(v.tensor_tensor(h3, w3, rb, OP.mult))
                    v.wait_ge(cVE.h, cVE.n)
                    if i >= 1:
                        v.wait_ge(cGP.h, s0["cGP"] + i * 2)
                    cVE.inc(v.tensor_scalar(qr[:, :F], qh[:, par, :F],
                                            MAGIC, -MAGIC, OP.add, OP.add))



            @block.tensor
            def _(te):
                gidx = 0
                for i, t in enumerate(tiles):
                    te.wait_ge(cGP.h, s0["cGP"] + (i + 1) * 2)
                    for grp in range(tF(t) // 512):
                        if gidx >= 2:
                            te.wait_ge(cAC.h, s0["cAC"] + gidx - 1)
                        pb = bB(2 + gidx % 2)
                        last = None
                        for j in range(4):
                            dt_i = grp * 4 + j
                            last = te.matmul(pb[:, j * 128:(j + 1) * 128],
                                             qh[:, i % 2, dt_i * 128:(dt_i + 1) * 128],
                                             ident_bf[:], start=True, stop=True)
                        cPE.inc(last)
                        gidx += 1

            @block.scalar
            def _(sc):
                for gidx in range(NBG):
                    sc.wait_ge(cPE.h, s0["cPE"] + gidx + 1)
                    if gidx >= 2:
                        SST0.wait(sc, SST0.tok(gidx - 2))
                    cAC.inc(sc.activation(out=qstage[:, gidx % 2, :],
                                          in_=bB(2 + gidx % 2), func=AF.Copy))

            @block.sync
            def _(sy):
                gidx = 0
                for i, t in enumerate(tiles):
                    for grp in range(tF(t) // 512):
                        sy.wait_ge(cAC.h, s0["cAC"] + gidx + 1)
                        dst = (ggu_in if t[0] == "gu" else gp_in)[
                            grp * 512:(grp + 1) * 512, t[1] * 128:(t[1] + 1) * 128]
                        SST0.issue(
                            sy, dst.rearrange("(a p) c -> p a c", p=128),
                            qstage[:, gidx % 2, :].rearrange("p (a c) -> p a c", c=128),
                            gidx)
                        gidx += 1

            st_p0 = [s.n for s in SST0.sems]

            @block.gpsimd
            def _(g):
                for s, v in zip(SST0.sems, st_p0):
                    g.wait_ge(s.h, v)
                cCC.inc(g.collective_compute("AllGather", OP.bypass, replica_groups=RG,
                                             ins=[ggu_in.opt()], outs=[ggu_g.opt()]), 1)
                cCC.inc(g.collective_compute("AllGather", OP.bypass, replica_groups=RG,
                                             ins=[gp_in.opt()], outs=[gp_g.opt()]), 1)
        cc_ffn = cCC.n

        s2 = snap()
        exB = es.enter_context(contextlib.ExitStack())
        QT = exB.enter_context(nc.sbuf_tensor("QT", [128, 2, SEQ], BF16))
        KT = exB.enter_context(nc.sbuf_tensor("KT", [128, 2, SEQ], BF16))
        Vsb = exB.enter_context(nc.sbuf_tensor("Vsb", [128, 64, 256], BF16))
        with contextlib.ExitStack() as e2:
            xg = e2.enter_context(nc.sbuf_tensor("xg", [128, 3, DT, 512], BF16))
            wqkv_bf = e2.enter_context(nc.sbuf_tensor("wqkv_bf", [128, DT, 768], BF16))
            bqkv_bf = e2.enter_context(nc.sbuf_tensor("bqkv_bf", [1, 768], BF16))
            SXG = PStream(nc, es, sems, "SXG", nsem=3)
            SXH = PStream(nc, es, sems, "SXH", nsem=3)

            @block.gpsimd
            def _(g):
                dG.inc(g.dma_start(out=bqkv_bf[:], in_=bqkv[:, :]), 16)
                dG.inc(g.dma_start(out=wqkv_bf[:],
                                   in_=wqkv.rearrange("(a p) c -> p a c", p=128)), 16)

            dg_wqkv = dG.n

            @block.sync
            def _(sy):
                sy.wait_ge(cCC.h, cc_xnt)
                for tp in range(16):
                    if tp >= 3:
                        sy.wait_ge(cPE.h, s2["cPE"] + (tp - 2) * 8)
                    r = tp // 2
                    coff = (tp % 2) * 512
                    src = xnt_g[r * 2048:(r + 1) * 2048, coff:coff + 256]
                    SXG.issue(sy, xg[:, tp % 3, :, 0:256],
                              src.rearrange("(a p) t -> p a t", p=128), tp)

            @block.gpsimd
            def _(g):
                g.wait_ge(cCC.h, cc_xnt)
                for tp in range(16):
                    if tp >= 3:
                        g.wait_ge(cPE.h, s2["cPE"] + (tp - 2) * 8)
                    r = tp // 2
                    coff = (tp % 2) * 512 + 256
                    src = xnt_g[r * 2048:(r + 1) * 2048, coff:coff + 256]
                    SXH.issue(g, xg[:, tp % 3, :, 256:512],
                              src.rearrange("(a p) t -> p a t", p=128), tp)

            @block.tensor
            def _(te):
                te.wait_ge(dG.h, dg_wqkv)
                for tp in range(16):
                    par = tp % 3
                    SXG.wait(te, SXG.tok(tp))
                    SXH.wait(te, SXH.tok(tp))
                    for mt in range(4):
                        if tp > 0 or mt >= 2:
                            pj, pmt = (tp, mt - 2) if mt >= 2 else (tp - 1, mt + 2)
                            te.wait_ge(cAC.h, s2["cAC"] + pj * 8 + pmt + 1)
                        pb = bA(mt % 2)
                        for dt_i in range(DT):
                            te.matmul(pb, wqkv_bf[:, dt_i, mt * 128:(mt + 1) * 128],
                                      xg[:, par, dt_i, :], start=(dt_i == 0), stop=False)
                        cPE.inc(te.matmul(pb, bqkv_bf[0:1, mt * 128:(mt + 1) * 128],
                                          ones_row[0:1, :], start=False, stop=True))
                    for vt in range(4):
                        if tp > 0 or vt >= 2:
                            pj, pvt = (tp, vt - 2) if vt >= 2 else (tp - 1, vt + 2)
                            te.wait_ge(cAC.h, s2["cAC"] + pj * 8 + 4 + pvt + 1)
                        pb = bA(2 + vt % 2)[:, 0:256]
                        for dt_i in range(DT):
                            te.matmul(pb, xg[:, par, dt_i, vt * 128:(vt + 1) * 128],
                                      wqkv_bf[:, dt_i, 512:768],
                                      start=(dt_i == 0), stop=False)
                        cPE.inc(te.matmul(pb, ones_row[0:1, 0:128],
                                          bqkv_bf[0:1, 512:768], start=False, stop=True))

            @block.scalar
            def _(sc):
                for tp in range(16):
                    for mt in range(4):
                        sc.wait_ge(cPE.h, s2["cPE"] + tp * 8 + mt + 1)
                        dstt = QT if mt < 2 else KT
                        scl = QK_SCALE if mt < 2 else 1.0
                        cAC.inc(sc.activation(
                            out=dstt[:, mt % 2, tp * 512:(tp + 1) * 512],
                            in_=bA(mt % 2), func=AF.Copy, scale=scl))
                    for vt in range(4):
                        sc.wait_ge(cPE.h, s2["cPE"] + tp * 8 + 4 + vt + 1)
                        cAC.inc(sc.activation(out=Vsb[:, tp * 4 + vt, :],
                                              in_=bA(2 + vt % 2)[:, 0:256], func=AF.Copy))
        s4 = snap()
        with contextlib.ExitStack() as e4:
            masksT = e4.enter_context(nc.sbuf_tensor("masksT", [128, 4, 512], F32))
            PT_sb = e4.enter_context(nc.sbuf_tensor("PT_sb", [128, 4, 1024], BF16))
            OT_raw = e4.enter_context(nc.sbuf_tensor("OT_raw", [128, 2, 512], BF16))
            a2a_st = e4.enter_context(nc.sbuf_tensor("a2a_st", [128, 2, 4, 128], BF16))
            m0 = e4.enter_context(nc.sbuf_tensor("m0", [128, 2, 4], F32))
            m0s_sb = e4.enter_context(nc.sbuf_tensor("m0s_sb", [1, 2], F32))
            nm0s = e4.enter_context(nc.sbuf_tensor("nm0s", [128, 32], F32))
            lrow_sb = e4.enter_context(nc.sbuf_tensor("lrow_sb", [1, 512], F32))
            onef = e4.enter_context(nc.sbuf_tensor("onef", [1, 1], F32))
            linv = e4.enter_context(nc.sbuf_tensor("linv", [128, 4], F32))
            SAS = PStream(nc, es, sems, "SAS")

            NHC = 2 * p4_nch
            hcs = [(head, chunk) for head in range(2) for chunk in range(p4_nch)]

            peL, acL, veL = {}, {}, {}
            pe_n = [s4["cPE"]]
            ac_n = [s4["cAC"]]
            ve_n = [s4["cVE"]]

            def _mk(counter, lab_dict):
                def _add(lab, k=1):
                    counter[0] += k
                    lab_dict[lab] = counter[0]
                return _add

            PEp, ACp, VEp = _mk(pe_n, peL), _mk(ac_n, acL), _mk(ve_n, veL)
            for hc in range(NHC + 2):
                if 1 <= hc <= NHC:
                    PEp(("pT1", hc - 1))
                if 2 <= hc <= NHC + 1:
                    PEp(("pTb", hc - 2))
                if hc < NHC:
                    PEp(("pSq", hc))
                if hc < NHC:
                    VEp(("pr1", hc))
                if 1 <= hc <= NHC:
                    VEp(("pr2", hc - 1))
                if 2 <= hc <= NHC + 1:
                    VEp(("pcp", hc - 2))
            for hc, (head, chunk) in enumerate(hcs):
                nu = 2 * (chunk + 1)
                s_order = [1, 0] + list(range(2, nu))
                for i, u in enumerate(s_order):
                    PEp(("ST", hc, u))
                    if i >= 2:
                        PEp(("PVh", hc, s_order[i - 2]))
                for t in (s_order[nu - 2], s_order[nu - 1]):
                    PEp(("PVh", hc, t))
                PEp(("otm", hc))
                PEp(("lrowT", hc))
                for u in range(nu):
                    ACp(("exph", hc, u))
                ACp(("otev", hc))
                for u in range(nu):
                    if u // 2 == chunk:
                        VEp(("maskT", hc, u), 2)
                VEp(("lrowev", hc))
                VEp(("recip", hc))
                VEp(("scale", hc), 4)

            def lastPV(hc):
                chunk = hcs[hc][1]
                return ("PVh", hc, 0) if chunk == 0 else ("PVh", hc, 2 * chunk + 1)

            @block.gpsimd
            def _(g):
                barrier(g, s4)
                for kc in range(4):
                    cGP.inc(g.memset(masksT[:, kc, :], 0.0))
                    g.wait_ge(cGP.h, cGP.n)
                    cGP.inc(g.affine_select(out=masksT[:, kc, :], in_=masksT[:, kc, :],
                                            compare_op=OP.is_ge, fill=NEG_BIG,
                                            base=-kc * 128, pattern=[[1, 512]],
                                            channel_multiplier=-1))
                cGP.inc(g.memset(onef[:], 1.0))

            gp_masks = cGP.n

            @block.tensor
            def _(te):
                barrier(te, s4)
                for it in range(NHC + 2):
                    if 1 <= it <= NHC:
                        k = it - 1
                        tgt = pA if k % 2 == 0 else pB
                        te.wait_ge(cVE.h, veL[("pr1", k)])
                        cPE.inc(te.matmul(tgt[0:1, 0:128], m0[:, k % 2, 0:1],
                                          ident_f[:], start=True, stop=True))
                    if 2 <= it <= NHC + 1:
                        k = it - 2
                        tgt = pB if k % 2 == 0 else pA
                        te.wait_ge(cVE.h, veL[("pr2", k)])
                        if k + 1 < NHC:
                            te.wait_ge(cVE.h, veL[("pr1", k + 1)])
                        cPE.inc(te.matmul(tgt[:, 192:193], ones_rowf[0:1, :],
                                          m0s_sb[0:1, k % 2:k % 2 + 1],
                                          start=True, stop=True))
                    if it < NHC:
                        k = it
                        head, chunk = hcs[k]
                        qb = chunk * 512
                        tgt = pA if k % 2 == 0 else pB
                        if k >= 3:
                            te.wait_ge(cVE.h, veL[("pcp", k - 3)])
                        last = None
                        for qt in range(4):
                            last = te.matmul(
                                tgt[:, qt * 512:(qt + 1) * 512],
                                QT[:, head, qb + qt * 128:qb + (qt + 1) * 128],
                                KT[:, head, 0:512], start=True, stop=True)
                        cPE.inc(last)
                te.wait_ge(cGP.h, gp_masks)
                for hc, (head, chunk) in enumerate(hcs):
                    qb = chunk * 512
                    nu = 2 * (chunk + 1)
                    s_order = [1, 0] + list(range(2, nu))
                    ob = bB(2 + hc % 2)

                    def emit_ST(u):
                        kb = u // 2
                        h = u % 2
                        p = u % 2
                        if u == 0:
                            if hc > 0:
                                te.wait_ge(cVE.h, veL[("scale", hc - 1)])
                        elif u == 1:
                            if hc > 0:
                                te.wait_ge(cAC.h, acL[("otev", hc - 1)])
                            else:
                                te.wait_ge(cVE.h, veL[("pcp", NHC - 1)])
                        else:
                            te.wait_ge(cAC.h, acL[("exph", hc, u - 2)])
                        last = None
                        for j in range(2):
                            kc = 2 * h + j
                            koff = kb * 512 + kc * 128
                            last = te.matmul(
                                pA[:, p * 1024 + j * 512:p * 1024 + (j + 1) * 512],
                                KT[:, head, koff:koff + 128],
                                QT[:, head, qb:qb + 512],
                                start=True, stop=True)
                        cPE.inc(last)

                    def emit_PVh(u, first, last_f):
                        kb = u // 2
                        h = u % 2
                        p = u % 4
                        te.wait_ge(cAC.h, acL[("exph", hc, u)])
                        if first and hc >= 2:
                            te.wait_ge(cAC.h, acL[("otev", hc - 2)])
                        last = None
                        for j in range(2):
                            kc = 2 * h + j
                            te.matmul(ob, Vsb[:, kb * 4 + kc,
                                              head * 128:(head + 1) * 128],
                                      PT_sb[:, p, j * 512:(j + 1) * 512],
                                      start=(first and j == 0),
                                      stop=(last_f and j == 1),
                                      skip_group_check=True)
                        for j in range(2):
                            last = te.matmul(
                                bB(0)[0:1, 0:512], ones_col[:],
                                PT_sb[:, p, j * 512:(j + 1) * 512],
                                start=(first and j == 0),
                                stop=(last_f and j == 1),
                                skip_group_check=True)
                        cPE.inc(last)

                    npv = 0
                    for i, u in enumerate(s_order):
                        emit_ST(u)
                        if i >= 2:
                            emit_PVh(s_order[i - 2], npv == 0, npv == nu - 1)
                            npv += 1
                    for t in (s_order[nu - 2], s_order[nu - 1]):
                        emit_PVh(t, npv == 0, npv == nu - 1)
                        npv += 1
                    te.wait_ge(cAC.h, acL[("otev", hc)])
                    last = None
                    for qt in range(4):
                        last = te.matmul(pA[:, qt * 128:(qt + 1) * 128],
                                         OT_raw[:, hc % 2, qt * 128:(qt + 1) * 128],
                                         ident_bf[:], start=True, stop=True)
                    cPE.inc(last)
                    te.wait_ge(cVE.h, veL[("lrowev", hc)])
                    last = None
                    for qt in range(4):
                        last = te.matmul(bB(1)[:, qt:qt + 1],
                                         lrow_sb[0:1, qt * 128:(qt + 1) * 128],
                                         onef[0:1, 0:1], start=True, stop=True)
                    cPE.inc(last)

            @block.scalar
            def _(sc):
                barrier(sc, s4)
                sc.wait_ge(cVE.h, veL[("pcp", NHC - 1)])
                for hc, (head, chunk) in enumerate(hcs):
                    nu = 2 * (chunk + 1)
                    for u in range(nu):
                        p = u % 4
                        sc.wait_ge(cPE.h, peL[("ST", hc, u)])
                        if u // 2 == chunk:
                            sc.wait_ge(cVE.h, veL[("maskT", hc, u)])
                        if u < 4:
                            if hc > 0:
                                sc.wait_ge(cPE.h, peL[lastPV(hc - 1)])
                        else:
                            sc.wait_ge(cPE.h, peL[("PVh", hc, u - 4)])
                        cAC.inc(sc.activation(out=PT_sb[:, p, 0:1024],
                                              in_=pA[:, (u % 2) * 1024:
                                                      (u % 2 + 1) * 1024],
                                              func=AF.Exp, bias=nm0s[:, hc:hc + 1],
                                              scale=1.0))
                    sc.wait_ge(cPE.h, peL[lastPV(hc)])
                    cAC.inc(sc.activation(out=OT_raw[:, hc % 2, :],
                                          in_=bB(2 + hc % 2), func=AF.Copy))

            @block.vector
            def _(v):
                barrier(v, s4)
                for it in range(NHC + 2):
                    if it < NHC:
                        k = it
                        tgt = pA if k % 2 == 0 else pB
                        v.wait_ge(cPE.h, peL[("pSq", k)])
                        cVE.inc(v.reduce_max(m0[:, k % 2, 0:1], tgt[:, :],
                                             AX.X, negate=True))
                    if 1 <= it <= NHC:
                        k = it - 1
                        tgt = pA if k % 2 == 0 else pB
                        v.wait_ge(cPE.h, peL[("pT1", k)])
                        v.wait_ge(cVE.h, cVE.n)
                        cVE.inc(v.tensor_reduce(m0s_sb[0:1, k % 2:k % 2 + 1],
                                                tgt[0:1, 0:128], AX.X, OP.min))
                    if 2 <= it <= NHC + 1:
                        k = it - 2
                        tgt = pB if k % 2 == 0 else pA
                        v.wait_ge(cPE.h, peL[("pTb", k)])
                        v.wait_ge(cVE.h, cVE.n)
                        cVE.inc(v.tensor_copy(nm0s[:, k:k + 1], tgt[:, 192:193]))
                v.wait_ge(cGP.h, gp_masks)
                for hc, (head, chunk) in enumerate(hcs):
                    nu = 2 * (chunk + 1)
                    for u in range(nu):
                        if u // 2 == chunk:
                            p = u % 2
                            v.wait_ge(cPE.h, peL[("ST", hc, u)])
                            for j in range(2):
                                kc = 2 * (u % 2) + j
                                cVE.inc(v.tensor_tensor(
                                    pA[:, p * 1024 + j * 512:p * 1024 + (j + 1) * 512],
                                    pA[:, p * 1024 + j * 512:p * 1024 + (j + 1) * 512],
                                    masksT[:, kc, :], OP.add))
                    v.wait_ge(cPE.h, peL[lastPV(hc)])
                    v.wait_ge(cVE.h, cVE.n)
                    cVE.inc(v.tensor_copy(lrow_sb[:], bB(0)[0:1, 0:512]))
                    v.wait_ge(cPE.h, peL[("otm", hc)])
                    v.wait_ge(cPE.h, peL[("lrowT", hc)])
                    if hc >= 2:
                        SAS.wait(v, SAS.tok(hc - 2))
                    v.wait_ge(cVE.h, cVE.n)
                    cVE.inc(v.reciprocal(linv[:], bB(1)[:, 0:4]))
                    v.wait_ge(cVE.h, cVE.n)
                    for qt in range(4):
                        cVE.inc(v.tensor_scalar(a2a_st[:, hc % 2, qt, :],
                                                pA[:, qt * 128:(qt + 1) * 128],
                                                linv[:, qt:qt + 1], None, OP.mult))

            @block.sync
            def _(sy):
                barrier(sy, s4)
                for hc, (head, chunk) in enumerate(hcs):
                    sy.wait_ge(cVE.h, veL[("scale", hc)])
                    dst = a2a_in[chunk * 512:(chunk + 1) * 512,
                                 head * 128:(head + 1) * 128]
                    SAS.issue(sy, dst.rearrange("(a p) c -> p a c", p=128),
                              a2a_st[:, hc % 2, :, :], hc)

            st_a2a = [s.n for s in SAS.sems]

            @block.gpsimd
            def _(g):
                for s, v in zip(SAS.sems, st_a2a):
                    g.wait_ge(s.h, v)
                cCC.inc(g.collective_compute("AllGather", OP.bypass, replica_groups=RG,
                                             ins=[a2a_in.opt()], outs=[ag_o.opt()]), 1)
                dG.inc(g.dma_start(out=oofs_sb[:], in_=oofs[:, :]), 16)
                g.wait_ge(cCC.h, cCC.n)
                g.wait_ge(dG.h, dG.n)
                with g.register("vreg") as vreg:
                    for i in range(NCORES):
                        g.reg_load(vreg, oofs_sb[0:1, i:i + 1])
                        v = g.snap(vreg, min_val=0, max_val=NCORES * SEQ - TOWN)
                        dG.inc(g.dma_start(out=a2a_out[i * TOWN:(i + 1) * TOWN, :],
                                           in_=ag_o[bass.ds(v, TOWN), :]), 16)

        exB.close()
        cc_a2a = cCC.n

        s3 = snap()
        with contextlib.ExitStack() as e3:
            xnt = e3.enter_context(nc.sbuf_tensor("xnt3", [128, DT, 1024], BF16))
            SXN = PStream(nc, es, sems, "SXN")
            ggs = e3.enter_context(nc.sbuf_tensor("ggs", [128, 2, 2, DT, 256], BF16))
            hst = e3.enter_context(nc.sbuf_tensor("hst", [128, 2, 1024], BF16))
            gssb = e3.enter_context(nc.sbuf_tensor("gssb", [128, 3, 512], F32))
            hsq = e3.enter_context(nc.sbuf_tensor("hsq", [128, 2, 1024], BF16))
            rrow = e3.enter_context(nc.sbuf_tensor("rrow", [1, TOWN], F32))
            rrow2 = e3.enter_context(nc.sbuf_tensor("rrow2", [1, TOWN], F32))
            SGG = PStream(nc, es, sems, "SGG")
            SHS = PStream(nc, es, sems, "SHS")

            def pe_hp(hp):
                return s3["cPE"] + (hp + 1) * 5

            @block.sync
            def _(sy):
                barrier(sy, s3, skip=("cCC", "dG"))
                sy.wait_ge(cCC.h, cc_ffn)
                SXN.issue(sy, xnt[:],
                          xnt_in.rearrange("(a p) t -> p a t", p=128), 0)

                def st_h(hp):
                    sy.wait_ge(cVE.h, s3["cVE"] + (hp + 1) * 2)
                    SHS.issue(sy, h_dram[hp * 128:(hp + 1) * 128, :],
                              hst[:, hp % 2, :], hp)

                for hq in range(32):
                    if hq >= 2:
                        sy.wait_ge(cPE.h, pe_hp(2 * hq - 3))
                    r = hq // 8
                    hcol = (hq * 256) % 2048
                    for gi, blk in enumerate([r, 4 + r]):
                        src = ggu_g[blk * 2048:(blk + 1) * 2048, hcol:hcol + 256]
                        SGG.issue(sy, ggs[:, hq % 2, gi, :, :],
                                  src.rearrange("(a p) c -> p a c", p=128), hq)
                    if hq >= 1:
                        st_h(2 * hq - 2)
                        st_h(2 * hq - 1)
                for hp in (62, 63):
                    st_h(hp)

            @block.tensor
            def _(te):
                barrier(te, s3, skip=("cCC", "dG"))
                SXN.wait(te, SXN.tok(0))
                for hp in range(64):
                    hq, hloc = hp // 2, hp % 2
                    SGG.wait(te, SGG.tok(hq, per_grp=2))
                    for th in range(2):
                        u = hp * 2 + th
                        if u >= 3:
                            v_ = u - 3
                            te.wait_ge(cAC.h, s3["cAC"] + (v_ // 2) * 3 + v_ % 2 + 1)
                            te.wait_ge(cVE.h, s3["cVE"] + u - 2)
                        gb = [bA(0), bA(1), bB(2)][u % 3]
                        ub = [bA(2), bA(3), bB(3)][u % 3]
                        for gi, pb in [(0, gb), (1, ub)]:
                            last = None
                            for dt_i in range(DT):
                                last = te.matmul(
                                    pb, ggs[:, hq % 2, gi, dt_i,
                                            hloc * 128:(hloc + 1) * 128],
                                    xnt[:, dt_i, th * 512:(th + 1) * 512],
                                    start=(dt_i == 0), stop=(dt_i == DT - 1))
                            cPE.inc(last)
                    te.wait_ge(cAC.h, s3["cAC"] + hp * 3 + 3)
                    last = None
                    for th in range(2):
                        last = te.matmul(bB(th)[0:1, :], ones_col[:],
                                         hsq[:, hp % 2, th * 512:(th + 1) * 512],
                                         start=(hp == 0), stop=(hp == 63),
                                         skip_group_check=True)
                    cPE.inc(last)

            @block.scalar
            def _(sc):
                barrier(sc, s3, skip=("cCC", "dG"))
                for hp in range(64):
                    for th in range(2):
                        u = hp * 2 + th
                        sc.wait_ge(cPE.h, s3["cPE"] + hp * 5 + th * 2 + 1)
                        if u >= 3:
                            sc.wait_ge(cVE.h, s3["cVE"] + u - 2)
                        cAC.inc(sc.activation(out=gssb[:, u % 3, :],
                                              in_=[bA(0), bA(1), bB(2)][u % 3],
                                              func=AF.Silu))
                    sc.wait_ge(cVE.h, s3["cVE"] + (hp + 1) * 2)
                    cAC.inc(sc.activation(out=hsq[:, hp % 2, :], in_=hst[:, hp % 2, :],
                                          func=AF.Square))

            @block.vector
            def _(v):
                barrier(v, s3, skip=("cCC", "dG"))
                for hp in range(64):
                    for th in range(2):
                        u = hp * 2 + th
                        v.wait_ge(cAC.h, s3["cAC"] + hp * 3 + th + 1)
                        v.wait_ge(cPE.h, s3["cPE"] + hp * 5 + th * 2 + 2)
                        if hp >= 2 and th == 0:
                            SHS.wait(v, SHS.tok(hp - 2))
                        cVE.inc(v.tensor_tensor(hst[:, hp % 2, th * 512:(th + 1) * 512],
                                                gssb[:, u % 3, :],
                                                [bA(2), bA(3), bB(3)][u % 3],
                                                OP.mult))
                v.wait_ge(cPE.h, pe_hp(63))
                for th in range(2):
                    cVE.inc(v.tensor_scalar(rrow[0:1, th * 512:(th + 1) * 512],
                                            bB(th)[0:1, :], 1.0 / HFFN, RMS_EPS,
                                            OP.mult, OP.add))

            @block.scalar
            def _(sc):
                sc.wait_ge(cVE.h, s3["cVE"] + 130)
                cAC.inc(sc.activation(out=rrow2[:], in_=rrow[:], func=AF.Sqrt))

            @block.vector
            def _(v):
                v.wait_ge(cAC.h, s3["cAC"] + 193)
                cVE.inc(v.reciprocal(rrow[:], rrow2[:]))
                v.wait_ge(cVE.h, cVE.n)

            @block.tensor
            def _(te):
                te.wait_ge(cVE.h, s3["cVE"] + 131)
                for th in range(2):
                    last = te.matmul(bB(2 + th), ones_rowf[0:1, :],
                                     rrow[0:1, th * 512:(th + 1) * 512],
                                     start=True, stop=True)
                cPE.inc(last)

            @block.vector
            def _(v):
                v.wait_ge(cPE.h, pe_hp(63) + 1)
                for th in range(2):
                    cVE.inc(v.tensor_copy(rstd_bc[:, th * 512:(th + 1) * 512],
                                          bB(2 + th)))
        s5 = snap()
        with contextlib.ExitStack() as e5:
            ot_in = e5.enter_context(nc.sbuf_tensor("ot_in", [128, 2, 2048], BF16))
            OTs = e5.enter_context(nc.sbuf_tensor("OTs", [128, DT, 1024], BF16))
            wo_b = e5.enter_context(nc.sbuf_tensor("wo_b", [128, 2, DT, 512], BF16))
            gp_b = e5.enter_context(nc.sbuf_tensor("gp_b", [128, 4, 4, 512], BF16))
            h_b = e5.enter_context(nc.sbuf_tensor("h_b", [128, 4, 4, 1024], BF16))
            hn_b = e5.enter_context(nc.sbuf_tensor("hn_b", [128, 2, 4, 1024], BF16))
            fst = e5.enter_context(nc.sbuf_tensor("fst", [128, 8, 512], F32))
            x_b = e5.enter_context(nc.sbuf_tensor("x_b", [128, 2, 512], F32))
            o_st = e5.enter_context(nc.sbuf_tensor("o_st", [128, 2, 512], F32))
            bout_bf = e5.enter_context(nc.sbuf_tensor("bout_bf", [1, D], BF16))
            SOT = PStream(nc, es, sems, "SOT")
            SGP = PStream(nc, es, sems, "SGP", nsem=4)
            SXB = PStream(nc, es, sems, "SXB")
            SOS = PStream(nc, es, sems, "SOS")
            SWO = PStream(nc, es, sems, "SWO")
            SGQ = PStream(nc, es, sems, "SGQ", nsem=4)

            @block.sync
            def _(sy):
                barrier(sy, s5)
                sy.wait_ge(cCC.h, cc_a2a)
                for tt in range(8):
                    if tt >= 2:
                        sy.wait_ge(cPE.h, s5["cPE"] + (tt - 1) * 4)
                    src = a2a_out.rearrange("(i q) c -> i q c", q=1024)[
                        :, tt * 128:(tt + 1) * 128, :].transpose([1, 0, 2])
                    SOT.issue(sy, ot_in[:, tt % 2, :].rearrange("p (i c) -> p i c",
                                                                c=256), src, tt)

            @block.tensor
            def _(te):
                barrier(te, s5)
                for tt in range(8):
                    SOT.wait(te, SOT.tok(tt))
                    for grp in range(4):
                        gidx = tt * 4 + grp
                        if gidx >= 2:
                            te.wait_ge(cAC.h, s5["cAC"] + gidx - 1)
                        last = None
                        for j in range(4):
                            dt_i = grp * 4 + j
                            last = te.matmul(bB(grp % 2)[:, j * 128:(j + 1) * 128],
                                             ot_in[:, tt % 2,
                                                   dt_i * 128:(dt_i + 1) * 128],
                                             ident_bf[:], start=True, stop=True)
                        cPE.inc(last)

            @block.scalar
            def _(sc):
                barrier(sc, s5)
                for gidx in range(32):
                    tt, grp = gidx // 4, gidx % 4
                    sc.wait_ge(cPE.h, s5["cPE"] + gidx + 1)
                    cAC.inc(sc.activation(
                        out=OTs[:, grp * 4:(grp + 1) * 4, tt * 128:(tt + 1) * 128],
                        in_=bB(grp % 2), func=AF.Copy))

            pe_ot = s5["cPE"] + 32
            ac_ot = s5["cAC"] + 32
            s5b = snap()

            pe_z = [[0] * 8 for _ in range(4)]
            pe_y = [[0] * 64 for _ in range(4)]
            pe_T = [0] * 4
            ve_fev = [0] * 4
            ve_xadd = [[0] * 8 for _ in range(4)]
            ve_hn = [[0] * 16 for _ in range(4)]
            pe = pe_ot
            ve = s5b["cVE"]
            for dg in range(4):
                for hq4 in range(16):
                    ve += 1; ve_hn[dg][hq4] = ve
                for b in range(8):
                    pe += 1; pe_z[dg][b] = pe
                for hc in range(64):
                    pe += 1; pe_y[dg][hc] = pe
                for b in range(8):
                    ve += 1
                ve_fev[dg] = ve
                pe += 1; pe_T[dg] = pe
                for tt in range(8):
                    ve += 1; ve_xadd[dg][tt] = ve

            @block.gpsimd
            def _(g):
                barrier(g, s5b)
                dG.inc(g.dma_start(out=bout_bf[:], in_=bout[:, :]), 16)
                for dg in range(4):
                    if dg >= 2:
                        g.wait_ge(cPE.h, pe_z[dg - 2][7])
                    src = wout[:, dg * 512:(dg + 1) * 512]
                    SWO.issue(g, wo_b[:, dg % 2, :, :],
                              src.rearrange("(a p) c -> p a c", p=128), dg)

            @block.sync
            def _(sy):
                for dg in range(4):
                    for hq4 in range(16):
                        if dg > 0 or hq4 >= 4:
                            pg, ph = (dg, hq4 - 4) if hq4 >= 4 else (dg - 1, 12 + hq4)
                            sy.wait_ge(cPE.h, pe_y[pg][ph * 4 + 3])
                            sy.wait_ge(cVE.h, ve_hn[pg][ph])
                        gidx = dg * 16 + hq4
                        SGP.issue(sy, h_b[:, hq4 % 4, :, :],
                                  h_dram[hq4 * 512:(hq4 + 1) * 512, :].rearrange(
                                      "(a p) c -> p a c", p=128), gidx)
                    def ld_x(tt):
                        if dg > 0 or tt >= 2:
                            pg, pt = (dg, tt - 2) if tt >= 2 else (dg - 1, 6 + tt)
                            sy.wait_ge(cVE.h, ve_xadd[pg][pt])
                        SXB.issue(sy, x_b[:, tt % 2, :],
                                  x_own[tt * 128:(tt + 1) * 128,
                                        dg * 512:(dg + 1) * 512], dg * 8 + tt)

                    def st_o(tt):
                        sy.wait_ge(cVE.h, ve_xadd[dg][tt])
                        SOS.issue(sy,
                                  out[tt * 128:(tt + 1) * 128,
                                      dg * 512:(dg + 1) * 512],
                                  o_st[:, tt % 2, :], dg * 8 + tt)

                    ld_x(0)
                    ld_x(1)
                    for tt in range(2, 8):
                        st_o(tt - 2)
                        ld_x(tt)
                    st_o(6)
                    st_o(7)
                for s in SOS.sems:
                    sy.wait_ge(s.h, s.n)
                sy.wait_ge(dL.h, dL.n)

            @block.scalar
            def _(sc):
                for dg in range(4):
                    for hq4 in range(16):
                        if dg > 0 or hq4 >= 4:
                            pg, ph = (dg, hq4 - 4) if hq4 >= 4 else (dg - 1, 12 + hq4)
                            sc.wait_ge(cPE.h, pe_y[pg][ph * 4 + 3])
                            sc.wait_ge(cVE.h, ve_hn[pg][ph])
                        gidx = dg * 16 + hq4
                        SGQ.issue(sc, gp_b[:, hq4 % 4, :, :],
                                  gp_g[hq4 * 512:(hq4 + 1) * 512,
                                       dg * 512:(dg + 1) * 512].rearrange(
                                           "(a p) c -> p a c", p=128), gidx)

            @block.tensor
            def _(te):
                te.wait_ge(cAC.h, ac_ot)
                te.wait_ge(dG.h, s5b["dG"] + 16)
                for dg in range(4):
                    SWO.wait(te, SWO.tok(dg))
                    for b in range(8):
                        mt, th = b // 2, b % 2
                        pb = bA(b % 4) if b < 4 else bB(b % 4)
                        if dg > 0:
                            te.wait_ge(cVE.h, ve_xadd[dg - 1][b])
                        for dt_i in range(DT):
                            te.matmul(pb, wo_b[:, dg % 2, dt_i, mt * 128:(mt + 1) * 128],
                                      OTs[:, dt_i, th * 512:(th + 1) * 512],
                                      start=(dt_i == 0), stop=False,
                                      skip_group_check=True)
                        cPE.inc(te.matmul(
                            pb,
                            bout_bf[0:1, dg * 512 + mt * 128:dg * 512 + (mt + 1) * 128],
                            ones_row[0:1, :], start=False, stop=False,
                            skip_group_check=True))
                    for hc in range(64):
                        hq4, hs = hc // 4, hc % 4
                        if hs == 0:
                            te.wait_ge(cVE.h, ve_hn[dg][hq4])
                            SGQ.wait(te, SGQ.tok(dg * 16 + hq4))
                        last = None
                        for b in range(8):
                            mt, th = b // 2, b % 2
                            pb = bA(b % 4) if b < 4 else bB(b % 4)
                            last = te.matmul(
                                pb, gp_b[:, hq4 % 4, hs, mt * 128:(mt + 1) * 128],
                                hn_b[:, hq4 % 2, hs, th * 512:(th + 1) * 512],
                                start=False, stop=(hc == 63),
                                skip_group_check=True)
                        cPE.inc(last)
                    te.wait_ge(cVE.h, ve_fev[dg])
                    last = None
                    for b in range(8):
                        mt, th = b // 2, b % 2
                        for tsub in range(4):
                            j = th * 4 + tsub
                            tb = bA(j % 4) if j < 4 else bB(j % 4)
                            last = te.matmul(tb[:, mt * 128:(mt + 1) * 128],
                                             fst[:, b, tsub * 128:(tsub + 1) * 128],
                                             ident_f[:], start=True, stop=True,
                                             skip_group_check=True)
                    cPE.inc(last)

            @block.vector
            def _(v):
                for dg in range(4):
                    for hq4 in range(16):
                        SGP.wait(v, SGP.tok(dg * 16 + hq4))
                        if dg > 0 or hq4 >= 2:
                            pg, ph = (dg, hq4 - 2) if hq4 >= 2 else (dg - 1, 14 + hq4)
                            v.wait_ge(cPE.h, pe_y[pg][ph * 4 + 3])
                        cVE.inc(v.tensor_tensor(
                            hn_b[:, hq4 % 2, :, :], h_b[:, hq4 % 4, :, :],
                            rstd_bc[:].unsqueeze(1).broadcast_to([128, 4, 1024]),
                            OP.mult))
                    v.wait_ge(cPE.h, pe_y[dg][63])
                    for b in range(8):
                        pb = bA(b % 4) if b < 4 else bB(b % 4)
                        cVE.inc(v.tensor_copy(fst[:, b, :], pb))
                    v.wait_ge(cPE.h, pe_T[dg])
                    for tt in range(8):
                        tb = bA(tt % 4) if tt < 4 else bB(tt % 4)
                        SXB.wait(v, SXB.tok(dg * 8 + tt))
                        if dg > 0 or tt >= 2:
                            pg, pt = (dg, tt - 2) if tt >= 2 else (dg - 1, 6 + tt)
                            SOS.wait(v, SOS.tok(pg * 8 + pt))
                        cVE.inc(v.tensor_tensor(o_st[:, tt % 2, :], x_b[:, tt % 2, :],
                                                tb, OP.add))

    return nc


def _finish_dbg(nc, block, sems, snp, dbg_aps, dbg, srcs=None, sb_srcs=None):
    dL = sems["dL"]

    @block.sync
    def _(sy):
        for nm, v in snp.items():
            if v > 0:
                sy.wait_ge(sems[nm].h, v)
        for name in dbg:
            if srcs and name in srcs:
                dL.inc(sy.dma_start(out=dbg_aps[name][:, :], in_=srcs[name][:, :]), 16)
            elif sb_srcs and name in sb_srcs:
                ap = sb_srcs[name]
                d = dbg_aps[name]
                if len(ap.shape) == 3:
                    d3 = d.rearrange("(a p) c -> p a c", p=128)
                    dL.inc(sy.dma_start(out=d3, in_=ap), 16)
                else:
                    dL.inc(sy.dma_start(out=d[:, :], in_=ap), 16)
        sy.wait_ge(dL.h, dL.n)


_NC_CACHE = {}


def _get_nc(phases=99, dbg=()):
    key = (phases, tuple(dbg))
    if key not in _NC_CACHE:
        _NC_CACHE[key] = build_nc(phases, dbg)
    return _NC_CACHE[key]


def make_in_maps(L0_attn_x, L0_attn_W_qkv, L0_attn_b_qkv, L0_attn_W_out,
                 L0_attn_b_out, L0_ffn_w_gate_up, L0_ffn_w_proj):
    x = np.ascontiguousarray(np.asarray(L0_attn_x).reshape(SEQ, D))
    Wqkv = np.asarray(L0_attn_W_qkv)
    bq = np.asarray(L0_attn_b_qkv)
    Wout = np.ascontiguousarray(np.asarray(L0_attn_W_out))
    bo = np.ascontiguousarray(np.asarray(L0_attn_b_out).reshape(1, D))
    wgu_full = np.asarray(L0_ffn_w_gate_up)
    wp_full = np.asarray(L0_ffn_w_proj)
    in_maps = []
    for c in range(NCORES):
        cols = [Wqkv[:, p * D + c * 256:p * D + (c + 1) * 256] for p in range(3)]
        wq = np.ascontiguousarray(np.concatenate(cols, axis=1))
        bqc = np.ascontiguousarray(
            np.concatenate([bq[p * D + c * 256:p * D + (c + 1) * 256]
                            for p in range(3)]).reshape(1, 768))
        in_maps.append({
            "x_own": np.ascontiguousarray(x[c * TOWN:(c + 1) * TOWN]),
            "oofs": np.array([[i * SEQ + c * TOWN for i in range(NCORES)]], np.int32),
            "wqkv": wq, "bqkv": bqc, "wout": Wout, "bout": bo,
            "wgu": np.ascontiguousarray(wgu_full[c * 2048:(c + 1) * 2048]),
            "wp": np.ascontiguousarray(wp_full[:, c * 1024:(c + 1) * 1024]),
        })
    return in_maps


def kernel(**inputs):
    nc = _get_nc()
    in_maps = make_in_maps(**inputs)
    res = run_bass_kernel_spmd(nc, in_maps, core_ids=list(range(NCORES)))
    outp = np.concatenate([res.results[c]["out"] for c in range(NCORES)], axis=0)
    return outp.reshape(1, SEQ, D)
